# version 15
# baseline (speedup 1.0000x reference)
import numpy as np
from contextlib import ExitStack

import concourse.bass as bass
import concourse.mybir as mybir
from concourse.bass_utils import run_bass_kernel_spmd

F32 = mybir.dt.float32
BF16 = mybir.dt.bfloat16
AF = mybir.ActivationFunctionType
ALU = mybir.AluOpType
AX = mybir.AxisListType

B, S, D, H, HD, DFF = 2, 8192, 1024, 16, 64, 4096
NCORE = 8
TILE = 512
CH = 128
EV = 68
LNEPS = 1e-5
ATTEPS = 1e-6
RMSEPS = 1e-8
NCH = TILE // CH
DC = D // 128
FC = DFF // 128

ENGS = ("sync", "tensor", "vector", "scalar", "gpsimd")
ESEM = {"tensor": "PE", "vector": "DV", "scalar": "AC", "gpsimd": "GP", "sync": "LD"}
SY, TE, VE, SC, GP = ENGS


class Prog:

    def __init__(self, nc):
        self.nc = nc
        self.ops = {e: [] for e in ENGS}
        self.semh = {}
        self.cnt = {}
        self.seen = {e: {} for e in ENGS}
        self.buf = {}
        self.dmarr = {}

    def mksem(self, es, *names):
        for n in names:
            self.semh[n] = es.enter_context(self.nc.semaphore(n))
            self.cnt[n] = 0

    def _wait(self, eng, sname, v):
        if self.seen[eng].get(sname, 0) >= v:
            return
        self.seen[eng][sname] = v
        h = self.semh[sname]
        self.ops[eng].append(lambda E, h=h, v=v: E.wait_ge(h, v))

    DMA_POOLS = {"LD": ["LD0", "LD1", "LD2", "LD3"],
                 "LDG": ["LG0", "LG1", "LG2", "LG3"],
                 "ST": ["ST0"]}

    def emit(self, eng, fns, reads=(), writes=(), sem=None, inc=1):
        if not isinstance(fns, (list, tuple)):
            fns = [fns]
        if sem is None:
            sem = ESEM[eng]
        if sem in self.DMA_POOLS:
            pool = self.DMA_POOLS[sem]
            idx = self.dmarr.get(sem, 0)
            self.dmarr[sem] = (idx + 1) % len(pool)
            sem = pool[idx]
            if self.cnt[sem] > 0:
                self._wait(eng, sem, self.cnt[sem])
        need = {}
        for b in reads:
            st = self.buf.get(b)
            if st and st["w"]:
                s, v = st["w"]
                need[s] = max(need.get(s, 0), v)
        for b in writes:
            st = self.buf.get(b)
            if st:
                toks = ([st["w"]] if st["w"] else []) + st["r"]
                for s, v in toks:
                    need[s] = max(need.get(s, 0), v)
        for s in sorted(need):
            self._wait(eng, s, need[s])
        for f in fns[:-1]:
            self.ops[eng].append(f)
        self.cnt[sem] += inc
        v = self.cnt[sem]
        h = self.semh[sem]
        last = fns[-1]
        self.ops[eng].append(lambda E, last=last, h=h, inc=inc: last(E).then_inc(h, inc))
        tok = (sem, v)
        for b in writes:
            self.buf[b] = {"w": tok, "r": []}
        for b in reads:
            st = self.buf.setdefault(b, {"w": None, "r": []})
            st["r"].append(tok)
        return tok

    def barrier(self, exclude=("CC",)):
        cur = {s: v for s, v in self.cnt.items() if v > 0 and s not in exclude}
        for e in ENGS:
            for s in sorted(cur):
                self._wait(e, s, cur[s])


def build_nc(npre_slabs, nown_tiles):
    OWN = nown_tiles * TILE
    PRE = max(npre_slabs * TILE, TILE)
    nc = bass.Bass("TRN2", target_bir_lowering=False)

    x_own = nc.dram_tensor("x_own", [OWN, D], F32, kind="ExternalInput").ap()
    smask = nc.dram_tensor("smask", [128, NCORE], F32, kind="ExternalInput").ap()
    wqT = nc.dram_tensor("wqT", [D, D], F32, kind="ExternalInput").ap()
    wkT = nc.dram_tensor("wkT", [D, D], F32, kind="ExternalInput").ap()
    wvT = nc.dram_tensor("wvT", [D, D], F32, kind="ExternalInput").ap()
    woT = nc.dram_tensor("woT", [D, D], F32, kind="ExternalInput").ap()
    bq8 = nc.dram_tensor("bq8", [128, DC], F32, kind="ExternalInput").ap()
    bk8 = nc.dram_tensor("bk8", [128, DC], F32, kind="ExternalInput").ap()
    bv_row = nc.dram_tensor("bv_row", [1, D], F32, kind="ExternalInput").ap()
    bo_row = nc.dram_tensor("bo_row", [1, D], F32, kind="ExternalInput").ap()
    wupT = nc.dram_tensor("wupT", [D, DFF], F32, kind="ExternalInput").ap()
    wdnT = nc.dram_tensor("wdnT", [DFF, D], F32, kind="ExternalInput").ap()
    out = nc.dram_tensor("out", [OWN, D], F32, kind="ExternalOutput").ap()
    wup_t = nc.dram_tensor("wup_t", [D, DFF], BF16, kind="Internal").ap()
    wdn_t = nc.dram_tensor("wdn_t", [DFF, D], BF16, kind="Internal").ap()
    qT_d = nc.dram_tensor("qT_d", [nown_tiles * 128, DC * TILE], BF16, kind="Internal").ap()
    num_d = nc.dram_tensor("num_d", [OWN, H * EV], BF16, kind="Internal").ap()
    st_loc = nc.dram_tensor("st_loc", [64, H * EV], F32, kind="Internal").ap()
    st_all = nc.dram_tensor("st_all", [NCORE * 64, H * EV], F32, kind="Internal").ap()

    with ExitStack() as es:
        pst = lambda name, shape, dt: es.enter_context(nc.psum_tensor(name, shape, dt))

        ARENA_BC = 106000
        arena = es.enter_context(nc.sbuf_tensor("arena", [128, ARENA_BC], BF16))
        cur = [0]

        def take(bcols, dt=BF16, parts=128):
            o = cur[0]
            bcols_al = (bcols + 15) // 16 * 16
            cur[0] += bcols_al
            assert cur[0] <= ARENA_BC, f"arena overflow at {cur[0]}"
            v = arena[0:parts, o:o + bcols]
            if dt == F32:
                v = v.bitcast(F32)
            return v

        x_sl = [take(8192, F32) for _ in range(2)]
        out_sl = take(8192, F32)
        xn_tok = take(4096)
        xnT = take(4096)
        st_f = take(2176, F32, parts=64)
        st_b = take(1088)
        bn_s = take(96, F32)
        mv_s = take(16, F32)
        rstd_s = take(8, F32)
        cA_s = take(8, F32)
        sffn_s = take(8, F32)
        cB_s = take(8, F32)
        lnt_s = take(32, F32)
        rcp_s = take(8, F32)
        s2col = take(8, F32)
        tpart = take(4, F32)
        tred = take(2, F32)
        thr_s = take(4, F32)
        nthr_s = take(4, F32)
        lneps_c = take(2, F32)
        mask8 = take(512)
        ident = take(128)
        ones_col = take(1)
        ones_cf = take(2, F32)
        bq_s = take(16, F32)
        bk_s = take(16, F32)
        sm_s = take(16, F32)
        shard_s = take(2176, F32, parts=64)
        ones_row = take(128, parts=1)
        ones_rf = take(256, F32, parts=1)
        ones_1f = take(2, F32, parts=1)
        bvr_s = take(1024, parts=1)
        bor_s = take(1024, parts=1)
        s2row = take(1024, F32, parts=1)
        trow_s = take(4, F32, parts=1)

        twf = [take(1024, F32) for _ in range(2)]
        tg1 = take(1024, F32)
        tout = [take(512) for _ in range(2)]
        dyn_base = cur[0]

        wo_s = take(8192)
        wts_top = cur[0]
        wq_s = take(8192)
        wk_s = take(8192)
        wv_s = take(8192)
        vtok = take(4352)
        ph_e = [take(1024, F32) for _ in range(2)]
        ph_r = [take(1024, F32) for _ in range(2)]
        var_base = cur[0]

        ktok = take(4096)
        qT = kT = ktok_c = o_tok = oT = am_s = None

        PS = [pst(f"P{i}", [128, 512], F32) for i in range(7)]
        P7 = pst("P7", [128, 1024], BF16)

        p = Prog(nc)
        p.mksem(es, "LD0", "LD1", "LD2", "LD3", "LG0", "LG1", "LG2", "LG3", "ST0", "CC", "PE", "DV", "AC", "GP")

        bank_fresh = {k: True for k in range(8)}

        def lead(i):
            f = bank_fresh[i]
            bank_fresh[i] = False
            return f

        def bk_done(i):
            bank_fresh[i] = True

        p.emit(GP, lambda E: E.memset(ident, 0.0), writes=["ident"])
        p.emit(GP, lambda E: E.affine_select(
            out=ident, in_=ident, compare_op=ALU.not_equal, fill=1.0,
            base=0, pattern=[[-1, 128]], channel_multiplier=1),
            reads=["ident"], writes=["ident"])
        p.emit(GP, lambda E: E.memset(mask8, 1.0), writes=["mask8"])
        for i in range(4):
            p.emit(GP, lambda E, i=i: E.affine_select(
                out=mask8[:, i * CH:(i + 1) * CH], in_=mask8[:, i * CH:(i + 1) * CH],
                compare_op=ALU.is_ge, fill=0.0, base=0,
                pattern=[[1, CH]], channel_multiplier=-1),
                reads=["mask8"], writes=["mask8"])
        p.emit(GP, lambda E: E.memset(lneps_c[:, 0:1], float(LNEPS)), writes=["lneps"])
        for t, n in [(ones_row, "ones_row"), (ones_col, "ones_col"),
                     (ones_rf, "ones_rf"), (ones_cf, "ones_cf"), (ones_1f, "ones_1f")]:
            p.emit(GP, lambda E, t=t: E.memset(t, 1.0), writes=[n])
        p.emit(VE, lambda E: E.memset(st_f, 0.0), writes=["st_f"])
        p.emit(VE, lambda E: E.memset(st_b, 0.0), writes=["st_b"])
        p.emit(VE, lambda E: E.memset(tpart, 0.0), writes=["tpart"])
        p.emit(VE, lambda E: E.memset(vtok, 0.0), writes=["vtok"])
        p.emit(VE, lambda E: E.memset(
            vtok.rearrange("q (c h e) -> q c h e", c=NCH, h=H)[:, :, :, HD:HD + 1], 1.0),
            reads=["vtok"], writes=["vtok"])

        for w_ap, tile, name in [(wqT, wq_s, "wq"), (wkT, wk_s, "wk"),
                                 (wvT, wv_s, "wv"), (woT, wo_s, "wo")]:
            p.emit(GP, lambda E, w=w_ap, t=tile: E.dma_start(
                out=t.rearrange("q (c d) -> q c d", c=DC),
                in_=w.rearrange("(c q) d -> q c d", q=128)),
                writes=[name], sem="LDG", inc=16)
        p.emit(SY, lambda E: E.dma_start(out=bq_s[:, 0:DC], in_=bq8), writes=["bq"], inc=16)
        p.emit(SY, lambda E: E.dma_start(out=bk_s[:, 0:DC], in_=bk8), writes=["bk"], inc=16)
        p.emit(GP, lambda E: E.dma_start(out=bvr_s, in_=bv_row), writes=["bv"], sem="LDG", inc=16)
        p.emit(GP, lambda E: E.dma_start(out=bor_s, in_=bo_row), writes=["bo"], sem="LDG", inc=16)
        p.emit(SY, lambda E: E.dma_start(out=sm_s[:, 0:NCORE], in_=smask),
               writes=["sm"], inc=16)

        def ln_stats(xb, xname, sffn=False):
            xs = x_sl[xb]
            for c in range(NCH):
                p.emit(VE, lambda E, c=c, xs=xs: E.bn_stats(
                    bn_s[:, c * 12:c * 12 + 6], xs[:, c * D:c * D + 512]),
                    reads=[xname], writes=[f"bn{c}"])
                p.emit(VE, lambda E, c=c, xs=xs: E.bn_stats(
                    bn_s[:, c * 12 + 6:c * 12 + 12], xs[:, c * D + 512:(c + 1) * D]),
                    reads=[xname, f"bn{c}"], writes=[f"bn{c}"])
                p.emit(VE, lambda E, c=c: E.bn_aggr(
                    mv_s[:, 2 * c:2 * c + 2], bn_s[:, c * 12:(c + 1) * 12]),
                    reads=[f"bn{c}"], writes=[f"mv{c}"])
            mvr = [f"mv{c}" for c in range(NCH)]
            p.emit(SC, lambda E: E.activation(
                lnt_s[:, 0:NCH], mv_s[:, 1:2 * NCH:2], AF.Ln, bias=lneps_c[:, 0:1], scale=1.0),
                reads=mvr + ["lneps"], writes=["lnt0"])
            p.emit(SC, lambda E: E.activation(
                rstd_s[:, 0:NCH], lnt_s[:, 0:NCH], AF.Exp, scale=-0.5),
                reads=["lnt0"], writes=["rstd"])
            p.emit(VE, lambda E: E.scalar_tensor_tensor(
                out=cA_s[:, 0:NCH], in0=mv_s[:, 0:2 * NCH:2], scalar=-1.0,
                in1=rstd_s[:, 0:NCH], op0=ALU.mult, op1=ALU.mult),
                reads=mvr + ["rstd"], writes=["cA"])
            if sffn:
                p.emit(SC, lambda E: E.activation(
                    lnt_s[:, NCH:2 * NCH], mv_s[:, 1:2 * NCH:2], AF.Ln, scale=1.0),
                    reads=mvr, writes=["lnt1"])
                p.emit(VE, lambda E: E.tensor_sub(
                    lnt_s[:, 2 * NCH:3 * NCH], lnt_s[:, NCH:2 * NCH], lnt_s[:, 0:NCH]),
                    reads=["lnt0", "lnt1"], writes=["lnt2"])
                p.emit(SC, lambda E: E.activation(
                    lnt_s[:, 3 * NCH:4 * NCH], lnt_s[:, 2 * NCH:3 * NCH], AF.Exp,
                    scale=0.5),
                    reads=["lnt2"], writes=["lnt3"])
                p.emit(VE, lambda E: E.tensor_scalar(
                    out=lnt_s[:, 3 * NCH:4 * NCH], in0=lnt_s[:, 3 * NCH:4 * NCH],
                    scalar1=float(RMSEPS), scalar2=None, op0=ALU.add),
                    reads=["lnt3"], writes=["lnt3"])
                p.emit(VE, lambda E: E.reciprocal(
                    lnt_s[:, 2 * NCH:3 * NCH], lnt_s[:, 3 * NCH:4 * NCH]),
                    reads=["lnt3"], writes=["lnt2"])
                p.emit(VE, lambda E: E.tensor_mul(
                    sffn_s[:, 0:NCH], rstd_s[:, 0:NCH], lnt_s[:, 2 * NCH:3 * NCH]),
                    reads=["rstd", "lnt2"], writes=["sffn"])
                p.emit(VE, lambda E: E.scalar_tensor_tensor(
                    out=cB_s[:, 0:NCH], in0=mv_s[:, 0:2 * NCH:2], scalar=-1.0,
                    in1=sffn_s[:, 0:NCH], op0=ALU.mult, op1=ALU.mult),
                    reads=mvr + ["sffn"], writes=["cB"])

        def xn_apply(xb, xname, scale_t, scale_name, shift_t, shift_name):
            xs = x_sl[xb]
            for c in range(NCH):
                p.emit(SC, lambda E, c=c, xs=xs: E.activation(
                    xn_tok[:, c * D:(c + 1) * D], xs[:, c * D:(c + 1) * D], AF.Identity,
                    bias=shift_t[:, c:c + 1], scale=scale_t[:, c:c + 1]),
                    reads=[xname, scale_name, shift_name], writes=["xn_tok"])

        def transpose_tok_to_ft(src, src_name, dst, dst_name):
            for c in range(NCH):
                fns = []
                for jm in range(DC):
                    def tr(E, c=c, jm=jm):
                        return E.transpose(
                            P7[:, jm * 128:(jm + 1) * 128],
                            src[:, c * D + jm * 128:c * D + (jm + 1) * 128], ident)
                    fns.append(tr)
                p.emit(TE, fns, reads=[src_name, "ident"], writes=["P7"])
                ov = dst.rearrange("q (jm t) -> q jm t", jm=DC)[:, :, c * 128:(c + 1) * 128]
                p.emit(SC, lambda E, ov=ov: E.copy(
                    ov, P7[:].rearrange("q (jm t) -> q jm t", jm=DC)),
                    reads=["P7"], writes=[dst_name])
                bk_done(7)

        def transpose_ft_to_tok(src, src_name, dst, dst_name):
            for c in range(NCH):
                fns = []
                for jd in range(DC):
                    def tr(E, c=c, jd=jd):
                        return E.transpose(
                            P7[:, jd * 128:(jd + 1) * 128],
                            src[:, jd * TILE + c * 128:jd * TILE + (c + 1) * 128], ident)
                    fns.append(tr)
                p.emit(TE, fns, reads=[src_name, "ident"], writes=["P7"])
                p.emit(VE, lambda E, c=c: E.tensor_copy(
                    dst[:, c * D:(c + 1) * D], P7[:]),
                    reads=["P7"], writes=[dst_name])
                bk_done(7)

        def proj_feature(wtile, wname, btile, bname, dst, dst_name):
            for jd in range(DC):
                bank = jd % 4
                pp = jd % 2
                fns = []
                for jm in range(DC):
                    def mm(E, jd=jd, jm=jm, bank=bank, st=(lead(bank) if (jm == 0) else False)):
                        return E.matmul(
                            PS[bank][:],
                            wtile[:, jm * D + jd * 128: jm * D + (jd + 1) * 128],
                            xnT[:, jm * TILE:(jm + 1) * TILE],
                            start=st, stop=(jm == DC - 1))
                    fns.append(mm)
                p.emit(TE, fns, reads=[wname, "xnT"], writes=[f"PB{bank}"])
                dv = dst[:, jd * TILE:(jd + 1) * TILE]
                p.emit(SC, lambda E, jd=jd, bank=bank, pp=pp: E.activation(
                    ph_e[pp][:, 0:512], PS[bank][:], AF.Exp, bias=btile[:, jd:jd + 1],
                    scale=1.0),
                    reads=[f"PB{bank}", bname], writes=[f"ph_e{pp}"])
                p.emit(SC, lambda E, jd=jd, bank=bank, pp=pp: E.activation(
                    ph_r[pp][:, 0:512], PS[bank][:], AF.Relu, bias=btile[:, jd:jd + 1],
                    scale=1.0),
                    reads=[f"PB{bank}", bname], writes=[f"ph_r{pp}"])
                p.emit(VE, lambda E, dv=dv, pp=pp: E.scalar_tensor_tensor(
                    out=dv, in0=ph_e[pp][:, 0:512], scalar=1.0, in1=ph_r[pp][:, 0:512],
                    op0=ALU.min, op1=ALU.add),
                    reads=[f"ph_e{pp}", f"ph_r{pp}"], writes=[dst_name])
                bk_done(bank)
                tern_tick()

        def proj_token(wtile, wname, brow, bname, writer, banks):
            for c in range(NCH):
                for half in range(2):
                    bank = banks[(c * 2 + half) % len(banks)]
                    fns = []
                    for jm in range(DC):
                        def mm(E, c=c, half=half, jm=jm, bank=bank, st=(lead(bank) if (jm == 0) else False)):
                            return E.matmul(
                                PS[bank][:],
                                xnT[:, jm * TILE + c * 128: jm * TILE + (c + 1) * 128],
                                wtile[:, jm * D + half * 512: jm * D + (half + 1) * 512],
                                start=st, stop=False)
                        fns.append(mm)

                    def mmb(E, half=half, bank=bank):
                        return E.matmul(
                            PS[bank][:], ones_row, brow[0:1, half * 512:(half + 1) * 512],
                            start=False, stop=True)
                    fns.append(mmb)
                    p.emit(TE, fns, reads=[wname, "xnT", bname, "ones_row"],
                           writes=[f"PB{bank}"])
                    writer(c, half, bank)
                    bk_done(bank)
                    tern_tick()

        SDSEG = [(4, 0, 6), (5, 6, 11), (6, 11, 16)]

        def state_fold():
            for bank, h0, h1 in SDSEG:
                p.emit(VE, lambda E, bank=bank, h0=h0, h1=h1: E.tensor_tensor(
                    st_f[:, h0 * EV:h1 * EV], st_f[:, h0 * EV:h1 * EV],
                    PS[bank][0:64, 0:(h1 - h0) * EV], ALU.add),
                    reads=[f"PB{bank}", "st_f"], writes=["st_f"])
                bk_done(bank)
            p.emit(VE, lambda E: E.tensor_copy(
                st_b[0:64, :].rearrange("q (h e) -> q h e", h=H)[:, 0:H:2, :],
                st_f[:, 0:H * EV].rearrange("q (h e) -> q h e", h=H)[:, 0:H:2, :]),
                reads=["st_f"], writes=["st_b"])
            p.emit(VE, lambda E: E.tensor_copy(
                st_b[64:128, :].rearrange("q (h e) -> q h e", h=H)[:, 1:H:2, :],
                st_f[:, 0:H * EV].rearrange("q (h e) -> q h e", h=H)[:, 1:H:2, :]),
                reads=["st_f"], writes=["st_b"])

        scanref = {}

        def scan_pre(ci):
            qT = scanref["qT"]; kT = scanref["kT"]
            ktok_c = scanref["ktok_c"][ci % 2]
            am_s = scanref["am_s"]
            fns = []
            for jd in range(DC):
                def tr(E, jd=jd, ci=ci):
                    return E.transpose(
                        P7[:, jd * 128:(jd + 1) * 128],
                        kT[:, jd * TILE + ci * 128:jd * TILE + (ci + 1) * 128], ident)
                fns.append(tr)
            p.emit(TE, fns, reads=["kT", "ident"], writes=["P7"])
            p.emit(SC, lambda E, ktok_c=ktok_c: E.copy(ktok_c[:, 0:D], P7[:]),
                   reads=["P7"], writes=[f"ktokc{ci % 2}"])
            bk_done(7)
            fns = []
            wr = set()
            for h in range(H):
                bank = h % 4
                col = (h // 4) * 128

                def m1(E, h=h, bank=bank, col=col, ci=ci,
                       st=(lead(bank) if (h // 4 == 0) else False)):
                    return E.matmul(
                        PS[bank][:, col:col + 128],
                        kT[(h % 2) * 64:(h % 2) * 64 + 64,
                           (h // 2) * TILE + ci * 128:(h // 2) * TILE + (ci + 1) * 128],
                        qT[(h % 2) * 64:(h % 2) * 64 + 64,
                           (h // 2) * TILE + ci * 128:(h // 2) * TILE + (ci + 1) * 128],
                        start=st, stop=(h >= H - 4))
                fns.append(m1)
                wr.add(f"PB{bank}")
            p.emit(TE, fns, reads=["kT", "qT"], writes=sorted(wr))
            for bank in range(4):
                ov = am_s.rearrange("q (g b c) -> q g b c", g=4, b=4)[:, :, bank, :]
                p.emit(VE, lambda E, bank=bank, ov=ov: E.tensor_tensor(
                    ov, PS[bank][:].rearrange("q (g c) -> q g c", g=4),
                    mask8.rearrange("q (g c) -> q g c", g=4), ALU.mult),
                    reads=[f"PB{bank}", "mask8"], writes=[f"am{bank}"])
                bk_done(bank)

        def scan_groups(ci):
            qT = scanref["qT"]; am_s = scanref["am_s"]; numsave = scanref["numsave"]
            amr = [f"am{b}" for b in range(4)]
            for g in range(4):
                nb = 4 + (g % 2)
                fns = []
                for i in range(4):
                    h = g * 4 + i

                    def m2(E, h=h, i=i, nb=nb, ci=ci, st=(lead(nb) if (i == 0) else False)):
                        return E.matmul(
                            PS[nb][:, i * EV:i * EV + EV],
                            am_s[:, h * 128:(h + 1) * 128],
                            vtok[:, ci * H * EV + h * EV: ci * H * EV + (h + 1) * EV],
                            start=st, stop=False)

                    def m3(E, h=h, i=i, nb=nb, ci=ci):
                        return E.matmul(
                            PS[nb][:, i * EV:i * EV + EV],
                            qT[:, (h // 2) * TILE + ci * 128:(h // 2) * TILE + (ci + 1) * 128],
                            st_b[:, h * EV:(h + 1) * EV],
                            start=False, stop=(i == 3))
                    fns += [m2, m3]
                p.emit(TE, fns, reads=amr + ["vtok", "qT", "st_b"], writes=[f"PB{nb}"])
                p.emit(SC, lambda E, nb=nb, g=g, ci=ci: E.copy(
                    numsave[:, ci * H * EV + g * 4 * EV: ci * H * EV + (g + 1) * 4 * EV],
                    PS[nb][:, 0:4 * EV]),
                    reads=[f"PB{nb}"], writes=[f"nsv{ci}_{g}"])
                bk_done(nb)
                tern_tick()

        def scan_tail(ci):
            ktok_c = scanref["ktok_c"][ci % 2]
            numsave = scanref["numsave"]
            fns = []
            wr = set()
            for bank, h0, h1 in SDSEG:
                for i in range(h1 - h0):
                    h = h0 + i

                    def m4(E, h=h, i=i, bank=bank, ci=ci, ktok_c=ktok_c,
                           last=(i == h1 - h0 - 1), st=(lead(bank) if (i == 0) else False)):
                        return E.matmul(
                            PS[bank][0:64, i * EV:i * EV + EV],
                            ktok_c[:, h * HD:(h + 1) * HD],
                            vtok[:, ci * H * EV + h * EV: ci * H * EV + (h + 1) * EV],
                            start=st, stop=last)
                    fns.append(m4)
                    wr.add(f"PB{bank}")
            p.emit(TE, fns, reads=[f"ktokc{ci % 2}", "vtok"], writes=sorted(wr))
            state_fold()
            if ci == NCH - 1:
                ti = scanref["ti"]
                p.emit(SY, lambda E, ti=ti: E.dma_start(
                    out=num_d[ti * TILE:(ti + 1) * TILE, :].rearrange(
                        "(c q) e -> q c e", q=128),
                    in_=numsave.rearrange("q (c e) -> q c e", c=NCH)),
                    reads=[f"nsv{c}_{g}" for c in range(NCH) for g in range(4)],
                    writes=["num_dd"], sem="LD", inc=16)

        tern_chunks = [("up", j) for j in range(64)] + [("dn", j) for j in range(64)]

        def tern_load(kind, j, b):
            if kind == "up":
                src = wupT[(j // 8) * 128:(j // 8 + 1) * 128,
                           (j % 8) * 512:(j % 8 + 1) * 512]
            else:
                src = wdnT[(j // 2) * 128:(j // 2 + 1) * 128,
                           (j % 2) * 512:(j % 2 + 1) * 512]
            p.emit(GP, lambda E, s=src, b=b: E.dma_start(out=twf[b], in_=s),
                   writes=[f"twf{b}"], sem="LDG", inc=16)

        def tern_abs(kind, j, b):
            col = 0 if kind == "up" else 1
            p.emit(SC, lambda E, b=b: E.activation(
                tg1, twf[b], AF.Abs, accum_out=tred[:, 0:1]),
                reads=[f"twf{b}"], writes=["tred", "tg1"])
            p.emit(VE, lambda E, col=col: E.tensor_tensor(
                tpart[:, col:col + 1], tpart[:, col:col + 1], tred[:, 0:1], ALU.add),
                reads=["tred", "tpart"], writes=["tpart"])

        def tern_finalize():
            p.emit(TE, lambda E, st=lead(6): E.matmul(
                PS[6][0:1, 0:2], ones_cf[:, 0:1], tpart[:, 0:2], start=st, stop=True),
                reads=["tpart", "ones_cf"], writes=["PB6"])
            p.emit(SC, lambda E: E.activation(
                trow_s[0:1, 0:2], PS[6][0:1, 0:2], AF.Copy,
                scale=float(0.5 / (D * DFF))),
                reads=["PB6"], writes=["trow"])
            bk_done(6)
            p.emit(TE, lambda E, st=lead(6): E.matmul(
                PS[6][:, 0:2], ones_rf[0:1, 0:128], trow_s[0:1, 0:2],
                start=st, stop=True),
                reads=["trow", "ones_rf"], writes=["PB6"])
            p.emit(VE, lambda E: E.tensor_copy(thr_s[:, 0:2], PS[6][:, 0:2]),
                   reads=["PB6"], writes=["thr"])
            p.emit(VE, lambda E: E.tensor_scalar(
                out=nthr_s[:, 0:2], in0=thr_s[:, 0:2], scalar1=-1.0, scalar2=None,
                op0=ALU.mult),
                reads=["thr"], writes=["nthr"])
            bk_done(6)

        def tern_quant(kind, j, b):
            col = 0 if kind == "up" else 1
            p.emit(VE, lambda E, b=b, col=col: E.tensor_scalar(
                out=tg1, in0=twf[b], scalar1=nthr_s[:, col:col + 1], scalar2=None,
                op0=ALU.is_lt),
                reads=[f"twf{b}", "nthr"], writes=["tg1"])
            p.emit(VE, lambda E, b=b, col=col: E.scalar_tensor_tensor(
                out=tout[b], in0=twf[b], scalar=thr_s[:, col:col + 1], in1=tg1,
                op0=ALU.is_gt, op1=ALU.subtract),
                reads=[f"twf{b}", "thr", "tg1"], writes=[f"tout{b}"])
            if kind == "up":
                dst = wup_t[(j // 8) * 128:(j // 8 + 1) * 128,
                            (j % 8) * 512:(j % 8 + 1) * 512]
                scr = "scr_up"
            else:
                dst = wdn_t[(j // 2) * 128:(j // 2 + 1) * 128,
                            (j % 2) * 512:(j % 2 + 1) * 512]
                scr = "scr_dn"
            p.emit(SY, lambda E, d=dst, b=b: E.dma_start(out=d, in_=tout[b]),
                   reads=[f"tout{b}"], writes=[scr], sem="LD", inc=16)

        tst = {"a": 0, "b": 0, "fin": False}
        quant_done = set()

        def tern_step(n):
            for _ in range(n):
                if tst["a"] < len(tern_chunks):
                    k, j = tern_chunks[tst["a"]]
                    tern_load(k, j, tst["b"])
                    tern_abs(k, j, tst["b"])
                    tst["a"] += 1
                    tst["b"] ^= 1

        def tern_abs_flush():
            while tst["a"] < len(tern_chunks):
                tern_step(1)
            if not tst["fin"]:
                tern_finalize()
                tst["fin"] = True

        def tern_quant_chunks(chunks):
            for k, j in chunks:
                if (k, j) in quant_done:
                    continue
                quant_done.add((k, j))
                tern_load(k, j, tst["b"])
                tern_quant(k, j, tst["b"])
                tst["b"] ^= 1

        def tern_tick(n=2):
            if tst["a"] < len(tern_chunks):
                tern_step(n)
            elif not tst["fin"]:
                tern_finalize()
                tst["fin"] = True
            else:
                tern_quant_step(n)

        qsched = {"i": 0}

        def tern_quant_step(n):
            c = 0
            while qsched["i"] < len(tern_chunks) and c < n:
                k, j = tern_chunks[qsched["i"]]
                qsched["i"] += 1
                if (k, j) in quant_done:
                    continue
                tern_quant_chunks([(k, j)])
                c += 1

        p.barrier()
        cur[0] = var_base
        qT = take(4096)
        kT = take(4096)
        ktok_c = [take(1024) for _ in range(2)]
        am_s = take(2048)
        numsave = take(4352)
        scanref["qT"] = qT
        scanref["kT"] = kT
        scanref["ktok_c"] = ktok_c
        scanref["am_s"] = am_s
        scanref["numsave"] = numsave
        for ti in range(nown_tiles):
            xb = ti % 2
            xname = f"x{xb}"
            p.emit(SY, lambda E, ti=ti, xb=xb: E.dma_start(
                out=x_sl[xb].rearrange("q (c d) -> q c d", c=NCH),
                in_=x_own[ti * TILE:(ti + 1) * TILE, :].rearrange("(c q) d -> q c d", q=128)),
                writes=[xname], inc=16)
            ln_stats(xb, xname)
            xn_apply(xb, xname, rstd_s, "rstd", cA_s, "cA")
            transpose_tok_to_ft(xn_tok, "xn_tok", xnT, "xnT")
            proj_feature(wq_s, "wq", bq_s, "bq", qT, "qT")
            proj_feature(wk_s, "wk", bk_s, "bk", kT, "kT")
            scanref["ti"] = ti
            p.emit(SY, lambda E, ti=ti: E.dma_start(
                out=qT_d[ti * 128:(ti + 1) * 128, :], in_=qT),
                reads=["qT"], writes=["qT_dd"], sem="LD", inc=16)

            def v_writer(c, half, bank):
                ov = vtok.rearrange("q (c h e) -> q c h e", c=NCH, h=H)[
                    :, c, half * 8:(half + 1) * 8, 0:HD]
                p.emit(SC, lambda E, ov=ov, bank=bank: E.copy(
                    ov, PS[bank][:].rearrange("q (h e) -> q h e", h=8)),
                    reads=[f"PB{bank}"], writes=["vtok"])
            proj_token(wv_s, "wv", bvr_s, "bv", v_writer, [0, 1, 2, 3])

            scan_pre(0)
            for ci in range(NCH):
                scan_groups(ci)
                if ci < NCH - 1:
                    scan_pre(ci + 1)
                scan_tail(ci)

        tern_abs_flush()
        tern_quant_step(len(tern_chunks))

        p.emit(SY, lambda E: E.dma_start(out=st_loc, in_=st_f[:, 0:H * EV]),
               reads=["st_f"], writes=["st_loc_d"], sem="LD", inc=16)
        p.emit(GP, lambda E: E.collective_compute(
            "AllGather", ALU.bypass,
            replica_groups=[list(range(NCORE))],
            ins=[st_loc.opt()], outs=[st_all.opt()]),
            reads=["st_loc_d"], writes=["st_all_d"], sem="CC", inc=1)
        p.barrier()
        cur[0] = wts_top
        hT = take(16384)
        fwu = [take(4096) for _ in range(2)]
        fwd = [take(2048) for _ in range(2)]
        hsq = [take(512) for _ in range(2)]

        def b_tile(ti):
            xb = ti % 2
            xname = f"x{xb}"
            p.emit(SY, lambda E, ti=ti, xb=xb: E.dma_start(
                out=x_sl[xb].rearrange("q (c d) -> q c d", c=NCH),
                in_=x_own[ti * TILE:(ti + 1) * TILE, :].rearrange("(c q) d -> q c d", q=128)),
                writes=[xname], inc=16)
            ln_stats(xb, xname, sffn=True)
            xn_apply(xb, xname, sffn_s, "sffn", cB_s, "cB")
            transpose_tok_to_ft(xn_tok, "xn_tok", xnT, "xnT")

            for fs in range(DFF // 512):
                fb = fs % 2
                tern_quant_chunks([("up", jm * 8 + fs) for jm in range(DC)])
                src = wup_t.rearrange("(jm q) f -> q jm f", q=128)[:, :, fs * 512:(fs + 1) * 512]
                p.emit(GP, lambda E, src=src, fb=fb: E.dma_start(
                    out=fwu[fb].rearrange("q (jm f) -> q jm f", jm=DC), in_=src),
                    reads=["scr_up"], writes=[f"fwu{fb}"], sem="LDG", inc=16)
                for lf in range(4):
                    jf = fs * 4 + lf
                    bank = jf % 4
                    fns = []
                    for jm in range(DC):
                        def mm(E, jm=jm, lf=lf, fb=fb, bank=bank, st=(lead(bank) if (jm == 0) else False)):
                            return E.matmul(
                                PS[bank][:],
                                fwu[fb][:, jm * 512 + lf * 128: jm * 512 + (lf + 1) * 128],
                                xnT[:, jm * TILE:(jm + 1) * TILE],
                                start=st, stop=(jm == DC - 1))
                        fns.append(mm)
                    p.emit(TE, fns, reads=[f"fwu{fb}", "xnT"], writes=[f"PB{bank}"])
                    if jf > 0:
                        pj, phb = jf - 1, (jf - 1) % 2
                        p.emit(TE, lambda E, pj=pj, phb=phb,
                               st=(lead(6) if pj == 0 else False): E.matmul(
                            PS[6][0:1, :], ones_col, hsq[phb],
                            start=st, stop=False),
                            reads=[f"hsq{phb}", "ones_col"], writes=["PB6"])
                    p.emit(SC, lambda E, jf=jf, bank=bank: E.activation(
                        hT[:, jf * TILE:(jf + 1) * TILE], PS[bank][:], AF.Gelu),
                        reads=[f"PB{bank}"], writes=[f"hTp{jf % 2}"])
                    hb = jf % 2
                    p.emit(VE, lambda E, jf=jf, hb=hb: E.tensor_mul(
                        hsq[hb], hT[:, jf * TILE:(jf + 1) * TILE],
                        hT[:, jf * TILE:(jf + 1) * TILE]),
                        reads=[f"hTp{jf % 2}"], writes=[f"hsq{hb}"])
                    bk_done(bank)
            p.emit(TE, lambda E: E.matmul(
                PS[6][0:1, :], ones_col, hsq[(FC - 1) % 2],
                start=False, stop=True),
                reads=[f"hsq{(FC - 1) % 2}", "ones_col"], writes=["PB6"])
            p.emit(SC, lambda E: E.activation(
                s2row[0:1, 0:TILE], PS[6][0:1, :], AF.Ln, scale=float(1.0 / DFF)),
                reads=["PB6"], writes=["s2row"])
            bk_done(6)
            p.emit(SC, lambda E: E.activation(
                s2row[0:1, 0:TILE], s2row[0:1, 0:TILE], AF.Exp, scale=0.5),
                reads=["s2row"], writes=["s2row"])
            p.emit(VE, lambda E: E.tensor_scalar(
                out=s2row[0:1, 0:TILE], in0=s2row[0:1, 0:TILE], scalar1=float(RMSEPS),
                scalar2=None, op0=ALU.add),
                reads=["s2row"], writes=["s2row"])
            p.emit(VE, lambda E: E.reciprocal(s2row[0:1, 0:TILE], s2row[0:1, 0:TILE]),
                   reads=["s2row"], writes=["s2row"])
            fns = []
            for c in range(NCH):
                def mm(E, c=c, st=(lead(6) if (c == 0) else False)):
                    return E.matmul(
                        PS[6][:, c:c + 1], s2row[0:1, c * 128:(c + 1) * 128],
                        ones_1f[0:1, 0:1], start=st, stop=(c == NCH - 1))
                fns.append(mm)
            p.emit(TE, fns, reads=["s2row", "ones_1f"], writes=["PB6"])
            p.emit(VE, lambda E: E.tensor_copy(s2col[:, 0:NCH], PS[6][:, 0:NCH]),
                   reads=["PB6"], writes=["s2col"])
            bk_done(6)

            for half in range(2):
                for fs in range(DFF // 512):
                    fb = fs % 2
                    tern_quant_chunks([("dn", (fs * 4 + r) * 2 + half) for r in range(4)])
                    src = wdn_t.rearrange("(jf q) d -> q jf d", q=128)[
                        :, fs * 4:(fs + 1) * 4, half * 512:(half + 1) * 512]
                    p.emit(GP, lambda E, src=src, fb=fb: E.dma_start(
                        out=fwd[fb].rearrange("q (jf d) -> q jf d", jf=4), in_=src),
                        reads=["scr_dn"], writes=[f"fwd{fb}"], sem="LDG", inc=16)
                    fns = []
                    for lf in range(4):
                        jf = fs * 4 + lf
                        for c in range(NCH):
                            def mm(E, jf=jf, lf=lf, c=c, fb=fb, st=(lead(c) if (jf == 0) else False)):
                                return E.matmul(
                                    PS[c][:],
                                    hT[:, jf * TILE + c * 128: jf * TILE + (c + 1) * 128],
                                    fwd[fb][:, lf * 512:(lf + 1) * 512],
                                    start=st, stop=(jf == FC - 1))
                            fns.append(mm)
                    p.emit(TE, fns, reads=[f"fwd{fb}", "hTp0", "hTp1"],
                           writes=[f"PB{c}" for c in range(NCH)])
                for c in range(NCH):
                    dv = out_sl[:, c * D + half * 512: c * D + (half + 1) * 512]
                    pv = x_sl[xb][:, c * D + half * 512: c * D + (half + 1) * 512]
                    p.emit(VE, lambda E, dv=dv, pv=pv, c=c: E.scalar_tensor_tensor(
                        out=dv, in0=PS[c][:], scalar=s2col[:, c:c + 1], in1=pv,
                        op0=ALU.mult, op1=ALU.add),
                        reads=[f"PB{c}", "s2col", xname], writes=["out_sl"])
                    bk_done(c)
            p.emit(SY, lambda E, ti=ti: E.dma_start(
                out=out[ti * TILE:(ti + 1) * TILE, :].rearrange("(c q) d -> q c d", q=128),
                in_=out_sl.rearrange("q (c d) -> q c d", c=NCH)),
                reads=["out_sl"], writes=[f"fdram{ti}"], sem="ST", inc=16)

        def s_in_block():
            p.emit(VE, lambda E: E.memset(st_f[:, 0:H * EV], 0.0),
                   reads=["st_f"], writes=["st_f"])
            for r in range(NCORE):
                p.emit(SY, lambda E, r=r: E.dma_start(
                    out=shard_s[:, 0:H * EV], in_=st_all[r * 64:(r + 1) * 64, :]),
                    reads=["st_all_d"], writes=["shard"], inc=16)
                p.emit(VE, lambda E, r=r: E.scalar_tensor_tensor(
                    out=st_f[:, 0:H * EV], in0=shard_s[:, 0:H * EV],
                    scalar=sm_s[0:64, r:r + 1], in1=st_f[:, 0:H * EV],
                    op0=ALU.mult, op1=ALU.add),
                    reads=["shard", "sm", "st_f"], writes=["st_f"])
            p.emit(VE, lambda E: E.memset(st_b, 0.0), reads=["st_b"], writes=["st_b"])
            p.emit(VE, lambda E: E.tensor_copy(
                st_b[0:64, :].rearrange("q (h e) -> q h e", h=H)[:, 0:H:2, :],
                st_f[:, 0:H * EV].rearrange("q (h e) -> q h e", h=H)[:, 0:H:2, :]),
                reads=["st_f", "st_b"], writes=["st_b"])
            p.emit(VE, lambda E: E.tensor_copy(
                st_b[64:128, :].rearrange("q (h e) -> q h e", h=H)[:, 1:H:2, :],
                st_f[:, 0:H * EV].rearrange("q (h e) -> q h e", h=H)[:, 1:H:2, :]),
                reads=["st_f", "st_b"], writes=["st_b"])

        qT2 = take(4096)
        num2 = take(4352)
        o_tok = take(1024)
        oT = take(4096)
        a2scr = take(544, F32)

        def a2_loads(ti):
            tb = ti % 2
            p.emit(SY, lambda E, ti=ti: E.dma_start(
                out=qT2, in_=qT_d[ti * 128:(ti + 1) * 128, :]),
                reads=["qT_dd"], writes=["qT2"], inc=16)
            p.emit(SY, lambda E, ti=ti: E.dma_start(
                out=num2.rearrange("q (c e) -> q c e", c=NCH),
                in_=num_d[ti * TILE:(ti + 1) * TILE, :].rearrange("(c q) e -> q c e", q=128)),
                reads=["num_dd"], writes=["num2"], inc=16)
            p.emit(SY, lambda E, ti=ti, tb=tb: E.dma_start(
                out=x_sl[tb].rearrange("q (c d) -> q c d", c=NCH),
                in_=out[ti * TILE:(ti + 1) * TILE, :].rearrange("(c q) d -> q c d", q=128)),
                reads=[f"fdram{ti}"], writes=[f"x{tb}"], inc=16)

        def a2_tile(ti):
            tb = ti % 2
            part2 = x_sl[tb]
            for ci in range(NCH):
                for g in range(4):
                    nb = g
                    fns = []
                    for i in range(4):
                        h = g * 4 + i

                        def m3c(E, h=h, i=i, nb=nb, ci=ci,
                                st=(lead(nb) if (i == 0) else False)):
                            return E.matmul(
                                PS[nb][:, i * EV:i * EV + EV],
                                qT2[:, (h // 2) * TILE + ci * 128:(h // 2) * TILE + (ci + 1) * 128],
                                st_b[:, h * EV:(h + 1) * EV],
                                start=st, stop=(i == 3))
                        fns.append(m3c)
                    p.emit(TE, fns, reads=["qT2", "st_b"], writes=[f"PB{nb}"])
                    nsv = num2[:, ci * H * EV + g * 4 * EV: ci * H * EV + (g + 1) * 4 * EV]
                    p.emit(VE, lambda E, nb=nb, nsv=nsv: E.tensor_tensor(
                        a2scr[:, 0:4 * EV], PS[nb][:, 0:4 * EV], nsv, ALU.add),
                        reads=[f"PB{nb}", "num2"], writes=["a2scr"])
                    bk_done(nb)
                    p.emit(VE, lambda E: E.tensor_scalar(
                        out=rcp_s[:, 0:4],
                        in0=a2scr[:, 0:4 * EV].rearrange("q (i e) -> q i e", i=4)[:, :, HD],
                        scalar1=float(ATTEPS), scalar2=None, op0=ALU.add),
                        reads=["a2scr"], writes=["rcp"])
                    p.emit(VE, lambda E: E.reciprocal(rcp_s[:, 0:4], rcp_s[:, 0:4]),
                           reads=["rcp"], writes=["rcp"])
                    ov = o_tok[:, g * 256:(g + 1) * 256].rearrange("q (i e) -> q i e", i=4)
                    p.emit(VE, lambda E, ov=ov: E.tensor_tensor(
                        ov, a2scr[:, 0:4 * EV].rearrange("q (i e) -> q i e", i=4)[:, :, 0:HD],
                        rcp_s[:, 0:4].rearrange("q i -> q i ()").broadcast_to([128, 4, HD]),
                        ALU.mult),
                        reads=["a2scr", "rcp"], writes=["o_tok"])
                fns = []
                for jd in range(DC):
                    def tr(E, jd=jd):
                        return E.transpose(
                            P7[:, jd * 128:(jd + 1) * 128],
                            o_tok[:, jd * 128:(jd + 1) * 128], ident)
                    fns.append(tr)
                p.emit(TE, fns, reads=["o_tok", "ident"], writes=["P7"])
                ov = oT.rearrange("q (jd t) -> q jd t", jd=DC)[:, :, ci * 128:(ci + 1) * 128]
                p.emit(SC, lambda E, ov=ov: E.copy(
                    ov, P7[:].rearrange("q (jd t) -> q jd t", jd=DC)),
                    reads=["P7"], writes=["oT"])
                bk_done(7)
            for c in range(NCH):
                for half in range(2):
                    bank = half
                    fns = []
                    for jd in range(DC):
                        def mm(E, c=c, half=half, jd=jd, bank=bank,
                               st=(lead(bank) if (jd == 0) else False)):
                            return E.matmul(
                                PS[bank][:],
                                oT[:, jd * TILE + c * 128: jd * TILE + (c + 1) * 128],
                                wo_s[:, jd * D + half * 512: jd * D + (half + 1) * 512],
                                start=st, stop=False)
                        fns.append(mm)

                    def mmb(E, half=half, bank=bank):
                        return E.matmul(
                            PS[bank][:], ones_row, bor_s[0:1, half * 512:(half + 1) * 512],
                            start=False, stop=True)
                    fns.append(mmb)
                    p.emit(TE, fns, reads=["oT", "wo", "bo", "ones_row"], writes=[f"PB{bank}"])
                    dv = out_sl[:, c * D + half * 512: c * D + (half + 1) * 512]
                    pv = part2[:, c * D + half * 512: c * D + (half + 1) * 512]
                    p.emit(VE, lambda E, dv=dv, pv=pv, bank=bank: E.tensor_tensor(
                        dv, PS[bank][:], pv, ALU.add),
                        reads=[f"PB{bank}", f"x{tb}"], writes=["out_sl"])
                    bk_done(bank)
            p.emit(SY, lambda E, ti=ti: E.dma_start(
                out=out[ti * TILE:(ti + 1) * TILE, :].rearrange("(c q) d -> q c d", q=128),
                in_=out_sl.rearrange("q (c d) -> q c d", c=NCH)),
                reads=["out_sl"], writes=[f"adram{ti}"], sem="ST", inc=16)

        for t in range(min(2, nown_tiles)):
            b_tile(t)
        s_in_block()
        a2_loads(0)
        for ti in range(nown_tiles):
            a2_tile(ti)
            if ti + 1 < nown_tiles:
                a2_loads(ti + 1)
            if ti + 2 < nown_tiles:
                b_tile(ti + 2)

        total_st = p.cnt["ST0"]
        h_st = p.semh["ST0"]
        p.ops["sync"].append(lambda E, h=h_st, v=total_st: E.wait_ge(h, v))

        block = es.enter_context(nc.Block())

        def replay(name):
            def f(E):
                for op in p.ops[name]:
                    op(E)
            return f
        block.sync(replay("sync"))
        block.tensor(replay("tensor"))
        block.vector(replay("vector"))
        block.scalar(replay("scalar"))
        block.gpsimd(replay("gpsimd"))

    return nc


def _prep(Wq, bq, Wk, bk, Wv, bv, Wo, bo, w_up, w_down):
    f = np.float32
    return {
        "wqT": np.ascontiguousarray(np.asarray(Wq, f).T),
        "wkT": np.ascontiguousarray(np.asarray(Wk, f).T),
        "wvT": np.ascontiguousarray(np.asarray(Wv, f).T),
        "woT": np.ascontiguousarray(np.asarray(Wo, f).T),
        "bq8": np.ascontiguousarray(np.asarray(bq, f).reshape(DC, 128).T),
        "bk8": np.ascontiguousarray(np.asarray(bk, f).reshape(DC, 128).T),
        "bv_row": np.asarray(bv, f).reshape(1, D).copy(),
        "bo_row": np.asarray(bo, f).reshape(1, D).copy(),
        "wupT": np.ascontiguousarray(np.asarray(w_up, f).T),
        "wdnT": np.ascontiguousarray(np.asarray(w_down, f).T),
    }


def kernel(L0_attn_x, L0_attn_Wq, L0_attn_bq, L0_attn_Wk, L0_attn_bk,
           L0_attn_Wv, L0_attn_bv, L0_attn_Wo, L0_attn_bo,
           L0_ffn_w_up, L0_ffn_w_down):
    x = np.ascontiguousarray(np.asarray(L0_attn_x, dtype=np.float32))
    weights = _prep(L0_attn_Wq, L0_attn_bq, L0_attn_Wk, L0_attn_bk, L0_attn_Wv,
                    L0_attn_bv, L0_attn_Wo, L0_attn_bo, L0_ffn_w_up, L0_ffn_w_down)

    OWN = S // 4
    nown = OWN // TILE

    nc = build_nc(0, nown)
    in_maps = []
    for core in range(NCORE):
        b, pq = core // 4, core % 4
        xb = x[b]
        start = pq * OWN
        sm = np.zeros((128, NCORE), np.float32)
        for r in range(NCORE):
            if r // 4 == b and r % 4 < pq:
                sm[:, r] = 1.0
        m = {"x_own": np.ascontiguousarray(xb[start:start + OWN]), "smask": sm}
        m.update(weights)
        in_maps.append(m)
    res = run_bass_kernel_spmd(nc, in_maps, core_ids=list(range(NCORE)))
    outp = np.empty((B, S, D), np.float32)
    for core in range(NCORE):
        b, pq = core // 4, core % 4
        outp[b, pq * OWN:(pq + 1) * OWN] = np.asarray(res.results[core]["out"])
    return outp


# version 16
# speedup vs baseline: 1.1991x; 1.1991x over previous
import numpy as np
from contextlib import ExitStack

import concourse.bass as bass
import concourse.mybir as mybir
from concourse.bass_utils import run_bass_kernel_spmd

F32 = mybir.dt.float32
BF16 = mybir.dt.bfloat16
AF = mybir.ActivationFunctionType
ALU = mybir.AluOpType
AX = mybir.AxisListType

B, S, D, H, HD, DFF = 2, 8192, 1024, 16, 64, 4096
NCORE = 8
TILE = 512
CH = 128
EV = 68
LNEPS = 1e-5
ATTEPS = 1e-6
RMSEPS = 1e-8
NCH = TILE // CH
DC = D // 128
FC = DFF // 128

ENGS = ("sync", "tensor", "vector", "scalar", "gpsimd")
ESEM = {"tensor": "PE", "vector": "DV", "scalar": "AC", "gpsimd": "GP", "sync": "LD"}
SY, TE, VE, SC, GP = ENGS


class Prog:

    def __init__(self, nc):
        self.nc = nc
        self.ops = {e: [] for e in ENGS}
        self.semh = {}
        self.cnt = {}
        self.seen = {e: {} for e in ENGS}
        self.buf = {}
        self.dmarr = {}

    def mksem(self, es, *names):
        for n in names:
            self.semh[n] = es.enter_context(self.nc.semaphore(n))
            self.cnt[n] = 0

    def _wait(self, eng, sname, v):
        if self.seen[eng].get(sname, 0) >= v:
            return
        self.seen[eng][sname] = v
        h = self.semh[sname]
        self.ops[eng].append(lambda E, h=h, v=v: E.wait_ge(h, v))

    DMA_POOLS = {"LD": ["LD0", "LD1", "LD2", "LD3"],
                 "LDG": ["LG0", "LG1", "LG2", "LG3"],
                 "ST": ["ST0"]}

    def emit(self, eng, fns, reads=(), writes=(), sem=None, inc=1):
        if not isinstance(fns, (list, tuple)):
            fns = [fns]
        if sem is None:
            sem = ESEM[eng]
        if sem in self.DMA_POOLS:
            pool = self.DMA_POOLS[sem]
            idx = self.dmarr.get(sem, 0)
            self.dmarr[sem] = (idx + 1) % len(pool)
            sem = pool[idx]
            if self.cnt[sem] > 0:
                self._wait(eng, sem, self.cnt[sem])
        need = {}
        for b in reads:
            st = self.buf.get(b)
            if st and st["w"]:
                s, v = st["w"]
                need[s] = max(need.get(s, 0), v)
        for b in writes:
            st = self.buf.get(b)
            if st:
                toks = ([st["w"]] if st["w"] else []) + st["r"]
                for s, v in toks:
                    need[s] = max(need.get(s, 0), v)
        for s in sorted(need):
            self._wait(eng, s, need[s])
        for f in fns[:-1]:
            self.ops[eng].append(f)
        self.cnt[sem] += inc
        v = self.cnt[sem]
        h = self.semh[sem]
        last = fns[-1]
        self.ops[eng].append(lambda E, last=last, h=h, inc=inc: last(E).then_inc(h, inc))
        tok = (sem, v)
        for b in writes:
            self.buf[b] = {"w": tok, "r": []}
        for b in reads:
            st = self.buf.setdefault(b, {"w": None, "r": []})
            st["r"].append(tok)
        return tok

    def barrier(self, exclude=("CC",)):
        cur = {s: v for s, v in self.cnt.items() if v > 0 and s not in exclude}
        for e in ENGS:
            for s in sorted(cur):
                self._wait(e, s, cur[s])


def build_nc(npre_slabs, nown_tiles):
    OWN = nown_tiles * TILE
    PRE = max(npre_slabs * TILE, TILE)
    nc = bass.Bass("TRN2", target_bir_lowering=False)

    x_own = nc.dram_tensor("x_own", [OWN, D], F32, kind="ExternalInput").ap()
    smask = nc.dram_tensor("smask", [128, NCORE], F32, kind="ExternalInput").ap()
    wqT = nc.dram_tensor("wqT", [D, D], F32, kind="ExternalInput").ap()
    wkT = nc.dram_tensor("wkT", [D, D], F32, kind="ExternalInput").ap()
    wvT = nc.dram_tensor("wvT", [D, D], F32, kind="ExternalInput").ap()
    woT = nc.dram_tensor("woT", [D, D], F32, kind="ExternalInput").ap()
    bq8 = nc.dram_tensor("bq8", [128, DC], F32, kind="ExternalInput").ap()
    bk8 = nc.dram_tensor("bk8", [128, DC], F32, kind="ExternalInput").ap()
    bv_row = nc.dram_tensor("bv_row", [1, D], F32, kind="ExternalInput").ap()
    bo_row = nc.dram_tensor("bo_row", [1, D], F32, kind="ExternalInput").ap()
    wupT = nc.dram_tensor("wupT", [D, DFF], F32, kind="ExternalInput").ap()
    wdnT = nc.dram_tensor("wdnT", [DFF, D], F32, kind="ExternalInput").ap()
    out = nc.dram_tensor("out", [OWN, D], F32, kind="ExternalOutput").ap()
    wup_t = nc.dram_tensor("wup_t", [D, DFF], BF16, kind="Internal").ap()
    wdn_t = nc.dram_tensor("wdn_t", [DFF, D], BF16, kind="Internal").ap()
    qT_d = nc.dram_tensor("qT_d", [nown_tiles * 128, DC * TILE], BF16, kind="Internal").ap()
    num_d = nc.dram_tensor("num_d", [OWN, H * EV], BF16, kind="Internal").ap()
    st_loc = nc.dram_tensor("st_loc", [64, H * EV], F32, kind="Internal").ap()
    st_all = nc.dram_tensor("st_all", [NCORE * 64, H * EV], F32, kind="Internal").ap()

    with ExitStack() as es:
        pst = lambda name, shape, dt: es.enter_context(nc.psum_tensor(name, shape, dt))

        ARENA_BC = 106000
        arena = es.enter_context(nc.sbuf_tensor("arena", [128, ARENA_BC], BF16))
        cur = [0]

        def take(bcols, dt=BF16, parts=128):
            o = cur[0]
            bcols_al = (bcols + 15) // 16 * 16
            cur[0] += bcols_al
            assert cur[0] <= ARENA_BC, f"arena overflow at {cur[0]}"
            v = arena[0:parts, o:o + bcols]
            if dt == F32:
                v = v.bitcast(F32)
            return v

        x_sl = [take(8192, F32) for _ in range(2)]
        out_sl = take(8192, F32)
        xn_tok = take(4096)
        xnT = take(4096)
        st_f = take(2176, F32, parts=64)
        st_b = take(1088)
        bn_s = take(96, F32)
        mv_s = take(16, F32)
        rstd_s = take(8, F32)
        cA_s = take(8, F32)
        sffn_s = take(8, F32)
        cB_s = take(8, F32)
        lnt_s = take(32, F32)
        rcp_s = take(8, F32)
        s2col = take(8, F32)
        tpart = take(4, F32)
        tred = take(2, F32)
        thr_s = take(4, F32)
        nthr_s = take(4, F32)
        lneps_c = take(2, F32)
        mask8 = take(512)
        ident = take(128)
        ones_col = take(1)
        ones_cf = take(2, F32)
        bq_s = take(16, F32)
        bk_s = take(16, F32)
        sm_s = take(16, F32)
        shard_s = take(2176, F32, parts=64)
        ones_row = take(128, parts=1)
        ones_rf = take(256, F32, parts=1)
        ones_1f = take(2, F32, parts=1)
        bvr_s = take(1024, parts=1)
        bor_s = take(1024, parts=1)
        s2row = take(1024, F32, parts=1)
        trow_s = take(4, F32, parts=1)

        twf = [take(1024, F32) for _ in range(2)]
        tg1 = take(1024, F32)
        tout = [take(512) for _ in range(2)]
        dyn_base = cur[0]

        wo_s = take(8192)
        wts_top = cur[0]
        wq_s = take(8192)
        wk_s = take(8192)
        wv_s = take(8192)
        vtok = take(4352)
        ph_e = [take(1024, F32) for _ in range(2)]
        ph_r = [take(1024, F32) for _ in range(2)]
        var_base = cur[0]

        ktok = take(4096)
        qT = kT = ktok_c = o_tok = oT = am_s = None

        PS = [pst(f"P{i}", [128, 512], F32) for i in range(7)]
        P7 = pst("P7", [128, 1024], BF16)

        p = Prog(nc)
        p.mksem(es, "LD0", "LD1", "LD2", "LD3", "LG0", "LG1", "LG2", "LG3", "ST0", "CC", "PE", "DV", "AC", "GP")

        bank_fresh = {k: True for k in range(8)}

        def lead(i):
            f = bank_fresh[i]
            bank_fresh[i] = False
            return f

        def bk_done(i):
            bank_fresh[i] = True

        p.emit(GP, lambda E: E.memset(ident, 0.0), writes=["ident"])
        p.emit(GP, lambda E: E.affine_select(
            out=ident, in_=ident, compare_op=ALU.not_equal, fill=1.0,
            base=0, pattern=[[-1, 128]], channel_multiplier=1),
            reads=["ident"], writes=["ident"])
        p.emit(GP, lambda E: E.memset(mask8, 1.0), writes=["mask8"])
        for i in range(4):
            p.emit(GP, lambda E, i=i: E.affine_select(
                out=mask8[:, i * CH:(i + 1) * CH], in_=mask8[:, i * CH:(i + 1) * CH],
                compare_op=ALU.is_ge, fill=0.0, base=0,
                pattern=[[1, CH]], channel_multiplier=-1),
                reads=["mask8"], writes=["mask8"])
        p.emit(GP, lambda E: E.memset(lneps_c[:, 0:1], float(LNEPS)), writes=["lneps"])
        for t, n in [(ones_row, "ones_row"), (ones_col, "ones_col"),
                     (ones_rf, "ones_rf"), (ones_cf, "ones_cf"), (ones_1f, "ones_1f")]:
            p.emit(GP, lambda E, t=t: E.memset(t, 1.0), writes=[n])
        p.emit(VE, lambda E: E.memset(st_f, 0.0), writes=["st_f"])
        p.emit(VE, lambda E: E.memset(st_b, 0.0), writes=["st_b"])
        p.emit(VE, lambda E: E.memset(tpart, 0.0), writes=["tpart"])
        p.emit(VE, lambda E: E.memset(vtok, 0.0), writes=["vtok"])
        p.emit(VE, lambda E: E.memset(
            vtok.rearrange("q (c h e) -> q c h e", c=NCH, h=H)[:, :, :, HD:HD + 1], 1.0),
            reads=["vtok"], writes=["vtok"])

        for w_ap, tile, name in [(wqT, wq_s, "wq"), (wkT, wk_s, "wk"),
                                 (wvT, wv_s, "wv"), (woT, wo_s, "wo")]:
            p.emit(GP, lambda E, w=w_ap, t=tile: E.dma_start(
                out=t.rearrange("q (c d) -> q c d", c=DC),
                in_=w.rearrange("(c q) d -> q c d", q=128)),
                writes=[name], sem="LDG", inc=16)
        p.emit(SY, lambda E: E.dma_start(out=bq_s[:, 0:DC], in_=bq8), writes=["bq"], inc=16)
        p.emit(SY, lambda E: E.dma_start(out=bk_s[:, 0:DC], in_=bk8), writes=["bk"], inc=16)
        p.emit(GP, lambda E: E.dma_start(out=bvr_s, in_=bv_row), writes=["bv"], sem="LDG", inc=16)
        p.emit(GP, lambda E: E.dma_start(out=bor_s, in_=bo_row), writes=["bo"], sem="LDG", inc=16)
        p.emit(SY, lambda E: E.dma_start(out=sm_s[:, 0:NCORE], in_=smask),
               writes=["sm"], inc=16)

        def ln_stats(xb, xname, sffn=False):
            xs = x_sl[xb]
            for c in range(NCH):
                p.emit(VE, lambda E, c=c, xs=xs: E.bn_stats(
                    bn_s[:, c * 12:c * 12 + 6], xs[:, c * D:c * D + 512]),
                    reads=[xname], writes=[f"bn{c}"])
                p.emit(VE, lambda E, c=c, xs=xs: E.bn_stats(
                    bn_s[:, c * 12 + 6:c * 12 + 12], xs[:, c * D + 512:(c + 1) * D]),
                    reads=[xname, f"bn{c}"], writes=[f"bn{c}"])
                p.emit(VE, lambda E, c=c: E.bn_aggr(
                    mv_s[:, 2 * c:2 * c + 2], bn_s[:, c * 12:(c + 1) * 12]),
                    reads=[f"bn{c}"], writes=[f"mv{c}"])
            mvr = [f"mv{c}" for c in range(NCH)]
            p.emit(SC, lambda E: E.activation(
                lnt_s[:, 0:NCH], mv_s[:, 1:2 * NCH:2], AF.Ln, bias=lneps_c[:, 0:1], scale=1.0),
                reads=mvr + ["lneps"], writes=["lnt0"])
            p.emit(SC, lambda E: E.activation(
                rstd_s[:, 0:NCH], lnt_s[:, 0:NCH], AF.Exp, scale=-0.5),
                reads=["lnt0"], writes=["rstd"])
            p.emit(VE, lambda E: E.scalar_tensor_tensor(
                out=cA_s[:, 0:NCH], in0=mv_s[:, 0:2 * NCH:2], scalar=-1.0,
                in1=rstd_s[:, 0:NCH], op0=ALU.mult, op1=ALU.mult),
                reads=mvr + ["rstd"], writes=["cA"])
            if sffn:
                p.emit(SC, lambda E: E.activation(
                    lnt_s[:, NCH:2 * NCH], mv_s[:, 1:2 * NCH:2], AF.Ln, scale=1.0),
                    reads=mvr, writes=["lnt1"])
                p.emit(VE, lambda E: E.tensor_sub(
                    lnt_s[:, 2 * NCH:3 * NCH], lnt_s[:, NCH:2 * NCH], lnt_s[:, 0:NCH]),
                    reads=["lnt0", "lnt1"], writes=["lnt2"])
                p.emit(SC, lambda E: E.activation(
                    lnt_s[:, 3 * NCH:4 * NCH], lnt_s[:, 2 * NCH:3 * NCH], AF.Exp,
                    scale=0.5),
                    reads=["lnt2"], writes=["lnt3"])
                p.emit(VE, lambda E: E.tensor_scalar(
                    out=lnt_s[:, 3 * NCH:4 * NCH], in0=lnt_s[:, 3 * NCH:4 * NCH],
                    scalar1=float(RMSEPS), scalar2=None, op0=ALU.add),
                    reads=["lnt3"], writes=["lnt3"])
                p.emit(VE, lambda E: E.reciprocal(
                    lnt_s[:, 2 * NCH:3 * NCH], lnt_s[:, 3 * NCH:4 * NCH]),
                    reads=["lnt3"], writes=["lnt2"])
                p.emit(VE, lambda E: E.tensor_mul(
                    sffn_s[:, 0:NCH], rstd_s[:, 0:NCH], lnt_s[:, 2 * NCH:3 * NCH]),
                    reads=["rstd", "lnt2"], writes=["sffn"])
                p.emit(VE, lambda E: E.scalar_tensor_tensor(
                    out=cB_s[:, 0:NCH], in0=mv_s[:, 0:2 * NCH:2], scalar=-1.0,
                    in1=sffn_s[:, 0:NCH], op0=ALU.mult, op1=ALU.mult),
                    reads=mvr + ["sffn"], writes=["cB"])

        def xn_apply(xb, xname, scale_t, scale_name, shift_t, shift_name):
            xs = x_sl[xb]
            for c in range(NCH):
                p.emit(SC, lambda E, c=c, xs=xs: E.activation(
                    xn_tok[:, c * D:(c + 1) * D], xs[:, c * D:(c + 1) * D], AF.Identity,
                    bias=shift_t[:, c:c + 1], scale=scale_t[:, c:c + 1]),
                    reads=[xname, scale_name, shift_name], writes=["xn_tok"])

        def transpose_tok_to_ft(src, src_name, dst, dst_name):
            for c in range(NCH):
                fns = []
                for jm in range(DC):
                    def tr(E, c=c, jm=jm):
                        return E.transpose(
                            P7[:, jm * 128:(jm + 1) * 128],
                            src[:, c * D + jm * 128:c * D + (jm + 1) * 128], ident)
                    fns.append(tr)
                p.emit(TE, fns, reads=[src_name, "ident"], writes=["P7"])
                ov = dst.rearrange("q (jm t) -> q jm t", jm=DC)[:, :, c * 128:(c + 1) * 128]
                p.emit(SC, lambda E, ov=ov: E.copy(
                    ov, P7[:].rearrange("q (jm t) -> q jm t", jm=DC)),
                    reads=["P7"], writes=[dst_name])
                bk_done(7)

        def transpose_ft_to_tok(src, src_name, dst, dst_name):
            for c in range(NCH):
                fns = []
                for jd in range(DC):
                    def tr(E, c=c, jd=jd):
                        return E.transpose(
                            P7[:, jd * 128:(jd + 1) * 128],
                            src[:, jd * TILE + c * 128:jd * TILE + (c + 1) * 128], ident)
                    fns.append(tr)
                p.emit(TE, fns, reads=[src_name, "ident"], writes=["P7"])
                p.emit(VE, lambda E, c=c: E.tensor_copy(
                    dst[:, c * D:(c + 1) * D], P7[:]),
                    reads=["P7"], writes=[dst_name])
                bk_done(7)

        def proj_feature(wtile, wname, btile, bname, dst, dst_name):
            for jd in range(DC):
                bank = jd % 4
                pp = jd % 2
                fns = []
                for jm in range(DC):
                    def mm(E, jd=jd, jm=jm, bank=bank, st=(lead(bank) if (jm == 0) else False)):
                        return E.matmul(
                            PS[bank][:],
                            wtile[:, jm * D + jd * 128: jm * D + (jd + 1) * 128],
                            xnT[:, jm * TILE:(jm + 1) * TILE],
                            start=st, stop=(jm == DC - 1))
                    fns.append(mm)
                p.emit(TE, fns, reads=[wname, "xnT"], writes=[f"PB{bank}"])
                dv = dst[:, jd * TILE:(jd + 1) * TILE]
                p.emit(SC, lambda E, jd=jd, bank=bank, pp=pp: E.activation(
                    ph_e[pp][:, 0:512], PS[bank][:], AF.Exp, bias=btile[:, jd:jd + 1],
                    scale=1.0),
                    reads=[f"PB{bank}", bname], writes=[f"ph_e{pp}"])
                p.emit(SC, lambda E, jd=jd, bank=bank, pp=pp: E.activation(
                    ph_r[pp][:, 0:512], PS[bank][:], AF.Relu, bias=btile[:, jd:jd + 1],
                    scale=1.0),
                    reads=[f"PB{bank}", bname], writes=[f"ph_r{pp}"])
                p.emit(VE, lambda E, dv=dv, pp=pp: E.scalar_tensor_tensor(
                    out=dv, in0=ph_e[pp][:, 0:512], scalar=1.0, in1=ph_r[pp][:, 0:512],
                    op0=ALU.min, op1=ALU.add),
                    reads=[f"ph_e{pp}", f"ph_r{pp}"], writes=[dst_name])
                bk_done(bank)
                tern_tick()

        def proj_token(wtile, wname, brow, bname, writer, banks):
            for c in range(NCH):
                for half in range(2):
                    bank = banks[(c * 2 + half) % len(banks)]
                    fns = []
                    for jm in range(DC):
                        def mm(E, c=c, half=half, jm=jm, bank=bank, st=(lead(bank) if (jm == 0) else False)):
                            return E.matmul(
                                PS[bank][:],
                                xnT[:, jm * TILE + c * 128: jm * TILE + (c + 1) * 128],
                                wtile[:, jm * D + half * 512: jm * D + (half + 1) * 512],
                                start=st, stop=False)
                        fns.append(mm)

                    def mmb(E, half=half, bank=bank):
                        return E.matmul(
                            PS[bank][:], ones_row, brow[0:1, half * 512:(half + 1) * 512],
                            start=False, stop=True)
                    fns.append(mmb)
                    p.emit(TE, fns, reads=[wname, "xnT", bname, "ones_row"],
                           writes=[f"PB{bank}"])
                    writer(c, half, bank)
                    bk_done(bank)
                    tern_tick()

        SDSEG = [(4, 0, 6), (5, 6, 11), (6, 11, 16)]

        def state_fold():
            for bank, h0, h1 in SDSEG:
                p.emit(VE, lambda E, bank=bank, h0=h0, h1=h1: E.tensor_tensor(
                    st_f[:, h0 * EV:h1 * EV], st_f[:, h0 * EV:h1 * EV],
                    PS[bank][0:64, 0:(h1 - h0) * EV], ALU.add),
                    reads=[f"PB{bank}", "st_f"], writes=["st_f"])
                bk_done(bank)
            p.emit(VE, lambda E: E.tensor_copy(
                st_b[0:64, :].rearrange("q (h e) -> q h e", h=H)[:, 0:H:2, :],
                st_f[:, 0:H * EV].rearrange("q (h e) -> q h e", h=H)[:, 0:H:2, :]),
                reads=["st_f"], writes=["st_b"])
            p.emit(VE, lambda E: E.tensor_copy(
                st_b[64:128, :].rearrange("q (h e) -> q h e", h=H)[:, 1:H:2, :],
                st_f[:, 0:H * EV].rearrange("q (h e) -> q h e", h=H)[:, 1:H:2, :]),
                reads=["st_f"], writes=["st_b"])

        scanref = {}

        def scan_pre(ci):
            qT = scanref["qT"]; kT = scanref["kT"]
            ktok_c = scanref["ktok_c"][ci % 2]
            am_s = scanref["am_s"]
            fns = []
            for jd in range(DC):
                def tr(E, jd=jd, ci=ci):
                    return E.transpose(
                        P7[:, jd * 128:(jd + 1) * 128],
                        kT[:, jd * TILE + ci * 128:jd * TILE + (ci + 1) * 128], ident)
                fns.append(tr)
            p.emit(TE, fns, reads=["kT", "ident"], writes=["P7"])
            p.emit(SC, lambda E, ktok_c=ktok_c: E.copy(ktok_c[:, 0:D], P7[:]),
                   reads=["P7"], writes=[f"ktokc{ci % 2}"])
            bk_done(7)
            fns = []
            wr = set()
            for h in range(H):
                bank = h % 4
                col = (h // 4) * 128

                def m1(E, h=h, bank=bank, col=col, ci=ci,
                       st=(lead(bank) if (h // 4 == 0) else False)):
                    return E.matmul(
                        PS[bank][:, col:col + 128],
                        kT[(h % 2) * 64:(h % 2) * 64 + 64,
                           (h // 2) * TILE + ci * 128:(h // 2) * TILE + (ci + 1) * 128],
                        qT[(h % 2) * 64:(h % 2) * 64 + 64,
                           (h // 2) * TILE + ci * 128:(h // 2) * TILE + (ci + 1) * 128],
                        start=st, stop=(h >= H - 4))
                fns.append(m1)
                wr.add(f"PB{bank}")
            p.emit(TE, fns, reads=["kT", "qT"], writes=sorted(wr))
            for bank in range(4):
                ov = am_s.rearrange("q (g b c) -> q g b c", g=4, b=4)[:, :, bank, :]
                p.emit(VE, lambda E, bank=bank, ov=ov: E.tensor_tensor(
                    ov, PS[bank][:].rearrange("q (g c) -> q g c", g=4),
                    mask8.rearrange("q (g c) -> q g c", g=4), ALU.mult),
                    reads=[f"PB{bank}", "mask8"], writes=[f"am{bank}"])
                bk_done(bank)

        def scan_groups(ci):
            qT = scanref["qT"]; am_s = scanref["am_s"]; numsave = scanref["numsave"]
            amr = [f"am{b}" for b in range(4)]
            for g in range(4):
                nb = 4 + (g % 2)
                fns = []
                for i in range(4):
                    h = g * 4 + i

                    def m2(E, h=h, i=i, nb=nb, ci=ci, st=(lead(nb) if (i == 0) else False)):
                        return E.matmul(
                            PS[nb][:, i * EV:i * EV + EV],
                            am_s[:, h * 128:(h + 1) * 128],
                            vtok[:, ci * H * EV + h * EV: ci * H * EV + (h + 1) * EV],
                            start=st, stop=False)

                    def m3(E, h=h, i=i, nb=nb, ci=ci):
                        return E.matmul(
                            PS[nb][:, i * EV:i * EV + EV],
                            qT[:, (h // 2) * TILE + ci * 128:(h // 2) * TILE + (ci + 1) * 128],
                            st_b[:, h * EV:(h + 1) * EV],
                            start=False, stop=(i == 3))
                    fns += [m2, m3]
                p.emit(TE, fns, reads=amr + ["vtok", "qT", "st_b"], writes=[f"PB{nb}"])
                p.emit(SC, lambda E, nb=nb, g=g, ci=ci: E.copy(
                    numsave[:, ci * H * EV + g * 4 * EV: ci * H * EV + (g + 1) * 4 * EV],
                    PS[nb][:, 0:4 * EV]),
                    reads=[f"PB{nb}"], writes=[f"nsv{ci}_{g}"])
                bk_done(nb)
                tern_tick()

        def scan_tail(ci):
            ktok_c = scanref["ktok_c"][ci % 2]
            numsave = scanref["numsave"]
            fns = []
            wr = set()
            for bank, h0, h1 in SDSEG:
                for i in range(h1 - h0):
                    h = h0 + i

                    def m4(E, h=h, i=i, bank=bank, ci=ci, ktok_c=ktok_c,
                           last=(i == h1 - h0 - 1), st=(lead(bank) if (i == 0) else False)):
                        return E.matmul(
                            PS[bank][0:64, i * EV:i * EV + EV],
                            ktok_c[:, h * HD:(h + 1) * HD],
                            vtok[:, ci * H * EV + h * EV: ci * H * EV + (h + 1) * EV],
                            start=st, stop=last)
                    fns.append(m4)
                    wr.add(f"PB{bank}")
            p.emit(TE, fns, reads=[f"ktokc{ci % 2}", "vtok"], writes=sorted(wr))
            state_fold()
            if ci == NCH - 1:
                ti = scanref["ti"]
                p.emit(SY, lambda E, ti=ti: E.dma_start(
                    out=num_d[ti * TILE:(ti + 1) * TILE, :].rearrange(
                        "(c q) e -> q c e", q=128),
                    in_=numsave.rearrange("q (c e) -> q c e", c=NCH)),
                    reads=[f"nsv{c}_{g}" for c in range(NCH) for g in range(4)],
                    writes=["num_dd"], sem="LD", inc=16)

        tern_chunks = [("up", j) for j in range(64)] + [("dn", j) for j in range(64)]

        def tern_load(kind, j, b):
            if kind == "up":
                src = wupT[(j // 8) * 128:(j // 8 + 1) * 128,
                           (j % 8) * 512:(j % 8 + 1) * 512]
            else:
                src = wdnT[(j // 2) * 128:(j // 2 + 1) * 128,
                           (j % 2) * 512:(j % 2 + 1) * 512]
            p.emit(GP, lambda E, s=src, b=b: E.dma_start(out=twf[b], in_=s),
                   writes=[f"twf{b}"], sem="LDG", inc=16)

        def tern_abs(kind, j, b):
            col = 0 if kind == "up" else 1
            p.emit(SC, lambda E, b=b: E.activation(
                tg1, twf[b], AF.Abs, accum_out=tred[:, 0:1]),
                reads=[f"twf{b}"], writes=["tred", "tg1"])
            p.emit(VE, lambda E, col=col: E.tensor_tensor(
                tpart[:, col:col + 1], tpart[:, col:col + 1], tred[:, 0:1], ALU.add),
                reads=["tred", "tpart"], writes=["tpart"])

        def tern_finalize():
            p.emit(TE, lambda E, st=lead(6): E.matmul(
                PS[6][0:1, 0:2], ones_cf[:, 0:1], tpart[:, 0:2], start=st, stop=True),
                reads=["tpart", "ones_cf"], writes=["PB6"])
            p.emit(SC, lambda E: E.activation(
                trow_s[0:1, 0:2], PS[6][0:1, 0:2], AF.Copy,
                scale=float(0.5 / (D * DFF))),
                reads=["PB6"], writes=["trow"])
            bk_done(6)
            p.emit(TE, lambda E, st=lead(6): E.matmul(
                PS[6][:, 0:2], ones_rf[0:1, 0:128], trow_s[0:1, 0:2],
                start=st, stop=True),
                reads=["trow", "ones_rf"], writes=["PB6"])
            p.emit(VE, lambda E: E.tensor_copy(thr_s[:, 0:2], PS[6][:, 0:2]),
                   reads=["PB6"], writes=["thr"])
            p.emit(VE, lambda E: E.tensor_scalar(
                out=nthr_s[:, 0:2], in0=thr_s[:, 0:2], scalar1=-1.0, scalar2=None,
                op0=ALU.mult),
                reads=["thr"], writes=["nthr"])
            bk_done(6)

        def tern_quant(kind, j, b):
            col = 0 if kind == "up" else 1
            p.emit(VE, lambda E, b=b, col=col: E.tensor_scalar(
                out=tg1, in0=twf[b], scalar1=nthr_s[:, col:col + 1], scalar2=None,
                op0=ALU.is_lt),
                reads=[f"twf{b}", "nthr"], writes=["tg1"])
            p.emit(VE, lambda E, b=b, col=col: E.scalar_tensor_tensor(
                out=tout[b], in0=twf[b], scalar=thr_s[:, col:col + 1], in1=tg1,
                op0=ALU.is_gt, op1=ALU.subtract),
                reads=[f"twf{b}", "thr", "tg1"], writes=[f"tout{b}"])
            if kind == "up":
                dst = wup_t[(j // 8) * 128:(j // 8 + 1) * 128,
                            (j % 8) * 512:(j % 8 + 1) * 512]
                scr = "scr_up"
            else:
                dst = wdn_t[(j // 2) * 128:(j // 2 + 1) * 128,
                            (j % 2) * 512:(j % 2 + 1) * 512]
                scr = "scr_dn"
            p.emit(SY, lambda E, d=dst, b=b: E.dma_start(out=d, in_=tout[b]),
                   reads=[f"tout{b}"], writes=[scr], sem="LD", inc=16)

        tst = {"a": 0, "b": 0, "fin": False}
        quant_done = set()

        def tern_step(n):
            for _ in range(n):
                if tst["a"] < len(tern_chunks):
                    k, j = tern_chunks[tst["a"]]
                    tern_load(k, j, tst["b"])
                    tern_abs(k, j, tst["b"])
                    tst["a"] += 1
                    tst["b"] ^= 1

        def tern_abs_flush():
            while tst["a"] < len(tern_chunks):
                tern_step(1)
            if not tst["fin"]:
                tern_finalize()
                tst["fin"] = True

        def tern_quant_chunks(chunks):
            for k, j in chunks:
                if (k, j) in quant_done:
                    continue
                quant_done.add((k, j))
                tern_load(k, j, tst["b"])
                tern_quant(k, j, tst["b"])
                tst["b"] ^= 1

        def tern_tick(n=2):
            if tst["a"] < len(tern_chunks):
                tern_step(n)
            elif not tst["fin"]:
                tern_finalize()
                tst["fin"] = True
            else:
                tern_quant_step(n)

        qsched = {"i": 0}

        def tern_quant_step(n):
            c = 0
            while qsched["i"] < len(tern_chunks) and c < n:
                k, j = tern_chunks[qsched["i"]]
                qsched["i"] += 1
                if (k, j) in quant_done:
                    continue
                tern_quant_chunks([(k, j)])
                c += 1

        p.barrier()
        cur[0] = var_base
        qT = take(4096)
        kT = take(4096)
        ktok_c = [take(1024) for _ in range(2)]
        am_s = take(2048)
        numsave = take(4352)
        scanref["qT"] = qT
        scanref["kT"] = kT
        scanref["ktok_c"] = ktok_c
        scanref["am_s"] = am_s
        scanref["numsave"] = numsave
        def a1_load_stats(ti):
            xb = ti % 2
            xname = f"x{xb}"
            p.emit(SY, lambda E, ti=ti, xb=xb: E.dma_start(
                out=x_sl[xb].rearrange("q (c d) -> q c d", c=NCH),
                in_=x_own[ti * TILE:(ti + 1) * TILE, :].rearrange("(c q) d -> q c d", q=128)),
                writes=[xname], inc=16)
            ln_stats(xb, xname)

        a1_load_stats(0)
        for ti in range(nown_tiles):
            xb = ti % 2
            xname = f"x{xb}"
            xn_apply(xb, xname, rstd_s, "rstd", cA_s, "cA")
            transpose_tok_to_ft(xn_tok, "xn_tok", xnT, "xnT")
            proj_feature(wq_s, "wq", bq_s, "bq", qT, "qT")
            proj_feature(wk_s, "wk", bk_s, "bk", kT, "kT")
            scanref["ti"] = ti
            p.emit(SY, lambda E, ti=ti: E.dma_start(
                out=qT_d[ti * 128:(ti + 1) * 128, :], in_=qT),
                reads=["qT"], writes=["qT_dd"], sem="LD", inc=16)
            if ti + 1 < nown_tiles:
                a1_load_stats(ti + 1)

            def v_writer(c, half, bank):
                ov = vtok.rearrange("q (c h e) -> q c h e", c=NCH, h=H)[
                    :, c, half * 8:(half + 1) * 8, 0:HD]
                p.emit(SC, lambda E, ov=ov, bank=bank: E.copy(
                    ov, PS[bank][:].rearrange("q (h e) -> q h e", h=8)),
                    reads=[f"PB{bank}"], writes=["vtok"])
            proj_token(wv_s, "wv", bvr_s, "bv", v_writer, [0, 1, 2, 3])

            scan_pre(0)
            for ci in range(NCH):
                scan_groups(ci)
                if ci < NCH - 1:
                    scan_pre(ci + 1)
                scan_tail(ci)

        tern_abs_flush()
        tern_quant_step(len(tern_chunks))

        p.emit(SY, lambda E: E.dma_start(out=st_loc, in_=st_f[:, 0:H * EV]),
               reads=["st_f"], writes=["st_loc_d"], sem="LD", inc=16)
        p.emit(GP, lambda E: E.collective_compute(
            "AllGather", ALU.bypass,
            replica_groups=[list(range(NCORE))],
            ins=[st_loc.opt()], outs=[st_all.opt()]),
            reads=["st_loc_d"], writes=["st_all_d"], sem="CC", inc=1)
        p.barrier()
        cur[0] = wts_top
        hT = take(16384)
        fwu = [take(4096) for _ in range(2)]
        fwd = [take(2048) for _ in range(2)]
        hsq = [take(512) for _ in range(2)]

        def b_load_stats(ti):
            xb = ti % 2
            xname = f"x{xb}"
            p.emit(SY, lambda E, ti=ti, xb=xb: E.dma_start(
                out=x_sl[xb].rearrange("q (c d) -> q c d", c=NCH),
                in_=x_own[ti * TILE:(ti + 1) * TILE, :].rearrange("(c q) d -> q c d", q=128)),
                writes=[xname], inc=16)
            ln_stats(xb, xname, sffn=True)

        b_load_stats(0)
        for ti in range(nown_tiles):
            xb = ti % 2
            xname = f"x{xb}"
            xn_apply(xb, xname, sffn_s, "sffn", cB_s, "cB")
            transpose_tok_to_ft(xn_tok, "xn_tok", xnT, "xnT")

            for fs in range(DFF // 512):
                fb = fs % 2
                tern_quant_chunks([("up", jm * 8 + fs) for jm in range(DC)])
                src = wup_t.rearrange("(jm q) f -> q jm f", q=128)[:, :, fs * 512:(fs + 1) * 512]
                p.emit(GP, lambda E, src=src, fb=fb: E.dma_start(
                    out=fwu[fb].rearrange("q (jm f) -> q jm f", jm=DC), in_=src),
                    reads=["scr_up"], writes=[f"fwu{fb}"], sem="LDG", inc=16)
                for lf in range(4):
                    jf = fs * 4 + lf
                    bank = jf % 4
                    fns = []
                    for jm in range(DC):
                        def mm(E, jm=jm, lf=lf, fb=fb, bank=bank, st=(lead(bank) if (jm == 0) else False)):
                            return E.matmul(
                                PS[bank][:],
                                fwu[fb][:, jm * 512 + lf * 128: jm * 512 + (lf + 1) * 128],
                                xnT[:, jm * TILE:(jm + 1) * TILE],
                                start=st, stop=(jm == DC - 1))
                        fns.append(mm)
                    p.emit(TE, fns, reads=[f"fwu{fb}", "xnT"], writes=[f"PB{bank}"])
                    if jf > 0:
                        pj, phb = jf - 1, (jf - 1) % 2
                        p.emit(TE, lambda E, pj=pj, phb=phb,
                               st=(lead(6) if pj == 0 else False): E.matmul(
                            PS[6][0:1, :], ones_col, hsq[phb],
                            start=st, stop=False),
                            reads=[f"hsq{phb}", "ones_col"], writes=["PB6"])
                    p.emit(SC, lambda E, jf=jf, bank=bank: E.activation(
                        hT[:, jf * TILE:(jf + 1) * TILE], PS[bank][:], AF.Gelu),
                        reads=[f"PB{bank}"], writes=[f"hTp{jf % 2}"])
                    hb = jf % 2
                    p.emit(VE, lambda E, jf=jf, hb=hb: E.tensor_mul(
                        hsq[hb], hT[:, jf * TILE:(jf + 1) * TILE],
                        hT[:, jf * TILE:(jf + 1) * TILE]),
                        reads=[f"hTp{jf % 2}"], writes=[f"hsq{hb}"])
                    bk_done(bank)
            p.emit(TE, lambda E: E.matmul(
                PS[6][0:1, :], ones_col, hsq[(FC - 1) % 2],
                start=False, stop=True),
                reads=[f"hsq{(FC - 1) % 2}", "ones_col"], writes=["PB6"])
            if ti + 1 < nown_tiles:
                b_load_stats(ti + 1)
            p.emit(SC, lambda E: E.activation(
                s2row[0:1, 0:TILE], PS[6][0:1, :], AF.Ln, scale=float(1.0 / DFF)),
                reads=["PB6"], writes=["s2row"])
            bk_done(6)
            p.emit(SC, lambda E: E.activation(
                s2row[0:1, 0:TILE], s2row[0:1, 0:TILE], AF.Exp, scale=0.5),
                reads=["s2row"], writes=["s2row"])
            p.emit(VE, lambda E: E.tensor_scalar(
                out=s2row[0:1, 0:TILE], in0=s2row[0:1, 0:TILE], scalar1=float(RMSEPS),
                scalar2=None, op0=ALU.add),
                reads=["s2row"], writes=["s2row"])
            p.emit(VE, lambda E: E.reciprocal(s2row[0:1, 0:TILE], s2row[0:1, 0:TILE]),
                   reads=["s2row"], writes=["s2row"])
            fns = []
            for c in range(NCH):
                def mm(E, c=c, st=(lead(6) if (c == 0) else False)):
                    return E.matmul(
                        PS[6][:, c:c + 1], s2row[0:1, c * 128:(c + 1) * 128],
                        ones_1f[0:1, 0:1], start=st, stop=(c == NCH - 1))
                fns.append(mm)
            p.emit(TE, fns, reads=["s2row", "ones_1f"], writes=["PB6"])
            p.emit(VE, lambda E: E.tensor_copy(s2col[:, 0:NCH], PS[6][:, 0:NCH]),
                   reads=["PB6"], writes=["s2col"])
            bk_done(6)

            for half in range(2):
                for fs in range(DFF // 512):
                    fb = fs % 2
                    tern_quant_chunks([("dn", (fs * 4 + r) * 2 + half) for r in range(4)])
                    src = wdn_t.rearrange("(jf q) d -> q jf d", q=128)[
                        :, fs * 4:(fs + 1) * 4, half * 512:(half + 1) * 512]
                    p.emit(GP, lambda E, src=src, fb=fb: E.dma_start(
                        out=fwd[fb].rearrange("q (jf d) -> q jf d", jf=4), in_=src),
                        reads=["scr_dn"], writes=[f"fwd{fb}"], sem="LDG", inc=16)
                    fns = []
                    for lf in range(4):
                        jf = fs * 4 + lf
                        for c in range(NCH):
                            def mm(E, jf=jf, lf=lf, c=c, fb=fb, st=(lead(c) if (jf == 0) else False)):
                                return E.matmul(
                                    PS[c][:],
                                    hT[:, jf * TILE + c * 128: jf * TILE + (c + 1) * 128],
                                    fwd[fb][:, lf * 512:(lf + 1) * 512],
                                    start=st, stop=(jf == FC - 1))
                            fns.append(mm)
                    p.emit(TE, fns, reads=[f"fwd{fb}", "hTp0", "hTp1"],
                           writes=[f"PB{c}" for c in range(NCH)])
                for c in range(NCH):
                    dv = out_sl[:, c * D + half * 512: c * D + (half + 1) * 512]
                    pv = x_sl[xb][:, c * D + half * 512: c * D + (half + 1) * 512]
                    p.emit(VE, lambda E, dv=dv, pv=pv, c=c: E.scalar_tensor_tensor(
                        out=dv, in0=PS[c][:], scalar=s2col[:, c:c + 1], in1=pv,
                        op0=ALU.mult, op1=ALU.add),
                        reads=[f"PB{c}", "s2col", xname], writes=["out_sl"])
                    bk_done(c)
            p.emit(SY, lambda E, ti=ti: E.dma_start(
                out=out[ti * TILE:(ti + 1) * TILE, :].rearrange("(c q) d -> q c d", q=128),
                in_=out_sl.rearrange("q (c d) -> q c d", c=NCH)),
                reads=["out_sl"], writes=[f"fdram{ti}"], sem="ST", inc=16)

        p.emit(VE, lambda E: E.memset(st_f[:, 0:H * EV], 0.0),
               reads=["st_f"], writes=["st_f"])
        for r in range(NCORE):
            p.emit(SY, lambda E, r=r: E.dma_start(
                out=shard_s[:, 0:H * EV], in_=st_all[r * 64:(r + 1) * 64, :]),
                reads=["st_all_d"], writes=["shard"], inc=16)
            p.emit(VE, lambda E, r=r: E.scalar_tensor_tensor(
                out=st_f[:, 0:H * EV], in0=shard_s[:, 0:H * EV],
                scalar=sm_s[0:64, r:r + 1], in1=st_f[:, 0:H * EV],
                op0=ALU.mult, op1=ALU.add),
                reads=["shard", "sm", "st_f"], writes=["st_f"])
        p.emit(VE, lambda E: E.memset(st_b, 0.0), reads=["st_b"], writes=["st_b"])
        p.emit(VE, lambda E: E.tensor_copy(
            st_b[0:64, :].rearrange("q (h e) -> q h e", h=H)[:, 0:H:2, :],
            st_f[:, 0:H * EV].rearrange("q (h e) -> q h e", h=H)[:, 0:H:2, :]),
            reads=["st_f", "st_b"], writes=["st_b"])
        p.emit(VE, lambda E: E.tensor_copy(
            st_b[64:128, :].rearrange("q (h e) -> q h e", h=H)[:, 1:H:2, :],
            st_f[:, 0:H * EV].rearrange("q (h e) -> q h e", h=H)[:, 1:H:2, :]),
            reads=["st_f", "st_b"], writes=["st_b"])

        p.barrier()
        cur[0] = wts_top
        qT2b = [take(4096) for _ in range(2)]
        num2b = [take(4352) for _ in range(2)]
        part2b = [take(8192, F32) for _ in range(2)]
        o_tok = take(1024)
        oT = take(4096)
        a2scr = take(544, F32)

        def a2_loads(ti):
            tb = ti % 2
            p.emit(SY, lambda E, ti=ti, tb=tb: E.dma_start(
                out=qT2b[tb], in_=qT_d[ti * 128:(ti + 1) * 128, :]),
                reads=["qT_dd"], writes=[f"qT2{tb}"], inc=16)
            p.emit(SY, lambda E, ti=ti, tb=tb: E.dma_start(
                out=num2b[tb].rearrange("q (c e) -> q c e", c=NCH),
                in_=num_d[ti * TILE:(ti + 1) * TILE, :].rearrange("(c q) e -> q c e", q=128)),
                reads=["num_dd"], writes=[f"num2{tb}"], inc=16)
            p.emit(SY, lambda E, ti=ti, tb=tb: E.dma_start(
                out=part2b[tb].rearrange("q (c d) -> q c d", c=NCH),
                in_=out[ti * TILE:(ti + 1) * TILE, :].rearrange("(c q) d -> q c d", q=128)),
                reads=[f"fdram{ti}"], writes=[f"part2{tb}"], inc=16)

        a2_loads(0)
        for ti in range(nown_tiles):
            tb = ti % 2
            qT2, num2, part2 = qT2b[tb], num2b[tb], part2b[tb]
            if ti + 1 < nown_tiles:
                a2_loads(ti + 1)
            for ci in range(NCH):
                for g in range(4):
                    nb = g
                    fns = []
                    for i in range(4):
                        h = g * 4 + i

                        def m3c(E, h=h, i=i, nb=nb, ci=ci,
                                st=(lead(nb) if (i == 0) else False)):
                            return E.matmul(
                                PS[nb][:, i * EV:i * EV + EV],
                                qT2[:, (h // 2) * TILE + ci * 128:(h // 2) * TILE + (ci + 1) * 128],
                                st_b[:, h * EV:(h + 1) * EV],
                                start=st, stop=(i == 3))
                        fns.append(m3c)
                    p.emit(TE, fns, reads=[f"qT2{tb}", "st_b"], writes=[f"PB{nb}"])
                    nsv = num2[:, ci * H * EV + g * 4 * EV: ci * H * EV + (g + 1) * 4 * EV]
                    p.emit(VE, lambda E, nb=nb, nsv=nsv: E.tensor_tensor(
                        a2scr[:, 0:4 * EV], PS[nb][:, 0:4 * EV], nsv, ALU.add),
                        reads=[f"PB{nb}", f"num2{tb}"], writes=["a2scr"])
                    bk_done(nb)
                    p.emit(VE, lambda E: E.tensor_scalar(
                        out=rcp_s[:, 0:4],
                        in0=a2scr[:, 0:4 * EV].rearrange("q (i e) -> q i e", i=4)[:, :, HD],
                        scalar1=float(ATTEPS), scalar2=None, op0=ALU.add),
                        reads=["a2scr"], writes=["rcp"])
                    p.emit(VE, lambda E: E.reciprocal(rcp_s[:, 0:4], rcp_s[:, 0:4]),
                           reads=["rcp"], writes=["rcp"])
                    ov = o_tok[:, g * 256:(g + 1) * 256].rearrange("q (i e) -> q i e", i=4)
                    p.emit(VE, lambda E, ov=ov: E.tensor_tensor(
                        ov, a2scr[:, 0:4 * EV].rearrange("q (i e) -> q i e", i=4)[:, :, 0:HD],
                        rcp_s[:, 0:4].rearrange("q i -> q i ()").broadcast_to([128, 4, HD]),
                        ALU.mult),
                        reads=["a2scr", "rcp"], writes=["o_tok"])
                fns = []
                for jd in range(DC):
                    def tr(E, jd=jd):
                        return E.transpose(
                            P7[:, jd * 128:(jd + 1) * 128],
                            o_tok[:, jd * 128:(jd + 1) * 128], ident)
                    fns.append(tr)
                p.emit(TE, fns, reads=["o_tok", "ident"], writes=["P7"])
                ov = oT.rearrange("q (jd t) -> q jd t", jd=DC)[:, :, ci * 128:(ci + 1) * 128]
                p.emit(SC, lambda E, ov=ov: E.copy(
                    ov, P7[:].rearrange("q (jd t) -> q jd t", jd=DC)),
                    reads=["P7"], writes=["oT"])
                bk_done(7)
            for c in range(NCH):
                for half in range(2):
                    bank = half
                    fns = []
                    for jd in range(DC):
                        def mm(E, c=c, half=half, jd=jd, bank=bank,
                               st=(lead(bank) if (jd == 0) else False)):
                            return E.matmul(
                                PS[bank][:],
                                oT[:, jd * TILE + c * 128: jd * TILE + (c + 1) * 128],
                                wo_s[:, jd * D + half * 512: jd * D + (half + 1) * 512],
                                start=st, stop=False)
                        fns.append(mm)

                    def mmb(E, half=half, bank=bank):
                        return E.matmul(
                            PS[bank][:], ones_row, bor_s[0:1, half * 512:(half + 1) * 512],
                            start=False, stop=True)
                    fns.append(mmb)
                    p.emit(TE, fns, reads=["oT", "wo", "bo", "ones_row"], writes=[f"PB{bank}"])
                    dv = out_sl[:, c * D + half * 512: c * D + (half + 1) * 512]
                    pv = part2[:, c * D + half * 512: c * D + (half + 1) * 512]
                    p.emit(VE, lambda E, dv=dv, pv=pv, bank=bank: E.tensor_tensor(
                        dv, PS[bank][:], pv, ALU.add),
                        reads=[f"PB{bank}", f"part2{tb}"], writes=["out_sl"])
                    bk_done(bank)
            p.emit(SY, lambda E, ti=ti: E.dma_start(
                out=out[ti * TILE:(ti + 1) * TILE, :].rearrange("(c q) d -> q c d", q=128),
                in_=out_sl.rearrange("q (c d) -> q c d", c=NCH)),
                reads=["out_sl"], writes=[f"adram{ti}"], sem="ST", inc=16)

        total_st = p.cnt["ST0"]
        h_st = p.semh["ST0"]
        p.ops["sync"].append(lambda E, h=h_st, v=total_st: E.wait_ge(h, v))

        block = es.enter_context(nc.Block())

        def replay(name):
            def f(E):
                for op in p.ops[name]:
                    op(E)
            return f
        block.sync(replay("sync"))
        block.tensor(replay("tensor"))
        block.vector(replay("vector"))
        block.scalar(replay("scalar"))
        block.gpsimd(replay("gpsimd"))

    return nc


def _prep(Wq, bq, Wk, bk, Wv, bv, Wo, bo, w_up, w_down):
    f = np.float32
    return {
        "wqT": np.ascontiguousarray(np.asarray(Wq, f).T),
        "wkT": np.ascontiguousarray(np.asarray(Wk, f).T),
        "wvT": np.ascontiguousarray(np.asarray(Wv, f).T),
        "woT": np.ascontiguousarray(np.asarray(Wo, f).T),
        "bq8": np.ascontiguousarray(np.asarray(bq, f).reshape(DC, 128).T),
        "bk8": np.ascontiguousarray(np.asarray(bk, f).reshape(DC, 128).T),
        "bv_row": np.asarray(bv, f).reshape(1, D).copy(),
        "bo_row": np.asarray(bo, f).reshape(1, D).copy(),
        "wupT": np.ascontiguousarray(np.asarray(w_up, f).T),
        "wdnT": np.ascontiguousarray(np.asarray(w_down, f).T),
    }


def kernel(L0_attn_x, L0_attn_Wq, L0_attn_bq, L0_attn_Wk, L0_attn_bk,
           L0_attn_Wv, L0_attn_bv, L0_attn_Wo, L0_attn_bo,
           L0_ffn_w_up, L0_ffn_w_down):
    x = np.ascontiguousarray(np.asarray(L0_attn_x, dtype=np.float32))
    weights = _prep(L0_attn_Wq, L0_attn_bq, L0_attn_Wk, L0_attn_bk, L0_attn_Wv,
                    L0_attn_bv, L0_attn_Wo, L0_attn_bo, L0_ffn_w_up, L0_ffn_w_down)

    OWN = S // 4
    nown = OWN // TILE

    nc = build_nc(0, nown)
    in_maps = []
    for core in range(NCORE):
        b, pq = core // 4, core % 4
        xb = x[b]
        start = pq * OWN
        sm = np.zeros((128, NCORE), np.float32)
        for r in range(NCORE):
            if r // 4 == b and r % 4 < pq:
                sm[:, r] = 1.0
        m = {"x_own": np.ascontiguousarray(xb[start:start + OWN]), "smask": sm}
        m.update(weights)
        in_maps.append(m)
    res = run_bass_kernel_spmd(nc, in_maps, core_ids=list(range(NCORE)))
    outp = np.empty((B, S, D), np.float32)
    for core in range(NCORE):
        b, pq = core // 4, core % 4
        outp[b, pq * OWN:(pq + 1) * OWN] = np.asarray(res.results[core]["out"])
    return outp


# version 17
# speedup vs baseline: 1.2051x; 1.0050x over previous
import numpy as np
from contextlib import ExitStack

import concourse.bass as bass
import concourse.mybir as mybir
from concourse.bass_utils import run_bass_kernel_spmd

F32 = mybir.dt.float32
BF16 = mybir.dt.bfloat16
AF = mybir.ActivationFunctionType
ALU = mybir.AluOpType
AX = mybir.AxisListType

B, S, D, H, HD, DFF = 2, 8192, 1024, 16, 64, 4096
NCORE = 8
TILE = 512
CH = 128
EV = 68
LNEPS = 1e-5
ATTEPS = 1e-6
RMSEPS = 1e-8
NCH = TILE // CH
DC = D // 128
FC = DFF // 128

ENGS = ("sync", "tensor", "vector", "scalar", "gpsimd")
ESEM = {"tensor": "PE", "vector": "DV", "scalar": "AC", "gpsimd": "GP", "sync": "LD"}
SY, TE, VE, SC, GP = ENGS


class Prog:

    def __init__(self, nc):
        self.nc = nc
        self.ops = {e: [] for e in ENGS}
        self.semh = {}
        self.cnt = {}
        self.seen = {e: {} for e in ENGS}
        self.buf = {}
        self.dmarr = {}

    def mksem(self, es, *names):
        for n in names:
            self.semh[n] = es.enter_context(self.nc.semaphore(n))
            self.cnt[n] = 0

    def _wait(self, eng, sname, v):
        if self.seen[eng].get(sname, 0) >= v:
            return
        self.seen[eng][sname] = v
        h = self.semh[sname]
        self.ops[eng].append(lambda E, h=h, v=v: E.wait_ge(h, v))

    DMA_POOLS = {"LD": ["LD0", "LD1", "LD2", "LD3"],
                 "LDG": ["LG0", "LG1", "LG2", "LG3"],
                 "ST": ["ST0"]}

    def emit(self, eng, fns, reads=(), writes=(), sem=None, inc=1):
        if not isinstance(fns, (list, tuple)):
            fns = [fns]
        if sem is None:
            sem = ESEM[eng]
        if sem in self.DMA_POOLS:
            pool = self.DMA_POOLS[sem]
            idx = self.dmarr.get(sem, 0)
            self.dmarr[sem] = (idx + 1) % len(pool)
            sem = pool[idx]
            if self.cnt[sem] > 0:
                self._wait(eng, sem, self.cnt[sem])
        need = {}
        for b in reads:
            st = self.buf.get(b)
            if st and st["w"]:
                s, v = st["w"]
                need[s] = max(need.get(s, 0), v)
        for b in writes:
            st = self.buf.get(b)
            if st:
                toks = ([st["w"]] if st["w"] else []) + st["r"]
                for s, v in toks:
                    need[s] = max(need.get(s, 0), v)
        for s in sorted(need):
            self._wait(eng, s, need[s])
        for f in fns[:-1]:
            self.ops[eng].append(f)
        self.cnt[sem] += inc
        v = self.cnt[sem]
        h = self.semh[sem]
        last = fns[-1]
        self.ops[eng].append(lambda E, last=last, h=h, inc=inc: last(E).then_inc(h, inc))
        tok = (sem, v)
        for b in writes:
            self.buf[b] = {"w": tok, "r": []}
        for b in reads:
            st = self.buf.setdefault(b, {"w": None, "r": []})
            st["r"].append(tok)
        return tok

    def barrier(self, exclude=("CC",)):
        cur = {s: v for s, v in self.cnt.items() if v > 0 and s not in exclude}
        for e in ENGS:
            for s in sorted(cur):
                self._wait(e, s, cur[s])


def build_nc(npre_slabs, nown_tiles):
    OWN = nown_tiles * TILE
    PRE = max(npre_slabs * TILE, TILE)
    nc = bass.Bass("TRN2", target_bir_lowering=False)

    x_own = nc.dram_tensor("x_own", [OWN, D], F32, kind="ExternalInput").ap()
    smask = nc.dram_tensor("smask", [128, NCORE], F32, kind="ExternalInput").ap()
    wqT = nc.dram_tensor("wqT", [D, D], F32, kind="ExternalInput").ap()
    wkT = nc.dram_tensor("wkT", [D, D], F32, kind="ExternalInput").ap()
    wvT = nc.dram_tensor("wvT", [D, D], F32, kind="ExternalInput").ap()
    woT = nc.dram_tensor("woT", [D, D], F32, kind="ExternalInput").ap()
    bq8 = nc.dram_tensor("bq8", [128, DC], F32, kind="ExternalInput").ap()
    bk8 = nc.dram_tensor("bk8", [128, DC], F32, kind="ExternalInput").ap()
    bv_row = nc.dram_tensor("bv_row", [1, D], F32, kind="ExternalInput").ap()
    bo_row = nc.dram_tensor("bo_row", [1, D], F32, kind="ExternalInput").ap()
    wupT = nc.dram_tensor("wupT", [D, DFF], F32, kind="ExternalInput").ap()
    wdnT = nc.dram_tensor("wdnT", [DFF, D], F32, kind="ExternalInput").ap()
    out = nc.dram_tensor("out", [OWN, D], F32, kind="ExternalOutput").ap()
    wup_t = nc.dram_tensor("wup_t", [D, DFF], BF16, kind="Internal").ap()
    wdn_t = nc.dram_tensor("wdn_t", [DFF, D], BF16, kind="Internal").ap()
    qT_d = nc.dram_tensor("qT_d", [nown_tiles * 128, DC * TILE], BF16, kind="Internal").ap()
    num_d = nc.dram_tensor("num_d", [OWN, H * EV], BF16, kind="Internal").ap()
    st_loc = nc.dram_tensor("st_loc", [64, H * EV], F32, kind="Internal").ap()
    st_all = nc.dram_tensor("st_all", [NCORE * 64, H * EV], F32, kind="Internal").ap()

    with ExitStack() as es:
        pst = lambda name, shape, dt: es.enter_context(nc.psum_tensor(name, shape, dt))

        ARENA_BC = 106000
        arena = es.enter_context(nc.sbuf_tensor("arena", [128, ARENA_BC], BF16))
        cur = [0]

        def take(bcols, dt=BF16, parts=128):
            o = cur[0]
            bcols_al = (bcols + 15) // 16 * 16
            cur[0] += bcols_al
            assert cur[0] <= ARENA_BC, f"arena overflow at {cur[0]}"
            v = arena[0:parts, o:o + bcols]
            if dt == F32:
                v = v.bitcast(F32)
            return v

        x_sl = [take(8192, F32) for _ in range(2)]
        out_sl = take(8192, F32)
        xn_tok = take(4096)
        xnT = take(4096)
        st_f = take(2176, F32, parts=64)
        st_b = take(1088)
        bn_s = take(96, F32)
        mv_s = take(16, F32)
        rstd_s = take(8, F32)
        cA_s = take(8, F32)
        sffn_s = take(8, F32)
        cB_s = take(8, F32)
        lnt_s = take(32, F32)
        rcp_s = take(8, F32)
        s2col = take(8, F32)
        tpart = take(4, F32)
        tred = take(2, F32)
        thr_s = take(4, F32)
        nthr_s = take(4, F32)
        lneps_c = take(2, F32)
        mask8 = take(512)
        ident = take(128)
        ones_col = take(1)
        ones_cf = take(2, F32)
        bq_s = take(16, F32)
        bk_s = take(16, F32)
        sm_s = take(16, F32)
        shard_s = take(2176, F32, parts=64)
        ones_row = take(128, parts=1)
        ones_rf = take(256, F32, parts=1)
        ones_1f = take(2, F32, parts=1)
        bvr_s = take(1024, parts=1)
        bor_s = take(1024, parts=1)
        s2row = take(1024, F32, parts=1)
        trow_s = take(4, F32, parts=1)

        twf = [take(1024, F32) for _ in range(2)]
        tg1 = take(1024, F32)
        tout = [take(512) for _ in range(2)]
        dyn_base = cur[0]

        wo_s = take(8192)
        wts_top = cur[0]
        wq_s = take(8192)
        wk_s = take(8192)
        wv_s = take(8192)
        vtok = take(4352)
        ph_e = [take(1024, F32) for _ in range(2)]
        ph_r = [take(1024, F32) for _ in range(2)]
        var_base = cur[0]

        ktok = take(4096)
        qT = kT = ktok_c = o_tok = oT = am_s = None

        PS = [pst(f"P{i}", [128, 512], F32) for i in range(7)]
        P7 = pst("P7", [128, 1024], BF16)

        p = Prog(nc)
        p.mksem(es, "LD0", "LD1", "LD2", "LD3", "LG0", "LG1", "LG2", "LG3", "ST0", "CC", "PE", "DV", "AC", "GP")

        bank_fresh = {k: True for k in range(8)}

        def lead(i):
            f = bank_fresh[i]
            bank_fresh[i] = False
            return f

        def bk_done(i):
            bank_fresh[i] = True

        p.emit(GP, lambda E: E.memset(ident, 0.0), writes=["ident"])
        p.emit(GP, lambda E: E.affine_select(
            out=ident, in_=ident, compare_op=ALU.not_equal, fill=1.0,
            base=0, pattern=[[-1, 128]], channel_multiplier=1),
            reads=["ident"], writes=["ident"])
        p.emit(GP, lambda E: E.memset(mask8, 1.0), writes=["mask8"])
        for i in range(4):
            p.emit(GP, lambda E, i=i: E.affine_select(
                out=mask8[:, i * CH:(i + 1) * CH], in_=mask8[:, i * CH:(i + 1) * CH],
                compare_op=ALU.is_ge, fill=0.0, base=0,
                pattern=[[1, CH]], channel_multiplier=-1),
                reads=["mask8"], writes=["mask8"])
        p.emit(GP, lambda E: E.memset(lneps_c[:, 0:1], float(LNEPS)), writes=["lneps"])
        for t, n in [(ones_row, "ones_row"), (ones_col, "ones_col"),
                     (ones_rf, "ones_rf"), (ones_cf, "ones_cf"), (ones_1f, "ones_1f")]:
            p.emit(GP, lambda E, t=t: E.memset(t, 1.0), writes=[n])
        p.emit(VE, lambda E: E.memset(st_f, 0.0), writes=["st_f"])
        p.emit(VE, lambda E: E.memset(st_b, 0.0), writes=["st_b"])
        p.emit(VE, lambda E: E.memset(tpart, 0.0), writes=["tpart"])
        p.emit(VE, lambda E: E.memset(vtok, 0.0), writes=["vtok"])
        p.emit(VE, lambda E: E.memset(
            vtok.rearrange("q (c h e) -> q c h e", c=NCH, h=H)[:, :, :, HD:HD + 1], 1.0),
            reads=["vtok"], writes=["vtok"])

        for w_ap, tile, name in [(wqT, wq_s, "wq"), (wkT, wk_s, "wk"),
                                 (wvT, wv_s, "wv"), (woT, wo_s, "wo")]:
            p.emit(GP, lambda E, w=w_ap, t=tile: E.dma_start(
                out=t.rearrange("q (c d) -> q c d", c=DC),
                in_=w.rearrange("(c q) d -> q c d", q=128)),
                writes=[name], sem="LDG", inc=16)
        p.emit(SY, lambda E: E.dma_start(out=bq_s[:, 0:DC], in_=bq8), writes=["bq"], inc=16)
        p.emit(SY, lambda E: E.dma_start(out=bk_s[:, 0:DC], in_=bk8), writes=["bk"], inc=16)
        p.emit(GP, lambda E: E.dma_start(out=bvr_s, in_=bv_row), writes=["bv"], sem="LDG", inc=16)
        p.emit(GP, lambda E: E.dma_start(out=bor_s, in_=bo_row), writes=["bo"], sem="LDG", inc=16)
        p.emit(SY, lambda E: E.dma_start(out=sm_s[:, 0:NCORE], in_=smask),
               writes=["sm"], inc=16)

        def ln_stats(xb, xname, sffn=False):
            xs = x_sl[xb]
            for c in range(NCH):
                p.emit(VE, lambda E, c=c, xs=xs: E.bn_stats(
                    bn_s[:, c * 12:c * 12 + 6], xs[:, c * D:c * D + 512]),
                    reads=[xname], writes=[f"bn{c}"])
                p.emit(VE, lambda E, c=c, xs=xs: E.bn_stats(
                    bn_s[:, c * 12 + 6:c * 12 + 12], xs[:, c * D + 512:(c + 1) * D]),
                    reads=[xname, f"bn{c}"], writes=[f"bn{c}"])
                p.emit(VE, lambda E, c=c: E.bn_aggr(
                    mv_s[:, 2 * c:2 * c + 2], bn_s[:, c * 12:(c + 1) * 12]),
                    reads=[f"bn{c}"], writes=[f"mv{c}"])
            mvr = [f"mv{c}" for c in range(NCH)]
            p.emit(SC, lambda E: E.activation(
                lnt_s[:, 0:NCH], mv_s[:, 1:2 * NCH:2], AF.Ln, bias=lneps_c[:, 0:1], scale=1.0),
                reads=mvr + ["lneps"], writes=["lnt0"])
            p.emit(SC, lambda E: E.activation(
                rstd_s[:, 0:NCH], lnt_s[:, 0:NCH], AF.Exp, scale=-0.5),
                reads=["lnt0"], writes=["rstd"])
            p.emit(VE, lambda E: E.scalar_tensor_tensor(
                out=cA_s[:, 0:NCH], in0=mv_s[:, 0:2 * NCH:2], scalar=-1.0,
                in1=rstd_s[:, 0:NCH], op0=ALU.mult, op1=ALU.mult),
                reads=mvr + ["rstd"], writes=["cA"])
            if sffn:
                p.emit(SC, lambda E: E.activation(
                    lnt_s[:, NCH:2 * NCH], mv_s[:, 1:2 * NCH:2], AF.Ln, scale=1.0),
                    reads=mvr, writes=["lnt1"])
                p.emit(VE, lambda E: E.tensor_sub(
                    lnt_s[:, 2 * NCH:3 * NCH], lnt_s[:, NCH:2 * NCH], lnt_s[:, 0:NCH]),
                    reads=["lnt0", "lnt1"], writes=["lnt2"])
                p.emit(SC, lambda E: E.activation(
                    lnt_s[:, 3 * NCH:4 * NCH], lnt_s[:, 2 * NCH:3 * NCH], AF.Exp,
                    scale=0.5),
                    reads=["lnt2"], writes=["lnt3"])
                p.emit(VE, lambda E: E.tensor_scalar(
                    out=lnt_s[:, 3 * NCH:4 * NCH], in0=lnt_s[:, 3 * NCH:4 * NCH],
                    scalar1=float(RMSEPS), scalar2=None, op0=ALU.add),
                    reads=["lnt3"], writes=["lnt3"])
                p.emit(VE, lambda E: E.reciprocal(
                    lnt_s[:, 2 * NCH:3 * NCH], lnt_s[:, 3 * NCH:4 * NCH]),
                    reads=["lnt3"], writes=["lnt2"])
                p.emit(VE, lambda E: E.tensor_mul(
                    sffn_s[:, 0:NCH], rstd_s[:, 0:NCH], lnt_s[:, 2 * NCH:3 * NCH]),
                    reads=["rstd", "lnt2"], writes=["sffn"])
                p.emit(VE, lambda E: E.scalar_tensor_tensor(
                    out=cB_s[:, 0:NCH], in0=mv_s[:, 0:2 * NCH:2], scalar=-1.0,
                    in1=sffn_s[:, 0:NCH], op0=ALU.mult, op1=ALU.mult),
                    reads=mvr + ["sffn"], writes=["cB"])

        def xn_apply(xb, xname, scale_t, scale_name, shift_t, shift_name):
            xs = x_sl[xb]
            for c in range(NCH):
                p.emit(SC, lambda E, c=c, xs=xs: E.activation(
                    xn_tok[:, c * D:(c + 1) * D], xs[:, c * D:(c + 1) * D], AF.Identity,
                    bias=shift_t[:, c:c + 1], scale=scale_t[:, c:c + 1]),
                    reads=[xname, scale_name, shift_name], writes=["xn_tok"])

        def transpose_tok_to_ft(src, src_name, dst, dst_name):
            for c in range(NCH):
                fns = []
                for jm in range(DC):
                    def tr(E, c=c, jm=jm):
                        return E.transpose(
                            P7[:, jm * 128:(jm + 1) * 128],
                            src[:, c * D + jm * 128:c * D + (jm + 1) * 128], ident)
                    fns.append(tr)
                p.emit(TE, fns, reads=[src_name, "ident"], writes=["P7"])
                ov = dst.rearrange("q (jm t) -> q jm t", jm=DC)[:, :, c * 128:(c + 1) * 128]
                p.emit(SC, lambda E, ov=ov: E.copy(
                    ov, P7[:].rearrange("q (jm t) -> q jm t", jm=DC)),
                    reads=["P7"], writes=[dst_name])
                bk_done(7)

        def transpose_ft_to_tok(src, src_name, dst, dst_name):
            for c in range(NCH):
                fns = []
                for jd in range(DC):
                    def tr(E, c=c, jd=jd):
                        return E.transpose(
                            P7[:, jd * 128:(jd + 1) * 128],
                            src[:, jd * TILE + c * 128:jd * TILE + (c + 1) * 128], ident)
                    fns.append(tr)
                p.emit(TE, fns, reads=[src_name, "ident"], writes=["P7"])
                p.emit(VE, lambda E, c=c: E.tensor_copy(
                    dst[:, c * D:(c + 1) * D], P7[:]),
                    reads=["P7"], writes=[dst_name])
                bk_done(7)

        def proj_feature(wtile, wname, btile, bname, dst, dst_name):
            for jd in range(DC):
                bank = jd % 4
                pp = jd % 2
                fns = []
                for jm in range(DC):
                    def mm(E, jd=jd, jm=jm, bank=bank, st=(lead(bank) if (jm == 0) else False)):
                        return E.matmul(
                            PS[bank][:],
                            wtile[:, jm * D + jd * 128: jm * D + (jd + 1) * 128],
                            xnT[:, jm * TILE:(jm + 1) * TILE],
                            start=st, stop=(jm == DC - 1))
                    fns.append(mm)
                p.emit(TE, fns, reads=[wname, "xnT"], writes=[f"PB{bank}"])
                dv = dst[:, jd * TILE:(jd + 1) * TILE]
                p.emit(SC, lambda E, jd=jd, bank=bank, pp=pp: E.activation(
                    ph_e[pp][:, 0:512], PS[bank][:], AF.Exp, bias=btile[:, jd:jd + 1],
                    scale=1.0),
                    reads=[f"PB{bank}", bname], writes=[f"ph_e{pp}"])
                p.emit(SC, lambda E, jd=jd, bank=bank, pp=pp: E.activation(
                    ph_r[pp][:, 0:512], PS[bank][:], AF.Relu, bias=btile[:, jd:jd + 1],
                    scale=1.0),
                    reads=[f"PB{bank}", bname], writes=[f"ph_r{pp}"])
                p.emit(VE, lambda E, dv=dv, pp=pp: E.scalar_tensor_tensor(
                    out=dv, in0=ph_e[pp][:, 0:512], scalar=1.0, in1=ph_r[pp][:, 0:512],
                    op0=ALU.min, op1=ALU.add),
                    reads=[f"ph_e{pp}", f"ph_r{pp}"], writes=[dst_name])
                bk_done(bank)
                tern_tick()

        def proj_token(wtile, wname, brow, bname, writer, banks):
            for c in range(NCH):
                for half in range(2):
                    bank = banks[(c * 2 + half) % len(banks)]
                    fns = []
                    for jm in range(DC):
                        def mm(E, c=c, half=half, jm=jm, bank=bank, st=(lead(bank) if (jm == 0) else False)):
                            return E.matmul(
                                PS[bank][:],
                                xnT[:, jm * TILE + c * 128: jm * TILE + (c + 1) * 128],
                                wtile[:, jm * D + half * 512: jm * D + (half + 1) * 512],
                                start=st, stop=False)
                        fns.append(mm)

                    def mmb(E, half=half, bank=bank):
                        return E.matmul(
                            PS[bank][:], ones_row, brow[0:1, half * 512:(half + 1) * 512],
                            start=False, stop=True)
                    fns.append(mmb)
                    p.emit(TE, fns, reads=[wname, "xnT", bname, "ones_row"],
                           writes=[f"PB{bank}"])
                    writer(c, half, bank)
                    bk_done(bank)
                    tern_tick()

        SDSEG = [(4, 0, 6), (5, 6, 11), (6, 11, 16)]

        def state_fold():
            for bank, h0, h1 in SDSEG:
                p.emit(VE, lambda E, bank=bank, h0=h0, h1=h1: E.tensor_tensor(
                    st_f[:, h0 * EV:h1 * EV], st_f[:, h0 * EV:h1 * EV],
                    PS[bank][0:64, 0:(h1 - h0) * EV], ALU.add),
                    reads=[f"PB{bank}", "st_f"], writes=["st_f"])
                bk_done(bank)
            p.emit(VE, lambda E: E.tensor_copy(
                st_b[0:64, :].rearrange("q (h e) -> q h e", h=H)[:, 0:H:2, :],
                st_f[:, 0:H * EV].rearrange("q (h e) -> q h e", h=H)[:, 0:H:2, :]),
                reads=["st_f"], writes=["st_b"])
            p.emit(VE, lambda E: E.tensor_copy(
                st_b[64:128, :].rearrange("q (h e) -> q h e", h=H)[:, 1:H:2, :],
                st_f[:, 0:H * EV].rearrange("q (h e) -> q h e", h=H)[:, 1:H:2, :]),
                reads=["st_f"], writes=["st_b"])

        scanref = {}

        def scan_pre(ci):
            qT = scanref["qT"]; kT = scanref["kT"]
            ktok_c = scanref["ktok_c"][ci % 2]
            am_s = scanref["am_s"]
            fns = []
            for jd in range(DC):
                def tr(E, jd=jd, ci=ci):
                    return E.transpose(
                        P7[:, jd * 128:(jd + 1) * 128],
                        kT[:, jd * TILE + ci * 128:jd * TILE + (ci + 1) * 128], ident)
                fns.append(tr)
            p.emit(TE, fns, reads=["kT", "ident"], writes=["P7"])
            p.emit(SC, lambda E, ktok_c=ktok_c: E.copy(ktok_c[:, 0:D], P7[:]),
                   reads=["P7"], writes=[f"ktokc{ci % 2}"])
            bk_done(7)
            fns = []
            wr = set()
            for h in range(H):
                bank = h % 4
                col = (h // 4) * 128

                def m1(E, h=h, bank=bank, col=col, ci=ci,
                       st=(lead(bank) if (h // 4 == 0) else False)):
                    return E.matmul(
                        PS[bank][:, col:col + 128],
                        kT[(h % 2) * 64:(h % 2) * 64 + 64,
                           (h // 2) * TILE + ci * 128:(h // 2) * TILE + (ci + 1) * 128],
                        qT[(h % 2) * 64:(h % 2) * 64 + 64,
                           (h // 2) * TILE + ci * 128:(h // 2) * TILE + (ci + 1) * 128],
                        start=st, stop=(h >= H - 4))
                fns.append(m1)
                wr.add(f"PB{bank}")
            p.emit(TE, fns, reads=["kT", "qT"], writes=sorted(wr))
            for bank in range(4):
                ov = am_s.rearrange("q (g b c) -> q g b c", g=4, b=4)[:, :, bank, :]
                p.emit(VE, lambda E, bank=bank, ov=ov: E.tensor_tensor(
                    ov, PS[bank][:].rearrange("q (g c) -> q g c", g=4),
                    mask8.rearrange("q (g c) -> q g c", g=4), ALU.mult),
                    reads=[f"PB{bank}", "mask8"], writes=[f"am{bank}"])
                bk_done(bank)

        def scan_groups(ci):
            qT = scanref["qT"]; am_s = scanref["am_s"]; numsave = scanref["numsave"]
            amr = [f"am{b}" for b in range(4)]
            for g in range(4):
                nb = 4 + (g % 2)
                fns = []
                for i in range(4):
                    h = g * 4 + i

                    def m2(E, h=h, i=i, nb=nb, ci=ci, st=(lead(nb) if (i == 0) else False)):
                        return E.matmul(
                            PS[nb][:, i * EV:i * EV + EV],
                            am_s[:, h * 128:(h + 1) * 128],
                            vtok[:, ci * H * EV + h * EV: ci * H * EV + (h + 1) * EV],
                            start=st, stop=False)

                    def m3(E, h=h, i=i, nb=nb, ci=ci):
                        return E.matmul(
                            PS[nb][:, i * EV:i * EV + EV],
                            qT[:, (h // 2) * TILE + ci * 128:(h // 2) * TILE + (ci + 1) * 128],
                            st_b[:, h * EV:(h + 1) * EV],
                            start=False, stop=(i == 3))
                    fns += [m2, m3]
                p.emit(TE, fns, reads=amr + ["vtok", "qT", "st_b"], writes=[f"PB{nb}"])
                p.emit(SC, lambda E, nb=nb, g=g, ci=ci: E.copy(
                    numsave[:, ci * H * EV + g * 4 * EV: ci * H * EV + (g + 1) * 4 * EV],
                    PS[nb][:, 0:4 * EV]),
                    reads=[f"PB{nb}"], writes=[f"nsv{ci}_{g}"])
                bk_done(nb)
                tern_tick()

        def scan_tail(ci):
            ktok_c = scanref["ktok_c"][ci % 2]
            numsave = scanref["numsave"]
            fns = []
            wr = set()
            for bank, h0, h1 in SDSEG:
                for i in range(h1 - h0):
                    h = h0 + i

                    def m4(E, h=h, i=i, bank=bank, ci=ci, ktok_c=ktok_c,
                           last=(i == h1 - h0 - 1), st=(lead(bank) if (i == 0) else False)):
                        return E.matmul(
                            PS[bank][0:64, i * EV:i * EV + EV],
                            ktok_c[:, h * HD:(h + 1) * HD],
                            vtok[:, ci * H * EV + h * EV: ci * H * EV + (h + 1) * EV],
                            start=st, stop=last)
                    fns.append(m4)
                    wr.add(f"PB{bank}")
            p.emit(TE, fns, reads=[f"ktokc{ci % 2}", "vtok"], writes=sorted(wr))
            state_fold()
            if ci == NCH - 1:
                ti = scanref["ti"]
                p.emit(SY, lambda E, ti=ti: E.dma_start(
                    out=num_d[ti * TILE:(ti + 1) * TILE, :].rearrange(
                        "(c q) e -> q c e", q=128),
                    in_=numsave.rearrange("q (c e) -> q c e", c=NCH)),
                    reads=[f"nsv{c}_{g}" for c in range(NCH) for g in range(4)],
                    writes=["num_dd"], sem="LD", inc=16)

        tern_chunks = [("up", j) for j in range(64)] + [("dn", j) for j in range(64)]

        def tern_load(kind, j, b):
            if kind == "up":
                src = wupT[(j // 8) * 128:(j // 8 + 1) * 128,
                           (j % 8) * 512:(j % 8 + 1) * 512]
            else:
                src = wdnT[(j // 2) * 128:(j // 2 + 1) * 128,
                           (j % 2) * 512:(j % 2 + 1) * 512]
            p.emit(GP, lambda E, s=src, b=b: E.dma_start(out=twf[b], in_=s),
                   writes=[f"twf{b}"], sem="LDG", inc=16)

        def tern_abs(kind, j, b):
            col = 0 if kind == "up" else 1
            p.emit(SC, lambda E, b=b: E.activation(
                tg1, twf[b], AF.Abs, accum_out=tred[:, 0:1]),
                reads=[f"twf{b}"], writes=["tred", "tg1"])
            p.emit(VE, lambda E, col=col: E.tensor_tensor(
                tpart[:, col:col + 1], tpart[:, col:col + 1], tred[:, 0:1], ALU.add),
                reads=["tred", "tpart"], writes=["tpart"])

        def tern_finalize():
            p.emit(TE, lambda E, st=lead(6): E.matmul(
                PS[6][0:1, 0:2], ones_cf[:, 0:1], tpart[:, 0:2], start=st, stop=True),
                reads=["tpart", "ones_cf"], writes=["PB6"])
            p.emit(SC, lambda E: E.activation(
                trow_s[0:1, 0:2], PS[6][0:1, 0:2], AF.Copy,
                scale=float(0.5 / (D * DFF))),
                reads=["PB6"], writes=["trow"])
            bk_done(6)
            p.emit(TE, lambda E, st=lead(6): E.matmul(
                PS[6][:, 0:2], ones_rf[0:1, 0:128], trow_s[0:1, 0:2],
                start=st, stop=True),
                reads=["trow", "ones_rf"], writes=["PB6"])
            p.emit(VE, lambda E: E.tensor_copy(thr_s[:, 0:2], PS[6][:, 0:2]),
                   reads=["PB6"], writes=["thr"])
            p.emit(VE, lambda E: E.tensor_scalar(
                out=nthr_s[:, 0:2], in0=thr_s[:, 0:2], scalar1=-1.0, scalar2=None,
                op0=ALU.mult),
                reads=["thr"], writes=["nthr"])
            bk_done(6)

        def tern_quant(kind, j, b):
            col = 0 if kind == "up" else 1
            p.emit(VE, lambda E, b=b, col=col: E.tensor_scalar(
                out=tg1, in0=twf[b], scalar1=nthr_s[:, col:col + 1], scalar2=None,
                op0=ALU.is_lt),
                reads=[f"twf{b}", "nthr"], writes=["tg1"])
            p.emit(VE, lambda E, b=b, col=col: E.scalar_tensor_tensor(
                out=tout[b], in0=twf[b], scalar=thr_s[:, col:col + 1], in1=tg1,
                op0=ALU.is_gt, op1=ALU.subtract),
                reads=[f"twf{b}", "thr", "tg1"], writes=[f"tout{b}"])
            if kind == "up":
                dst = wup_t[(j // 8) * 128:(j // 8 + 1) * 128,
                            (j % 8) * 512:(j % 8 + 1) * 512]
                scr = "scr_up"
            else:
                dst = wdn_t[(j // 2) * 128:(j // 2 + 1) * 128,
                            (j % 2) * 512:(j % 2 + 1) * 512]
                scr = "scr_dn"
            p.emit(SY, lambda E, d=dst, b=b: E.dma_start(out=d, in_=tout[b]),
                   reads=[f"tout{b}"], writes=[scr], sem="LD", inc=16)

        tst = {"a": 0, "b": 0, "fin": False}
        quant_done = set()

        def tern_step(n):
            for _ in range(n):
                if tst["a"] < len(tern_chunks):
                    k, j = tern_chunks[tst["a"]]
                    tern_load(k, j, tst["b"])
                    tern_abs(k, j, tst["b"])
                    tst["a"] += 1
                    tst["b"] ^= 1

        def tern_abs_flush():
            while tst["a"] < len(tern_chunks):
                tern_step(1)
            if not tst["fin"]:
                tern_finalize()
                tst["fin"] = True

        def tern_quant_chunks(chunks):
            for k, j in chunks:
                if (k, j) in quant_done:
                    continue
                quant_done.add((k, j))
                tern_load(k, j, tst["b"])
                tern_quant(k, j, tst["b"])
                tst["b"] ^= 1

        def tern_tick(n=2):
            if tst["a"] < len(tern_chunks):
                tern_step(n)
            elif not tst["fin"]:
                tern_finalize()
                tst["fin"] = True
            else:
                tern_quant_step(n)

        qsched = {"i": 0}

        def tern_quant_step(n):
            c = 0
            while qsched["i"] < len(tern_chunks) and c < n:
                k, j = tern_chunks[qsched["i"]]
                qsched["i"] += 1
                if (k, j) in quant_done:
                    continue
                tern_quant_chunks([(k, j)])
                c += 1

        p.barrier()
        cur[0] = var_base
        qT = take(4096)
        kT = take(4096)
        ktok_c = [take(1024) for _ in range(2)]
        am_s = take(2048)
        numsave = take(4352)
        scanref["qT"] = qT
        scanref["kT"] = kT
        scanref["ktok_c"] = ktok_c
        scanref["am_s"] = am_s
        scanref["numsave"] = numsave
        def a1_load_stats(ti):
            xb = ti % 2
            xname = f"x{xb}"
            p.emit(SY, lambda E, ti=ti, xb=xb: E.dma_start(
                out=x_sl[xb].rearrange("q (c d) -> q c d", c=NCH),
                in_=x_own[ti * TILE:(ti + 1) * TILE, :].rearrange("(c q) d -> q c d", q=128)),
                writes=[xname], inc=16)
            ln_stats(xb, xname)

        a1_load_stats(0)
        for ti in range(nown_tiles):
            xb = ti % 2
            xname = f"x{xb}"
            xn_apply(xb, xname, rstd_s, "rstd", cA_s, "cA")
            transpose_tok_to_ft(xn_tok, "xn_tok", xnT, "xnT")
            proj_feature(wq_s, "wq", bq_s, "bq", qT, "qT")
            proj_feature(wk_s, "wk", bk_s, "bk", kT, "kT")
            scanref["ti"] = ti
            p.emit(SY, lambda E, ti=ti: E.dma_start(
                out=qT_d[ti * 128:(ti + 1) * 128, :], in_=qT),
                reads=["qT"], writes=["qT_dd"], sem="LD", inc=16)
            if ti + 1 < nown_tiles:
                a1_load_stats(ti + 1)

            def v_writer(c, half, bank):
                ov = vtok.rearrange("q (c h e) -> q c h e", c=NCH, h=H)[
                    :, c, half * 8:(half + 1) * 8, 0:HD]
                p.emit(SC, lambda E, ov=ov, bank=bank: E.copy(
                    ov, PS[bank][:].rearrange("q (h e) -> q h e", h=8)),
                    reads=[f"PB{bank}"], writes=["vtok"])
            proj_token(wv_s, "wv", bvr_s, "bv", v_writer, [0, 1, 2, 3])

            scan_pre(0)
            for ci in range(NCH):
                scan_groups(ci)
                if ci < NCH - 1:
                    scan_pre(ci + 1)
                scan_tail(ci)

        tern_abs_flush()
        tern_quant_step(len(tern_chunks))

        p.emit(SY, lambda E: E.dma_start(out=st_loc, in_=st_f[:, 0:H * EV]),
               reads=["st_f"], writes=["st_loc_d"], sem="LD", inc=16)
        p.emit(GP, lambda E: E.collective_compute(
            "AllGather", ALU.bypass,
            replica_groups=[list(range(NCORE))],
            ins=[st_loc.opt()], outs=[st_all.opt()]),
            reads=["st_loc_d"], writes=["st_all_d"], sem="CC", inc=1)
        p.barrier()
        cur[0] = wts_top
        hT = take(16384)
        fwu = [take(4096) for _ in range(2)]
        fwd = [take(2048) for _ in range(2)]
        hsq = [take(512) for _ in range(2)]

        def b_load_stats(ti):
            xb = ti % 2
            xname = f"x{xb}"
            p.emit(SY, lambda E, ti=ti, xb=xb: E.dma_start(
                out=x_sl[xb].rearrange("q (c d) -> q c d", c=NCH),
                in_=x_own[ti * TILE:(ti + 1) * TILE, :].rearrange("(c q) d -> q c d", q=128)),
                writes=[xname], inc=16)
            ln_stats(xb, xname, sffn=True)

        b_load_stats(0)
        for ti in range(nown_tiles):
            xb = ti % 2
            xname = f"x{xb}"
            xn_apply(xb, xname, sffn_s, "sffn", cB_s, "cB")
            transpose_tok_to_ft(xn_tok, "xn_tok", xnT, "xnT")

            def fwu_load(fs):
                fb = fs % 2
                tern_quant_chunks([("up", jm * 8 + fs) for jm in range(DC)])
                srcv = wup_t.rearrange("(jm q) f -> q jm f", q=128)[:, :, fs * 512:(fs + 1) * 512]
                p.emit(GP, lambda E, srcv=srcv, fb=fb: E.dma_start(
                    out=fwu[fb].rearrange("q (jm f) -> q jm f", jm=DC), in_=srcv),
                    reads=["scr_up"], writes=[f"fwu{fb}"], sem="LDG", inc=16)

            fwu_load(0)
            for fs in range(DFF // 512):
                fb = fs % 2
                if fs + 1 < DFF // 512:
                    fwu_load(fs + 1)
                for lf in range(4):
                    jf = fs * 4 + lf
                    bank = jf % 4
                    fns = []
                    for jm in range(DC):
                        def mm(E, jm=jm, lf=lf, fb=fb, bank=bank, st=(lead(bank) if (jm == 0) else False)):
                            return E.matmul(
                                PS[bank][:],
                                fwu[fb][:, jm * 512 + lf * 128: jm * 512 + (lf + 1) * 128],
                                xnT[:, jm * TILE:(jm + 1) * TILE],
                                start=st, stop=(jm == DC - 1))
                        fns.append(mm)
                    p.emit(TE, fns, reads=[f"fwu{fb}", "xnT"], writes=[f"PB{bank}"])
                    if jf > 0:
                        pj, phb = jf - 1, (jf - 1) % 2
                        p.emit(TE, lambda E, pj=pj, phb=phb,
                               st=(lead(6) if pj == 0 else False): E.matmul(
                            PS[6][0:1, :], ones_col, hsq[phb],
                            start=st, stop=False),
                            reads=[f"hsq{phb}", "ones_col"], writes=["PB6"])
                    p.emit(SC, lambda E, jf=jf, bank=bank: E.activation(
                        hT[:, jf * TILE:(jf + 1) * TILE], PS[bank][:], AF.Gelu),
                        reads=[f"PB{bank}"], writes=[f"hTp{jf % 2}"])
                    hb = jf % 2
                    p.emit(VE, lambda E, jf=jf, hb=hb: E.tensor_mul(
                        hsq[hb], hT[:, jf * TILE:(jf + 1) * TILE],
                        hT[:, jf * TILE:(jf + 1) * TILE]),
                        reads=[f"hTp{jf % 2}"], writes=[f"hsq{hb}"])
                    bk_done(bank)
            p.emit(TE, lambda E: E.matmul(
                PS[6][0:1, :], ones_col, hsq[(FC - 1) % 2],
                start=False, stop=True),
                reads=[f"hsq{(FC - 1) % 2}", "ones_col"], writes=["PB6"])
            if ti + 1 < nown_tiles:
                b_load_stats(ti + 1)
            p.emit(SC, lambda E: E.activation(
                s2row[0:1, 0:TILE], PS[6][0:1, :], AF.Ln, scale=float(1.0 / DFF)),
                reads=["PB6"], writes=["s2row"])
            bk_done(6)
            p.emit(SC, lambda E: E.activation(
                s2row[0:1, 0:TILE], s2row[0:1, 0:TILE], AF.Exp, scale=0.5),
                reads=["s2row"], writes=["s2row"])
            p.emit(VE, lambda E: E.tensor_scalar(
                out=s2row[0:1, 0:TILE], in0=s2row[0:1, 0:TILE], scalar1=float(RMSEPS),
                scalar2=None, op0=ALU.add),
                reads=["s2row"], writes=["s2row"])
            p.emit(VE, lambda E: E.reciprocal(s2row[0:1, 0:TILE], s2row[0:1, 0:TILE]),
                   reads=["s2row"], writes=["s2row"])
            fns = []
            for c in range(NCH):
                def mm(E, c=c, st=(lead(6) if (c == 0) else False)):
                    return E.matmul(
                        PS[6][:, c:c + 1], s2row[0:1, c * 128:(c + 1) * 128],
                        ones_1f[0:1, 0:1], start=st, stop=(c == NCH - 1))
                fns.append(mm)
            p.emit(TE, fns, reads=["s2row", "ones_1f"], writes=["PB6"])
            p.emit(VE, lambda E: E.tensor_copy(s2col[:, 0:NCH], PS[6][:, 0:NCH]),
                   reads=["PB6"], writes=["s2col"])
            bk_done(6)

            NDS = DFF // 512

            def fwd_load(idx):
                half, fs = idx // NDS, idx % NDS
                fb = idx % 2
                tern_quant_chunks([("dn", (fs * 4 + r) * 2 + half) for r in range(4)])
                srcv = wdn_t.rearrange("(jf q) d -> q jf d", q=128)[
                    :, fs * 4:(fs + 1) * 4, half * 512:(half + 1) * 512]
                p.emit(GP, lambda E, srcv=srcv, fb=fb: E.dma_start(
                    out=fwd[fb].rearrange("q (jf d) -> q jf d", jf=4), in_=srcv),
                    reads=["scr_dn"], writes=[f"fwd{fb}"], sem="LDG", inc=16)

            fwd_load(0)
            for half in range(2):
                for fs in range(DFF // 512):
                    idx = half * NDS + fs
                    fb = idx % 2
                    if idx + 1 < 2 * NDS:
                        fwd_load(idx + 1)
                    fns = []
                    for lf in range(4):
                        jf = fs * 4 + lf
                        for c in range(NCH):
                            def mm(E, jf=jf, lf=lf, c=c, fb=fb, st=(lead(c) if (jf == 0) else False)):
                                return E.matmul(
                                    PS[c][:],
                                    hT[:, jf * TILE + c * 128: jf * TILE + (c + 1) * 128],
                                    fwd[fb][:, lf * 512:(lf + 1) * 512],
                                    start=st, stop=(jf == FC - 1))
                            fns.append(mm)
                    p.emit(TE, fns, reads=[f"fwd{fb}", "hTp0", "hTp1"],
                           writes=[f"PB{c}" for c in range(NCH)])
                for c in range(NCH):
                    dv = out_sl[:, c * D + half * 512: c * D + (half + 1) * 512]
                    pv = x_sl[xb][:, c * D + half * 512: c * D + (half + 1) * 512]
                    p.emit(VE, lambda E, dv=dv, pv=pv, c=c: E.scalar_tensor_tensor(
                        out=dv, in0=PS[c][:], scalar=s2col[:, c:c + 1], in1=pv,
                        op0=ALU.mult, op1=ALU.add),
                        reads=[f"PB{c}", "s2col", xname], writes=["out_sl"])
                    bk_done(c)
            p.emit(SY, lambda E, ti=ti: E.dma_start(
                out=out[ti * TILE:(ti + 1) * TILE, :].rearrange("(c q) d -> q c d", q=128),
                in_=out_sl.rearrange("q (c d) -> q c d", c=NCH)),
                reads=["out_sl"], writes=[f"fdram{ti}"], sem="ST", inc=16)

        p.emit(VE, lambda E: E.memset(st_f[:, 0:H * EV], 0.0),
               reads=["st_f"], writes=["st_f"])
        for r in range(NCORE):
            p.emit(SY, lambda E, r=r: E.dma_start(
                out=shard_s[:, 0:H * EV], in_=st_all[r * 64:(r + 1) * 64, :]),
                reads=["st_all_d"], writes=["shard"], inc=16)
            p.emit(VE, lambda E, r=r: E.scalar_tensor_tensor(
                out=st_f[:, 0:H * EV], in0=shard_s[:, 0:H * EV],
                scalar=sm_s[0:64, r:r + 1], in1=st_f[:, 0:H * EV],
                op0=ALU.mult, op1=ALU.add),
                reads=["shard", "sm", "st_f"], writes=["st_f"])
        p.emit(VE, lambda E: E.memset(st_b, 0.0), reads=["st_b"], writes=["st_b"])
        p.emit(VE, lambda E: E.tensor_copy(
            st_b[0:64, :].rearrange("q (h e) -> q h e", h=H)[:, 0:H:2, :],
            st_f[:, 0:H * EV].rearrange("q (h e) -> q h e", h=H)[:, 0:H:2, :]),
            reads=["st_f", "st_b"], writes=["st_b"])
        p.emit(VE, lambda E: E.tensor_copy(
            st_b[64:128, :].rearrange("q (h e) -> q h e", h=H)[:, 1:H:2, :],
            st_f[:, 0:H * EV].rearrange("q (h e) -> q h e", h=H)[:, 1:H:2, :]),
            reads=["st_f", "st_b"], writes=["st_b"])

        p.barrier()
        cur[0] = wts_top
        qT2b = [take(4096) for _ in range(2)]
        num2b = [take(4352) for _ in range(2)]
        part2b = [take(8192, F32) for _ in range(2)]
        o_tok = take(1024)
        oT = take(4096)
        a2scr = take(544, F32)

        def a2_loads(ti):
            tb = ti % 2
            p.emit(SY, lambda E, ti=ti, tb=tb: E.dma_start(
                out=qT2b[tb], in_=qT_d[ti * 128:(ti + 1) * 128, :]),
                reads=["qT_dd"], writes=[f"qT2{tb}"], inc=16)
            p.emit(SY, lambda E, ti=ti, tb=tb: E.dma_start(
                out=num2b[tb].rearrange("q (c e) -> q c e", c=NCH),
                in_=num_d[ti * TILE:(ti + 1) * TILE, :].rearrange("(c q) e -> q c e", q=128)),
                reads=["num_dd"], writes=[f"num2{tb}"], inc=16)
            p.emit(SY, lambda E, ti=ti, tb=tb: E.dma_start(
                out=part2b[tb].rearrange("q (c d) -> q c d", c=NCH),
                in_=out[ti * TILE:(ti + 1) * TILE, :].rearrange("(c q) d -> q c d", q=128)),
                reads=[f"fdram{ti}"], writes=[f"part2{tb}"], inc=16)

        a2_loads(0)
        for ti in range(nown_tiles):
            tb = ti % 2
            qT2, num2, part2 = qT2b[tb], num2b[tb], part2b[tb]
            if ti + 1 < nown_tiles:
                a2_loads(ti + 1)
            for ci in range(NCH):
                for g in range(4):
                    nb = g
                    fns = []
                    for i in range(4):
                        h = g * 4 + i

                        def m3c(E, h=h, i=i, nb=nb, ci=ci,
                                st=(lead(nb) if (i == 0) else False)):
                            return E.matmul(
                                PS[nb][:, i * EV:i * EV + EV],
                                qT2[:, (h // 2) * TILE + ci * 128:(h // 2) * TILE + (ci + 1) * 128],
                                st_b[:, h * EV:(h + 1) * EV],
                                start=st, stop=(i == 3))
                        fns.append(m3c)
                    p.emit(TE, fns, reads=[f"qT2{tb}", "st_b"], writes=[f"PB{nb}"])
                    nsv = num2[:, ci * H * EV + g * 4 * EV: ci * H * EV + (g + 1) * 4 * EV]
                    p.emit(VE, lambda E, nb=nb, nsv=nsv: E.tensor_tensor(
                        a2scr[:, 0:4 * EV], PS[nb][:, 0:4 * EV], nsv, ALU.add),
                        reads=[f"PB{nb}", f"num2{tb}"], writes=["a2scr"])
                    bk_done(nb)
                    p.emit(VE, lambda E: E.tensor_scalar(
                        out=rcp_s[:, 0:4],
                        in0=a2scr[:, 0:4 * EV].rearrange("q (i e) -> q i e", i=4)[:, :, HD],
                        scalar1=float(ATTEPS), scalar2=None, op0=ALU.add),
                        reads=["a2scr"], writes=["rcp"])
                    p.emit(VE, lambda E: E.reciprocal(rcp_s[:, 0:4], rcp_s[:, 0:4]),
                           reads=["rcp"], writes=["rcp"])
                    ov = o_tok[:, g * 256:(g + 1) * 256].rearrange("q (i e) -> q i e", i=4)
                    p.emit(VE, lambda E, ov=ov: E.tensor_tensor(
                        ov, a2scr[:, 0:4 * EV].rearrange("q (i e) -> q i e", i=4)[:, :, 0:HD],
                        rcp_s[:, 0:4].rearrange("q i -> q i ()").broadcast_to([128, 4, HD]),
                        ALU.mult),
                        reads=["a2scr", "rcp"], writes=["o_tok"])
                fns = []
                for jd in range(DC):
                    def tr(E, jd=jd):
                        return E.transpose(
                            P7[:, jd * 128:(jd + 1) * 128],
                            o_tok[:, jd * 128:(jd + 1) * 128], ident)
                    fns.append(tr)
                p.emit(TE, fns, reads=["o_tok", "ident"], writes=["P7"])
                ov = oT.rearrange("q (jd t) -> q jd t", jd=DC)[:, :, ci * 128:(ci + 1) * 128]
                p.emit(SC, lambda E, ov=ov: E.copy(
                    ov, P7[:].rearrange("q (jd t) -> q jd t", jd=DC)),
                    reads=["P7"], writes=["oT"])
                bk_done(7)
            for c in range(NCH):
                for half in range(2):
                    bank = half
                    fns = []
                    for jd in range(DC):
                        def mm(E, c=c, half=half, jd=jd, bank=bank,
                               st=(lead(bank) if (jd == 0) else False)):
                            return E.matmul(
                                PS[bank][:],
                                oT[:, jd * TILE + c * 128: jd * TILE + (c + 1) * 128],
                                wo_s[:, jd * D + half * 512: jd * D + (half + 1) * 512],
                                start=st, stop=False)
                        fns.append(mm)

                    def mmb(E, half=half, bank=bank):
                        return E.matmul(
                            PS[bank][:], ones_row, bor_s[0:1, half * 512:(half + 1) * 512],
                            start=False, stop=True)
                    fns.append(mmb)
                    p.emit(TE, fns, reads=["oT", "wo", "bo", "ones_row"], writes=[f"PB{bank}"])
                    dv = out_sl[:, c * D + half * 512: c * D + (half + 1) * 512]
                    pv = part2[:, c * D + half * 512: c * D + (half + 1) * 512]
                    p.emit(VE, lambda E, dv=dv, pv=pv, bank=bank: E.tensor_tensor(
                        dv, PS[bank][:], pv, ALU.add),
                        reads=[f"PB{bank}", f"part2{tb}"], writes=["out_sl"])
                    bk_done(bank)
            p.emit(SY, lambda E, ti=ti: E.dma_start(
                out=out[ti * TILE:(ti + 1) * TILE, :].rearrange("(c q) d -> q c d", q=128),
                in_=out_sl.rearrange("q (c d) -> q c d", c=NCH)),
                reads=["out_sl"], writes=[f"adram{ti}"], sem="ST", inc=16)

        total_st = p.cnt["ST0"]
        h_st = p.semh["ST0"]
        p.ops["sync"].append(lambda E, h=h_st, v=total_st: E.wait_ge(h, v))

        block = es.enter_context(nc.Block())

        def replay(name):
            def f(E):
                for op in p.ops[name]:
                    op(E)
            return f
        block.sync(replay("sync"))
        block.tensor(replay("tensor"))
        block.vector(replay("vector"))
        block.scalar(replay("scalar"))
        block.gpsimd(replay("gpsimd"))

    return nc


def _prep(Wq, bq, Wk, bk, Wv, bv, Wo, bo, w_up, w_down):
    f = np.float32
    return {
        "wqT": np.ascontiguousarray(np.asarray(Wq, f).T),
        "wkT": np.ascontiguousarray(np.asarray(Wk, f).T),
        "wvT": np.ascontiguousarray(np.asarray(Wv, f).T),
        "woT": np.ascontiguousarray(np.asarray(Wo, f).T),
        "bq8": np.ascontiguousarray(np.asarray(bq, f).reshape(DC, 128).T),
        "bk8": np.ascontiguousarray(np.asarray(bk, f).reshape(DC, 128).T),
        "bv_row": np.asarray(bv, f).reshape(1, D).copy(),
        "bo_row": np.asarray(bo, f).reshape(1, D).copy(),
        "wupT": np.ascontiguousarray(np.asarray(w_up, f).T),
        "wdnT": np.ascontiguousarray(np.asarray(w_down, f).T),
    }


def kernel(L0_attn_x, L0_attn_Wq, L0_attn_bq, L0_attn_Wk, L0_attn_bk,
           L0_attn_Wv, L0_attn_bv, L0_attn_Wo, L0_attn_bo,
           L0_ffn_w_up, L0_ffn_w_down):
    x = np.ascontiguousarray(np.asarray(L0_attn_x, dtype=np.float32))
    weights = _prep(L0_attn_Wq, L0_attn_bq, L0_attn_Wk, L0_attn_bk, L0_attn_Wv,
                    L0_attn_bv, L0_attn_Wo, L0_attn_bo, L0_ffn_w_up, L0_ffn_w_down)

    OWN = S // 4
    nown = OWN // TILE

    nc = build_nc(0, nown)
    in_maps = []
    for core in range(NCORE):
        b, pq = core // 4, core % 4
        xb = x[b]
        start = pq * OWN
        sm = np.zeros((128, NCORE), np.float32)
        for r in range(NCORE):
            if r // 4 == b and r % 4 < pq:
                sm[:, r] = 1.0
        m = {"x_own": np.ascontiguousarray(xb[start:start + OWN]), "smask": sm}
        m.update(weights)
        in_maps.append(m)
    res = run_bass_kernel_spmd(nc, in_maps, core_ids=list(range(NCORE)))
    outp = np.empty((B, S, D), np.float32)
    for core in range(NCORE):
        b, pq = core // 4, core % 4
        outp[b, pq * OWN:(pq + 1) * OWN] = np.asarray(res.results[core]["out"])
    return outp


# version 18
# speedup vs baseline: 1.2205x; 1.0128x over previous
import numpy as np
from contextlib import ExitStack

import concourse.bass as bass
import concourse.mybir as mybir
from concourse.bass_utils import run_bass_kernel_spmd

F32 = mybir.dt.float32
BF16 = mybir.dt.bfloat16
AF = mybir.ActivationFunctionType
ALU = mybir.AluOpType
AX = mybir.AxisListType

B, S, D, H, HD, DFF = 2, 8192, 1024, 16, 64, 4096
NCORE = 8
TILE = 512
CH = 128
EV = 68
LNEPS = 1e-5
ATTEPS = 1e-6
RMSEPS = 1e-8
NCH = TILE // CH
DC = D // 128
FC = DFF // 128

ENGS = ("sync", "tensor", "vector", "scalar", "gpsimd")
ESEM = {"tensor": "PE", "vector": "DV", "scalar": "AC", "gpsimd": "GP", "sync": "LD"}
SY, TE, VE, SC, GP = ENGS


class Prog:

    def __init__(self, nc):
        self.nc = nc
        self.ops = {e: [] for e in ENGS}
        self.semh = {}
        self.cnt = {}
        self.seen = {e: {} for e in ENGS}
        self.buf = {}
        self.dmarr = {}

    def mksem(self, es, *names):
        for n in names:
            self.semh[n] = es.enter_context(self.nc.semaphore(n))
            self.cnt[n] = 0

    def _wait(self, eng, sname, v):
        if self.seen[eng].get(sname, 0) >= v:
            return
        self.seen[eng][sname] = v
        h = self.semh[sname]
        self.ops[eng].append(lambda E, h=h, v=v: E.wait_ge(h, v))

    DMA_POOLS = {"LD": ["LD0", "LD1", "LD2", "LD3"],
                 "LDG": ["LG0", "LG1", "LG2", "LG3"],
                 "ST": ["ST0"]}

    def emit(self, eng, fns, reads=(), writes=(), sem=None, inc=1):
        if not isinstance(fns, (list, tuple)):
            fns = [fns]
        if sem is None:
            sem = ESEM[eng]
        if sem in self.DMA_POOLS:
            pool = self.DMA_POOLS[sem]
            idx = self.dmarr.get(sem, 0)
            self.dmarr[sem] = (idx + 1) % len(pool)
            sem = pool[idx]
            if self.cnt[sem] > 0:
                self._wait(eng, sem, self.cnt[sem])
        need = {}
        for b in reads:
            st = self.buf.get(b)
            if st and st["w"]:
                s, v = st["w"]
                need[s] = max(need.get(s, 0), v)
        for b in writes:
            st = self.buf.get(b)
            if st:
                toks = ([st["w"]] if st["w"] else []) + st["r"]
                for s, v in toks:
                    need[s] = max(need.get(s, 0), v)
        for s in sorted(need):
            self._wait(eng, s, need[s])
        for f in fns[:-1]:
            self.ops[eng].append(f)
        self.cnt[sem] += inc
        v = self.cnt[sem]
        h = self.semh[sem]
        last = fns[-1]
        self.ops[eng].append(lambda E, last=last, h=h, inc=inc: last(E).then_inc(h, inc))
        tok = (sem, v)
        for b in writes:
            self.buf[b] = {"w": tok, "r": []}
        for b in reads:
            st = self.buf.setdefault(b, {"w": None, "r": []})
            st["r"].append(tok)
        return tok

    def barrier(self, exclude=("CC",)):
        cur = {s: v for s, v in self.cnt.items() if v > 0 and s not in exclude}
        for e in ENGS:
            for s in sorted(cur):
                self._wait(e, s, cur[s])


def build_nc(npre_slabs, nown_tiles):
    OWN = nown_tiles * TILE
    PRE = max(npre_slabs * TILE, TILE)
    nc = bass.Bass("TRN2", target_bir_lowering=False)

    x_own = nc.dram_tensor("x_own", [OWN, D], F32, kind="ExternalInput").ap()
    smask = nc.dram_tensor("smask", [128, NCORE], F32, kind="ExternalInput").ap()
    wqT = nc.dram_tensor("wqT", [D, D], F32, kind="ExternalInput").ap()
    wkT = nc.dram_tensor("wkT", [D, D], F32, kind="ExternalInput").ap()
    wvT = nc.dram_tensor("wvT", [D, D], F32, kind="ExternalInput").ap()
    woT = nc.dram_tensor("woT", [D, D], F32, kind="ExternalInput").ap()
    bq8 = nc.dram_tensor("bq8", [128, DC], F32, kind="ExternalInput").ap()
    bk8 = nc.dram_tensor("bk8", [128, DC], F32, kind="ExternalInput").ap()
    bv_row = nc.dram_tensor("bv_row", [1, D], F32, kind="ExternalInput").ap()
    bo_row = nc.dram_tensor("bo_row", [1, D], F32, kind="ExternalInput").ap()
    wupT = nc.dram_tensor("wupT", [D, DFF], F32, kind="ExternalInput").ap()
    wdnT = nc.dram_tensor("wdnT", [DFF, D], F32, kind="ExternalInput").ap()
    out = nc.dram_tensor("out", [OWN, D], F32, kind="ExternalOutput").ap()
    wup_t = nc.dram_tensor("wup_t", [D, DFF], BF16, kind="Internal").ap()
    wdn_t = nc.dram_tensor("wdn_t", [DFF, D], BF16, kind="Internal").ap()
    qT_d = nc.dram_tensor("qT_d", [nown_tiles * 128, DC * TILE], BF16, kind="Internal").ap()
    num_d = nc.dram_tensor("num_d", [OWN, H * EV], BF16, kind="Internal").ap()
    st_loc = nc.dram_tensor("st_loc", [64, H * EV], F32, kind="Internal").ap()
    st_all = nc.dram_tensor("st_all", [NCORE * 64, H * EV], F32, kind="Internal").ap()

    with ExitStack() as es:
        pst = lambda name, shape, dt: es.enter_context(nc.psum_tensor(name, shape, dt))

        ARENA_BC = 106000
        arena = es.enter_context(nc.sbuf_tensor("arena", [128, ARENA_BC], BF16))
        cur = [0]

        def take(bcols, dt=BF16, parts=128):
            o = cur[0]
            bcols_al = (bcols + 15) // 16 * 16
            cur[0] += bcols_al
            assert cur[0] <= ARENA_BC, f"arena overflow at {cur[0]}"
            v = arena[0:parts, o:o + bcols]
            if dt == F32:
                v = v.bitcast(F32)
            return v

        x_sl = [take(8192, F32) for _ in range(2)]
        out_sl = take(8192, F32)
        xn_tok = take(4096)
        xnT = take(4096)
        st_f = take(2176, F32, parts=64)
        st_b = take(1088)
        bn_s = take(96, F32)
        mv_s = take(16, F32)
        rstd_s = take(8, F32)
        cA_s = take(8, F32)
        sffn_s = take(8, F32)
        cB_s = take(8, F32)
        lnt_s = take(32, F32)
        rcp_s = take(8, F32)
        s2col = take(8, F32)
        tpart = take(4, F32)
        tred = take(2, F32)
        thr_s = take(4, F32)
        nthr_s = take(4, F32)
        lneps_c = take(2, F32)
        mask8 = take(512)
        ident = take(128)
        ones_col = take(1)
        ones_cf = take(2, F32)
        bq_s = take(16, F32)
        bk_s = take(16, F32)
        sm_s = take(16, F32)
        shard_s = take(2176, F32, parts=64)
        ones_row = take(128, parts=1)
        ones_rf = take(256, F32, parts=1)
        ones_1f = take(2, F32, parts=1)
        bvr_s = take(1024, parts=1)
        bor_s = take(1024, parts=1)
        s2row = take(1024, F32, parts=1)
        trow_s = take(4, F32, parts=1)

        twf = [take(1024, F32) for _ in range(2)]
        tg1 = take(1024, F32)
        tout = [take(512) for _ in range(2)]
        dyn_base = cur[0]

        wo_s = take(8192)
        wts_top = cur[0]
        wq_s = take(8192)
        wk_s = take(8192)
        wv_s = take(8192)
        vtok = take(4352)
        ph_e = [take(1024, F32) for _ in range(2)]
        ph_r = [take(1024, F32) for _ in range(2)]
        var_base = cur[0]

        ktok = take(4096)
        qT = kT = ktok_c = o_tok = oT = am_s = None

        PS = [pst(f"P{i}", [128, 512], F32) for i in range(7)]
        P7 = pst("P7", [128, 1024], BF16)

        p = Prog(nc)
        p.mksem(es, "LD0", "LD1", "LD2", "LD3", "LG0", "LG1", "LG2", "LG3", "ST0", "CC", "PE", "DV", "AC", "GP")

        bank_fresh = {k: True for k in range(8)}

        def lead(i):
            f = bank_fresh[i]
            bank_fresh[i] = False
            return f

        def bk_done(i):
            bank_fresh[i] = True

        p.emit(GP, lambda E: E.memset(ident, 0.0), writes=["ident"])
        p.emit(GP, lambda E: E.affine_select(
            out=ident, in_=ident, compare_op=ALU.not_equal, fill=1.0,
            base=0, pattern=[[-1, 128]], channel_multiplier=1),
            reads=["ident"], writes=["ident"])
        p.emit(GP, lambda E: E.memset(mask8, 1.0), writes=["mask8"])
        for i in range(4):
            p.emit(GP, lambda E, i=i: E.affine_select(
                out=mask8[:, i * CH:(i + 1) * CH], in_=mask8[:, i * CH:(i + 1) * CH],
                compare_op=ALU.is_ge, fill=0.0, base=0,
                pattern=[[1, CH]], channel_multiplier=-1),
                reads=["mask8"], writes=["mask8"])
        p.emit(GP, lambda E: E.memset(lneps_c[:, 0:1], float(LNEPS)), writes=["lneps"])
        for t, n in [(ones_row, "ones_row"), (ones_col, "ones_col"),
                     (ones_rf, "ones_rf"), (ones_cf, "ones_cf"), (ones_1f, "ones_1f")]:
            p.emit(GP, lambda E, t=t: E.memset(t, 1.0), writes=[n])
        p.emit(VE, lambda E: E.memset(st_f, 0.0), writes=["st_f"])
        p.emit(VE, lambda E: E.memset(st_b, 0.0), writes=["st_b"])
        p.emit(VE, lambda E: E.memset(tpart, 0.0), writes=["tpart"])
        p.emit(VE, lambda E: E.memset(vtok, 0.0), writes=["vtok"])
        p.emit(VE, lambda E: E.memset(
            vtok.rearrange("q (c h e) -> q c h e", c=NCH, h=H)[:, :, :, HD:HD + 1], 1.0),
            reads=["vtok"], writes=["vtok"])

        for w_ap, tile, name in [(wqT, wq_s, "wq"), (wkT, wk_s, "wk"),
                                 (wvT, wv_s, "wv"), (woT, wo_s, "wo")]:
            p.emit(GP, lambda E, w=w_ap, t=tile: E.dma_start(
                out=t.rearrange("q (c d) -> q c d", c=DC),
                in_=w.rearrange("(c q) d -> q c d", q=128)),
                writes=[name], sem="LDG", inc=16)
        p.emit(SY, lambda E: E.dma_start(out=bq_s[:, 0:DC], in_=bq8), writes=["bq"], inc=16)
        p.emit(SY, lambda E: E.dma_start(out=bk_s[:, 0:DC], in_=bk8), writes=["bk"], inc=16)
        p.emit(GP, lambda E: E.dma_start(out=bvr_s, in_=bv_row), writes=["bv"], sem="LDG", inc=16)
        p.emit(GP, lambda E: E.dma_start(out=bor_s, in_=bo_row), writes=["bo"], sem="LDG", inc=16)
        p.emit(SY, lambda E: E.dma_start(out=sm_s[:, 0:NCORE], in_=smask),
               writes=["sm"], inc=16)

        def ln_stats(xb, xname, sffn=False):
            xs = x_sl[xb]
            for c in range(NCH):
                p.emit(VE, lambda E, c=c, xs=xs: E.bn_stats(
                    bn_s[:, c * 12:c * 12 + 6], xs[:, c * D:c * D + 512]),
                    reads=[xname], writes=[f"bn{c}"])
                p.emit(VE, lambda E, c=c, xs=xs: E.bn_stats(
                    bn_s[:, c * 12 + 6:c * 12 + 12], xs[:, c * D + 512:(c + 1) * D]),
                    reads=[xname, f"bn{c}"], writes=[f"bn{c}"])
                p.emit(VE, lambda E, c=c: E.bn_aggr(
                    mv_s[:, 2 * c:2 * c + 2], bn_s[:, c * 12:(c + 1) * 12]),
                    reads=[f"bn{c}"], writes=[f"mv{c}"])
            mvr = [f"mv{c}" for c in range(NCH)]
            p.emit(SC, lambda E: E.activation(
                lnt_s[:, 0:NCH], mv_s[:, 1:2 * NCH:2], AF.Ln, bias=lneps_c[:, 0:1], scale=1.0),
                reads=mvr + ["lneps"], writes=["lnt0"])
            p.emit(SC, lambda E: E.activation(
                rstd_s[:, 0:NCH], lnt_s[:, 0:NCH], AF.Exp, scale=-0.5),
                reads=["lnt0"], writes=["rstd"])
            p.emit(VE, lambda E: E.scalar_tensor_tensor(
                out=cA_s[:, 0:NCH], in0=mv_s[:, 0:2 * NCH:2], scalar=-1.0,
                in1=rstd_s[:, 0:NCH], op0=ALU.mult, op1=ALU.mult),
                reads=mvr + ["rstd"], writes=["cA"])
            if sffn:
                p.emit(SC, lambda E: E.activation(
                    lnt_s[:, NCH:2 * NCH], mv_s[:, 1:2 * NCH:2], AF.Ln, scale=1.0),
                    reads=mvr, writes=["lnt1"])
                p.emit(VE, lambda E: E.tensor_sub(
                    lnt_s[:, 2 * NCH:3 * NCH], lnt_s[:, NCH:2 * NCH], lnt_s[:, 0:NCH]),
                    reads=["lnt0", "lnt1"], writes=["lnt2"])
                p.emit(SC, lambda E: E.activation(
                    lnt_s[:, 3 * NCH:4 * NCH], lnt_s[:, 2 * NCH:3 * NCH], AF.Exp,
                    scale=0.5),
                    reads=["lnt2"], writes=["lnt3"])
                p.emit(VE, lambda E: E.tensor_scalar(
                    out=lnt_s[:, 3 * NCH:4 * NCH], in0=lnt_s[:, 3 * NCH:4 * NCH],
                    scalar1=float(RMSEPS), scalar2=None, op0=ALU.add),
                    reads=["lnt3"], writes=["lnt3"])
                p.emit(VE, lambda E: E.reciprocal(
                    lnt_s[:, 2 * NCH:3 * NCH], lnt_s[:, 3 * NCH:4 * NCH]),
                    reads=["lnt3"], writes=["lnt2"])
                p.emit(VE, lambda E: E.tensor_mul(
                    sffn_s[:, 0:NCH], rstd_s[:, 0:NCH], lnt_s[:, 2 * NCH:3 * NCH]),
                    reads=["rstd", "lnt2"], writes=["sffn"])
                p.emit(VE, lambda E: E.scalar_tensor_tensor(
                    out=cB_s[:, 0:NCH], in0=mv_s[:, 0:2 * NCH:2], scalar=-1.0,
                    in1=sffn_s[:, 0:NCH], op0=ALU.mult, op1=ALU.mult),
                    reads=mvr + ["sffn"], writes=["cB"])

        def xn_apply(xb, xname, scale_t, scale_name, shift_t, shift_name):
            xs = x_sl[xb]
            for c in range(NCH):
                p.emit(SC, lambda E, c=c, xs=xs: E.activation(
                    xn_tok[:, c * D:(c + 1) * D], xs[:, c * D:(c + 1) * D], AF.Identity,
                    bias=shift_t[:, c:c + 1], scale=scale_t[:, c:c + 1]),
                    reads=[xname, scale_name, shift_name], writes=["xn_tok"])

        def transpose_tok_to_ft(src, src_name, dst, dst_name):
            for c in range(NCH):
                fns = []
                for jm in range(DC):
                    def tr(E, c=c, jm=jm):
                        return E.transpose(
                            P7[:, jm * 128:(jm + 1) * 128],
                            src[:, c * D + jm * 128:c * D + (jm + 1) * 128], ident)
                    fns.append(tr)
                p.emit(TE, fns, reads=[src_name, "ident"], writes=["P7"])
                ov = dst.rearrange("q (jm t) -> q jm t", jm=DC)[:, :, c * 128:(c + 1) * 128]
                p.emit(SC, lambda E, ov=ov: E.copy(
                    ov, P7[:].rearrange("q (jm t) -> q jm t", jm=DC)),
                    reads=["P7"], writes=[dst_name])
                bk_done(7)

        def transpose_ft_to_tok(src, src_name, dst, dst_name):
            for c in range(NCH):
                fns = []
                for jd in range(DC):
                    def tr(E, c=c, jd=jd):
                        return E.transpose(
                            P7[:, jd * 128:(jd + 1) * 128],
                            src[:, jd * TILE + c * 128:jd * TILE + (c + 1) * 128], ident)
                    fns.append(tr)
                p.emit(TE, fns, reads=[src_name, "ident"], writes=["P7"])
                p.emit(VE, lambda E, c=c: E.tensor_copy(
                    dst[:, c * D:(c + 1) * D], P7[:]),
                    reads=["P7"], writes=[dst_name])
                bk_done(7)

        def proj_feature(wtile, wname, btile, bname, dst, dst_name):
            for jd in range(DC):
                bank = jd % 4
                pp = jd % 2
                fns = []
                for jm in range(DC):
                    def mm(E, jd=jd, jm=jm, bank=bank, st=(lead(bank) if (jm == 0) else False)):
                        return E.matmul(
                            PS[bank][:],
                            wtile[:, jm * D + jd * 128: jm * D + (jd + 1) * 128],
                            xnT[:, jm * TILE:(jm + 1) * TILE],
                            start=st, stop=(jm == DC - 1))
                    fns.append(mm)
                p.emit(TE, fns, reads=[wname, "xnT"], writes=[f"PB{bank}"])
                dv = dst[:, jd * TILE:(jd + 1) * TILE]
                p.emit(SC, lambda E, jd=jd, bank=bank, pp=pp: E.activation(
                    ph_e[pp][:, 0:512], PS[bank][:], AF.Exp, bias=btile[:, jd:jd + 1],
                    scale=1.0),
                    reads=[f"PB{bank}", bname], writes=[f"ph_e{pp}"])
                p.emit(SC, lambda E, jd=jd, bank=bank, pp=pp: E.activation(
                    ph_r[pp][:, 0:512], PS[bank][:], AF.Relu, bias=btile[:, jd:jd + 1],
                    scale=1.0),
                    reads=[f"PB{bank}", bname], writes=[f"ph_r{pp}"])
                p.emit(VE, lambda E, dv=dv, pp=pp: E.scalar_tensor_tensor(
                    out=dv, in0=ph_e[pp][:, 0:512], scalar=1.0, in1=ph_r[pp][:, 0:512],
                    op0=ALU.min, op1=ALU.add),
                    reads=[f"ph_e{pp}", f"ph_r{pp}"], writes=[dst_name])
                bk_done(bank)
                tern_tick()

        def proj_token(wtile, wname, brow, bname, writer, banks):
            for c in range(NCH):
                for half in range(2):
                    bank = banks[(c * 2 + half) % len(banks)]
                    fns = []
                    for jm in range(DC):
                        def mm(E, c=c, half=half, jm=jm, bank=bank, st=(lead(bank) if (jm == 0) else False)):
                            return E.matmul(
                                PS[bank][:],
                                xnT[:, jm * TILE + c * 128: jm * TILE + (c + 1) * 128],
                                wtile[:, jm * D + half * 512: jm * D + (half + 1) * 512],
                                start=st, stop=False)
                        fns.append(mm)

                    def mmb(E, half=half, bank=bank):
                        return E.matmul(
                            PS[bank][:], ones_row, brow[0:1, half * 512:(half + 1) * 512],
                            start=False, stop=True)
                    fns.append(mmb)
                    p.emit(TE, fns, reads=[wname, "xnT", bname, "ones_row"],
                           writes=[f"PB{bank}"])
                    writer(c, half, bank)
                    bk_done(bank)
                    tern_tick()

        SDSEG = [(4, 0, 6), (5, 6, 11), (6, 11, 16)]

        def state_fold():
            for bank, h0, h1 in SDSEG:
                p.emit(VE, lambda E, bank=bank, h0=h0, h1=h1: E.tensor_tensor(
                    st_f[:, h0 * EV:h1 * EV], st_f[:, h0 * EV:h1 * EV],
                    PS[bank][0:64, 0:(h1 - h0) * EV], ALU.add),
                    reads=[f"PB{bank}", "st_f"], writes=["st_f"])
                bk_done(bank)
            p.emit(VE, lambda E: E.tensor_copy(
                st_b[0:64, :].rearrange("q (h e) -> q h e", h=H)[:, 0:H:2, :],
                st_f[:, 0:H * EV].rearrange("q (h e) -> q h e", h=H)[:, 0:H:2, :]),
                reads=["st_f"], writes=["st_b"])
            p.emit(VE, lambda E: E.tensor_copy(
                st_b[64:128, :].rearrange("q (h e) -> q h e", h=H)[:, 1:H:2, :],
                st_f[:, 0:H * EV].rearrange("q (h e) -> q h e", h=H)[:, 1:H:2, :]),
                reads=["st_f"], writes=["st_b"])

        scanref = {}

        def scan_pre(ci):
            qT = scanref["qT"]; kT = scanref["kT"]
            ktok_c = scanref["ktok_c"][ci % 2]
            am_s = scanref["am_s"]
            fns = []
            for jd in range(DC):
                def tr(E, jd=jd, ci=ci):
                    return E.transpose(
                        P7[:, jd * 128:(jd + 1) * 128],
                        kT[:, jd * TILE + ci * 128:jd * TILE + (ci + 1) * 128], ident)
                fns.append(tr)
            p.emit(TE, fns, reads=["kT", "ident"], writes=["P7"])
            p.emit(SC, lambda E, ktok_c=ktok_c: E.copy(ktok_c[:, 0:D], P7[:]),
                   reads=["P7"], writes=[f"ktokc{ci % 2}"])
            bk_done(7)
            fns = []
            wr = set()
            for h in range(H):
                bank = h % 4
                col = (h // 4) * 128

                def m1(E, h=h, bank=bank, col=col, ci=ci,
                       st=(lead(bank) if (h // 4 == 0) else False)):
                    return E.matmul(
                        PS[bank][:, col:col + 128],
                        kT[(h % 2) * 64:(h % 2) * 64 + 64,
                           (h // 2) * TILE + ci * 128:(h // 2) * TILE + (ci + 1) * 128],
                        qT[(h % 2) * 64:(h % 2) * 64 + 64,
                           (h // 2) * TILE + ci * 128:(h // 2) * TILE + (ci + 1) * 128],
                        start=st, stop=(h >= H - 4))
                fns.append(m1)
                wr.add(f"PB{bank}")
            p.emit(TE, fns, reads=["kT", "qT"], writes=sorted(wr))
            for bank in range(4):
                ov = am_s.rearrange("q (g b c) -> q g b c", g=4, b=4)[:, :, bank, :]
                p.emit(VE, lambda E, bank=bank, ov=ov: E.tensor_tensor(
                    ov, PS[bank][:].rearrange("q (g c) -> q g c", g=4),
                    mask8.rearrange("q (g c) -> q g c", g=4), ALU.mult),
                    reads=[f"PB{bank}", "mask8"], writes=[f"am{bank}"])
                bk_done(bank)

        def scan_groups(ci):
            qT = scanref["qT"]; am_s = scanref["am_s"]; numsave = scanref["numsave"]
            amr = [f"am{b}" for b in range(4)]
            for g in range(4):
                nb = 4 + (g % 2)
                fns = []
                for i in range(4):
                    h = g * 4 + i

                    def m2(E, h=h, i=i, nb=nb, ci=ci, st=(lead(nb) if (i == 0) else False)):
                        return E.matmul(
                            PS[nb][:, i * EV:i * EV + EV],
                            am_s[:, h * 128:(h + 1) * 128],
                            vtok[:, ci * H * EV + h * EV: ci * H * EV + (h + 1) * EV],
                            start=st, stop=False)

                    def m3(E, h=h, i=i, nb=nb, ci=ci):
                        return E.matmul(
                            PS[nb][:, i * EV:i * EV + EV],
                            qT[:, (h // 2) * TILE + ci * 128:(h // 2) * TILE + (ci + 1) * 128],
                            st_b[:, h * EV:(h + 1) * EV],
                            start=False, stop=(i == 3))
                    fns += [m2, m3]
                p.emit(TE, fns, reads=amr + ["vtok", "qT", "st_b"], writes=[f"PB{nb}"])
                p.emit(SC, lambda E, nb=nb, g=g, ci=ci: E.copy(
                    numsave[:, ci * H * EV + g * 4 * EV: ci * H * EV + (g + 1) * 4 * EV],
                    PS[nb][:, 0:4 * EV]),
                    reads=[f"PB{nb}"], writes=[f"nsv{ci}_{g}"])
                bk_done(nb)
                tern_tick()

        def scan_tail(ci):
            ktok_c = scanref["ktok_c"][ci % 2]
            numsave = scanref["numsave"]
            fns = []
            wr = set()
            for bank, h0, h1 in SDSEG:
                for i in range(h1 - h0):
                    h = h0 + i

                    def m4(E, h=h, i=i, bank=bank, ci=ci, ktok_c=ktok_c,
                           last=(i == h1 - h0 - 1), st=(lead(bank) if (i == 0) else False)):
                        return E.matmul(
                            PS[bank][0:64, i * EV:i * EV + EV],
                            ktok_c[:, h * HD:(h + 1) * HD],
                            vtok[:, ci * H * EV + h * EV: ci * H * EV + (h + 1) * EV],
                            start=st, stop=last)
                    fns.append(m4)
                    wr.add(f"PB{bank}")
            p.emit(TE, fns, reads=[f"ktokc{ci % 2}", "vtok"], writes=sorted(wr))
            state_fold()
            if ci == NCH - 1:
                ti = scanref["ti"]
                p.emit(SY, lambda E, ti=ti: E.dma_start(
                    out=num_d[ti * TILE:(ti + 1) * TILE, :].rearrange(
                        "(c q) e -> q c e", q=128),
                    in_=numsave.rearrange("q (c e) -> q c e", c=NCH)),
                    reads=[f"nsv{c}_{g}" for c in range(NCH) for g in range(4)],
                    writes=["num_dd"], sem="LD", inc=16)

        tern_chunks = [("up", j) for j in range(64)] + [("dn", j) for j in range(64)]

        def tern_load(kind, j, b):
            if kind == "up":
                src = wupT[(j // 8) * 128:(j // 8 + 1) * 128,
                           (j % 8) * 512:(j % 8 + 1) * 512]
            else:
                src = wdnT[(j // 2) * 128:(j // 2 + 1) * 128,
                           (j % 2) * 512:(j % 2 + 1) * 512]
            p.emit(GP, lambda E, s=src, b=b: E.dma_start(out=twf[b], in_=s),
                   writes=[f"twf{b}"], sem="LDG", inc=16)

        def tern_abs(kind, j, b):
            col = 0 if kind == "up" else 1
            p.emit(SC, lambda E, b=b: E.activation(
                tg1, twf[b], AF.Abs, accum_out=tred[:, 0:1]),
                reads=[f"twf{b}"], writes=["tred", "tg1"])
            p.emit(VE, lambda E, col=col: E.tensor_tensor(
                tpart[:, col:col + 1], tpart[:, col:col + 1], tred[:, 0:1], ALU.add),
                reads=["tred", "tpart"], writes=["tpart"])

        def tern_finalize():
            p.emit(TE, lambda E, st=lead(6): E.matmul(
                PS[6][0:1, 0:2], ones_cf[:, 0:1], tpart[:, 0:2], start=st, stop=True),
                reads=["tpart", "ones_cf"], writes=["PB6"])
            p.emit(SC, lambda E: E.activation(
                trow_s[0:1, 0:2], PS[6][0:1, 0:2], AF.Copy,
                scale=float(0.5 / (D * DFF))),
                reads=["PB6"], writes=["trow"])
            bk_done(6)
            p.emit(TE, lambda E, st=lead(6): E.matmul(
                PS[6][:, 0:2], ones_rf[0:1, 0:128], trow_s[0:1, 0:2],
                start=st, stop=True),
                reads=["trow", "ones_rf"], writes=["PB6"])
            p.emit(VE, lambda E: E.tensor_copy(thr_s[:, 0:2], PS[6][:, 0:2]),
                   reads=["PB6"], writes=["thr"])
            p.emit(VE, lambda E: E.tensor_scalar(
                out=nthr_s[:, 0:2], in0=thr_s[:, 0:2], scalar1=-1.0, scalar2=None,
                op0=ALU.mult),
                reads=["thr"], writes=["nthr"])
            bk_done(6)

        def tern_quant(kind, j, b):
            col = 0 if kind == "up" else 1
            p.emit(VE, lambda E, b=b, col=col: E.tensor_scalar(
                out=tg1, in0=twf[b], scalar1=nthr_s[:, col:col + 1], scalar2=None,
                op0=ALU.is_lt),
                reads=[f"twf{b}", "nthr"], writes=["tg1"])
            p.emit(VE, lambda E, b=b, col=col: E.scalar_tensor_tensor(
                out=tout[b], in0=twf[b], scalar=thr_s[:, col:col + 1], in1=tg1,
                op0=ALU.is_gt, op1=ALU.subtract),
                reads=[f"twf{b}", "thr", "tg1"], writes=[f"tout{b}"])
            if kind == "up":
                dst = wup_t[(j // 8) * 128:(j // 8 + 1) * 128,
                            (j % 8) * 512:(j % 8 + 1) * 512]
                scr = "scr_up"
            else:
                dst = wdn_t[(j // 2) * 128:(j // 2 + 1) * 128,
                            (j % 2) * 512:(j % 2 + 1) * 512]
                scr = "scr_dn"
            p.emit(SY, lambda E, d=dst, b=b: E.dma_start(out=d, in_=tout[b]),
                   reads=[f"tout{b}"], writes=[scr], sem="LD", inc=16)

        tst = {"a": 0, "b": 0, "fin": False}
        quant_done = set()

        def tern_step(n):
            for _ in range(n):
                if tst["a"] < len(tern_chunks):
                    k, j = tern_chunks[tst["a"]]
                    tern_load(k, j, tst["b"])
                    tern_abs(k, j, tst["b"])
                    tst["a"] += 1
                    tst["b"] ^= 1

        def tern_abs_flush():
            while tst["a"] < len(tern_chunks):
                tern_step(1)
            if not tst["fin"]:
                tern_finalize()
                tst["fin"] = True

        def tern_quant_chunks(chunks):
            for k, j in chunks:
                if (k, j) in quant_done:
                    continue
                quant_done.add((k, j))
                tern_load(k, j, tst["b"])
                tern_quant(k, j, tst["b"])
                tst["b"] ^= 1

        def tern_tick(n=2):
            if tst["a"] < len(tern_chunks):
                tern_step(n)
            elif not tst["fin"]:
                tern_finalize()
                tst["fin"] = True
            else:
                tern_quant_step(n)

        qsched = {"i": 0}

        def tern_quant_step(n):
            c = 0
            while qsched["i"] < len(tern_chunks) and c < n:
                k, j = tern_chunks[qsched["i"]]
                qsched["i"] += 1
                if (k, j) in quant_done:
                    continue
                tern_quant_chunks([(k, j)])
                c += 1

        p.barrier()
        cur[0] = var_base
        qT = take(4096)
        kT = take(4096)
        ktok_c = [take(1024) for _ in range(2)]
        am_s = take(2048)
        numsave = take(4352)
        scanref["qT"] = qT
        scanref["kT"] = kT
        scanref["ktok_c"] = ktok_c
        scanref["am_s"] = am_s
        scanref["numsave"] = numsave
        def a1_load_stats(ti):
            xb = ti % 2
            xname = f"x{xb}"
            p.emit(SY, lambda E, ti=ti, xb=xb: E.dma_start(
                out=x_sl[xb].rearrange("q (c d) -> q c d", c=NCH),
                in_=x_own[ti * TILE:(ti + 1) * TILE, :].rearrange("(c q) d -> q c d", q=128)),
                writes=[xname], inc=16)
            ln_stats(xb, xname)

        a1_load_stats(0)
        for ti in range(nown_tiles):
            xb = ti % 2
            xname = f"x{xb}"
            xn_apply(xb, xname, rstd_s, "rstd", cA_s, "cA")
            transpose_tok_to_ft(xn_tok, "xn_tok", xnT, "xnT")
            proj_feature(wq_s, "wq", bq_s, "bq", qT, "qT")
            proj_feature(wk_s, "wk", bk_s, "bk", kT, "kT")
            scanref["ti"] = ti
            p.emit(SY, lambda E, ti=ti: E.dma_start(
                out=qT_d[ti * 128:(ti + 1) * 128, :], in_=qT),
                reads=["qT"], writes=["qT_dd"], sem="LD", inc=16)
            if ti + 1 < nown_tiles:
                a1_load_stats(ti + 1)

            def v_writer(c, half, bank):
                ov = vtok.rearrange("q (c h e) -> q c h e", c=NCH, h=H)[
                    :, c, half * 8:(half + 1) * 8, 0:HD]
                p.emit(SC, lambda E, ov=ov, bank=bank: E.copy(
                    ov, PS[bank][:].rearrange("q (h e) -> q h e", h=8)),
                    reads=[f"PB{bank}"], writes=["vtok"])
            proj_token(wv_s, "wv", bvr_s, "bv", v_writer, [0, 1, 2, 3])

            scan_pre(0)
            for ci in range(NCH):
                scan_groups(ci)
                if ci < NCH - 1:
                    scan_pre(ci + 1)
                scan_tail(ci)

        tern_abs_flush()
        tern_quant_step(len(tern_chunks))

        p.emit(SY, lambda E: E.dma_start(out=st_loc, in_=st_f[:, 0:H * EV]),
               reads=["st_f"], writes=["st_loc_d"], sem="LD", inc=16)
        p.emit(GP, lambda E: E.collective_compute(
            "AllGather", ALU.bypass,
            replica_groups=[list(range(NCORE))],
            ins=[st_loc.opt()], outs=[st_all.opt()]),
            reads=["st_loc_d"], writes=["st_all_d"], sem="CC", inc=1)
        p.barrier()
        cur[0] = wts_top
        hT = take(16384)
        fwu = [take(4096) for _ in range(2)]
        fwd = [take(2048) for _ in range(2)]
        hsq = [take(512) for _ in range(2)]

        def b_load_stats(ti):
            xb = ti % 2
            xname = f"x{xb}"
            p.emit(SY, lambda E, ti=ti, xb=xb: E.dma_start(
                out=x_sl[xb].rearrange("q (c d) -> q c d", c=NCH),
                in_=x_own[ti * TILE:(ti + 1) * TILE, :].rearrange("(c q) d -> q c d", q=128)),
                writes=[xname], inc=16)
            ln_stats(xb, xname, sffn=True)

        b_load_stats(0)
        for ti in range(nown_tiles):
            xb = ti % 2
            xname = f"x{xb}"
            xn_apply(xb, xname, sffn_s, "sffn", cB_s, "cB")
            transpose_tok_to_ft(xn_tok, "xn_tok", xnT, "xnT")

            for fs in range(DFF // 512):
                fb = fs % 2
                tern_quant_chunks([("up", jm * 8 + fs) for jm in range(DC)])
                src = wup_t.rearrange("(jm q) f -> q jm f", q=128)[:, :, fs * 512:(fs + 1) * 512]
                p.emit(GP, lambda E, src=src, fb=fb: E.dma_start(
                    out=fwu[fb].rearrange("q (jm f) -> q jm f", jm=DC), in_=src),
                    reads=["scr_up"], writes=[f"fwu{fb}"], sem="LDG", inc=16)
                for lf in range(4):
                    jf = fs * 4 + lf
                    bank = jf % 4
                    fns = []
                    for jm in range(DC):
                        def mm(E, jm=jm, lf=lf, fb=fb, bank=bank, st=(lead(bank) if (jm == 0) else False)):
                            return E.matmul(
                                PS[bank][:],
                                fwu[fb][:, jm * 512 + lf * 128: jm * 512 + (lf + 1) * 128],
                                xnT[:, jm * TILE:(jm + 1) * TILE],
                                start=st, stop=(jm == DC - 1))
                        fns.append(mm)
                    p.emit(TE, fns, reads=[f"fwu{fb}", "xnT"], writes=[f"PB{bank}"])
                    if jf > 0:
                        pj, phb = jf - 1, (jf - 1) % 2
                        p.emit(TE, lambda E, pj=pj, phb=phb,
                               st=(lead(6) if pj == 0 else False): E.matmul(
                            PS[6][0:1, :], ones_col, hsq[phb],
                            start=st, stop=False),
                            reads=[f"hsq{phb}", "ones_col"], writes=["PB6"])
                    p.emit(SC, lambda E, jf=jf, bank=bank: E.activation(
                        hT[:, jf * TILE:(jf + 1) * TILE], PS[bank][:], AF.Gelu),
                        reads=[f"PB{bank}"], writes=[f"hTp{jf % 2}"])
                    hb = jf % 2
                    p.emit(VE, lambda E, jf=jf, hb=hb: E.tensor_mul(
                        hsq[hb], hT[:, jf * TILE:(jf + 1) * TILE],
                        hT[:, jf * TILE:(jf + 1) * TILE]),
                        reads=[f"hTp{jf % 2}"], writes=[f"hsq{hb}"])
                    bk_done(bank)
            p.emit(TE, lambda E: E.matmul(
                PS[6][0:1, :], ones_col, hsq[(FC - 1) % 2],
                start=False, stop=True),
                reads=[f"hsq{(FC - 1) % 2}", "ones_col"], writes=["PB6"])
            if ti + 1 < nown_tiles:
                b_load_stats(ti + 1)
            p.emit(SC, lambda E: E.activation(
                s2row[0:1, 0:TILE], PS[6][0:1, :], AF.Ln, scale=float(1.0 / DFF)),
                reads=["PB6"], writes=["s2row"])
            bk_done(6)
            p.emit(SC, lambda E: E.activation(
                s2row[0:1, 0:TILE], s2row[0:1, 0:TILE], AF.Exp, scale=0.5),
                reads=["s2row"], writes=["s2row"])
            p.emit(VE, lambda E: E.tensor_scalar(
                out=s2row[0:1, 0:TILE], in0=s2row[0:1, 0:TILE], scalar1=float(RMSEPS),
                scalar2=None, op0=ALU.add),
                reads=["s2row"], writes=["s2row"])
            p.emit(VE, lambda E: E.reciprocal(s2row[0:1, 0:TILE], s2row[0:1, 0:TILE]),
                   reads=["s2row"], writes=["s2row"])
            fns = []
            for c in range(NCH):
                def mm(E, c=c, st=(lead(6) if (c == 0) else False)):
                    return E.matmul(
                        PS[6][:, c:c + 1], s2row[0:1, c * 128:(c + 1) * 128],
                        ones_1f[0:1, 0:1], start=st, stop=(c == NCH - 1))
                fns.append(mm)
            p.emit(TE, fns, reads=["s2row", "ones_1f"], writes=["PB6"])
            p.emit(VE, lambda E: E.tensor_copy(s2col[:, 0:NCH], PS[6][:, 0:NCH]),
                   reads=["PB6"], writes=["s2col"])
            bk_done(6)

            for half in range(2):
                for fs in range(DFF // 512):
                    fb = fs % 2
                    tern_quant_chunks([("dn", (fs * 4 + r) * 2 + half) for r in range(4)])
                    src = wdn_t.rearrange("(jf q) d -> q jf d", q=128)[
                        :, fs * 4:(fs + 1) * 4, half * 512:(half + 1) * 512]
                    p.emit(GP, lambda E, src=src, fb=fb: E.dma_start(
                        out=fwd[fb].rearrange("q (jf d) -> q jf d", jf=4), in_=src),
                        reads=["scr_dn"], writes=[f"fwd{fb}"], sem="LDG", inc=16)
                    fns = []
                    for lf in range(4):
                        jf = fs * 4 + lf
                        for c in range(NCH):
                            def mm(E, jf=jf, lf=lf, c=c, fb=fb, st=(lead(c) if (jf == 0) else False)):
                                return E.matmul(
                                    PS[c][:],
                                    hT[:, jf * TILE + c * 128: jf * TILE + (c + 1) * 128],
                                    fwd[fb][:, lf * 512:(lf + 1) * 512],
                                    start=st, stop=(jf == FC - 1))
                            fns.append(mm)
                    p.emit(TE, fns, reads=[f"fwd{fb}", "hTp0", "hTp1"],
                           writes=[f"PB{c}" for c in range(NCH)])
                for c in range(NCH):
                    dv = out_sl[:, c * D + half * 512: c * D + (half + 1) * 512]
                    pv = x_sl[xb][:, c * D + half * 512: c * D + (half + 1) * 512]
                    p.emit(VE, lambda E, dv=dv, pv=pv, c=c: E.scalar_tensor_tensor(
                        out=dv, in0=PS[c][:], scalar=s2col[:, c:c + 1], in1=pv,
                        op0=ALU.mult, op1=ALU.add),
                        reads=[f"PB{c}", "s2col", xname], writes=["out_sl"])
                    bk_done(c)
            p.emit(SY, lambda E, ti=ti: E.dma_start(
                out=out[ti * TILE:(ti + 1) * TILE, :].rearrange("(c q) d -> q c d", q=128),
                in_=out_sl.rearrange("q (c d) -> q c d", c=NCH)),
                reads=["out_sl"], writes=[f"fdram{ti}"], sem="ST", inc=16)

        p.emit(VE, lambda E: E.memset(st_f[:, 0:H * EV], 0.0),
               reads=["st_f"], writes=["st_f"])
        for r in range(NCORE):
            p.emit(SY, lambda E, r=r: E.dma_start(
                out=shard_s[:, 0:H * EV], in_=st_all[r * 64:(r + 1) * 64, :]),
                reads=["st_all_d"], writes=["shard"], inc=16)
            p.emit(VE, lambda E, r=r: E.scalar_tensor_tensor(
                out=st_f[:, 0:H * EV], in0=shard_s[:, 0:H * EV],
                scalar=sm_s[0:64, r:r + 1], in1=st_f[:, 0:H * EV],
                op0=ALU.mult, op1=ALU.add),
                reads=["shard", "sm", "st_f"], writes=["st_f"])
        p.emit(VE, lambda E: E.memset(st_b, 0.0), reads=["st_b"], writes=["st_b"])
        p.emit(VE, lambda E: E.tensor_copy(
            st_b[0:64, :].rearrange("q (h e) -> q h e", h=H)[:, 0:H:2, :],
            st_f[:, 0:H * EV].rearrange("q (h e) -> q h e", h=H)[:, 0:H:2, :]),
            reads=["st_f", "st_b"], writes=["st_b"])
        p.emit(VE, lambda E: E.tensor_copy(
            st_b[64:128, :].rearrange("q (h e) -> q h e", h=H)[:, 1:H:2, :],
            st_f[:, 0:H * EV].rearrange("q (h e) -> q h e", h=H)[:, 1:H:2, :]),
            reads=["st_f", "st_b"], writes=["st_b"])

        p.barrier()
        cur[0] = wts_top
        qT2b = [take(4096) for _ in range(2)]
        num2b = [take(4352) for _ in range(2)]
        part2b = [take(8192, F32) for _ in range(2)]
        o_tok2 = [take(1024) for _ in range(2)]
        oT = take(4096)
        a2scr = take(544, F32)

        def a2_loads(ti):
            tb = ti % 2
            p.emit(SY, lambda E, ti=ti, tb=tb: E.dma_start(
                out=qT2b[tb], in_=qT_d[ti * 128:(ti + 1) * 128, :]),
                reads=["qT_dd"], writes=[f"qT2{tb}"], inc=16)
            p.emit(SY, lambda E, ti=ti, tb=tb: E.dma_start(
                out=num2b[tb].rearrange("q (c e) -> q c e", c=NCH),
                in_=num_d[ti * TILE:(ti + 1) * TILE, :].rearrange("(c q) e -> q c e", q=128)),
                reads=["num_dd"], writes=[f"num2{tb}"], inc=16)
            p.emit(SY, lambda E, ti=ti, tb=tb: E.dma_start(
                out=part2b[tb].rearrange("q (c d) -> q c d", c=NCH),
                in_=out[ti * TILE:(ti + 1) * TILE, :].rearrange("(c q) d -> q c d", q=128)),
                reads=[f"fdram{ti}"], writes=[f"part2{tb}"], inc=16)

        a2_loads(0)
        for ti in range(nown_tiles):
            tb = ti % 2
            qT2, num2, part2 = qT2b[tb], num2b[tb], part2b[tb]
            if ti + 1 < nown_tiles:
                a2_loads(ti + 1)
            def a2_otr(ci):
                o_tok = o_tok2[ci % 2]
                fns = []
                for jd in range(DC):
                    def tr(E, jd=jd, o_tok=o_tok):
                        return E.transpose(
                            P7[:, jd * 128:(jd + 1) * 128],
                            o_tok[:, jd * 128:(jd + 1) * 128], ident)
                    fns.append(tr)
                p.emit(TE, fns, reads=[f"o_tok{ci % 2}", "ident"], writes=["P7"])
                ov = oT.rearrange("q (jd t) -> q jd t", jd=DC)[:, :, ci * 128:(ci + 1) * 128]
                p.emit(SC, lambda E, ov=ov: E.copy(
                    ov, P7[:].rearrange("q (jd t) -> q jd t", jd=DC)),
                    reads=["P7"], writes=["oT"])
                bk_done(7)

            for ci in range(NCH):
                o_tok = o_tok2[ci % 2]
                for g in range(4):
                    nb = g
                    fns = []
                    for i in range(4):
                        h = g * 4 + i

                        def m3c(E, h=h, i=i, nb=nb, ci=ci,
                                st=(lead(nb) if (i == 0) else False)):
                            return E.matmul(
                                PS[nb][:, i * EV:i * EV + EV],
                                qT2[:, (h // 2) * TILE + ci * 128:(h // 2) * TILE + (ci + 1) * 128],
                                st_b[:, h * EV:(h + 1) * EV],
                                start=st, stop=(i == 3))
                        fns.append(m3c)
                    p.emit(TE, fns, reads=[f"qT2{tb}", "st_b"], writes=[f"PB{nb}"])
                    nsv = num2[:, ci * H * EV + g * 4 * EV: ci * H * EV + (g + 1) * 4 * EV]
                    p.emit(VE, lambda E, nb=nb, nsv=nsv: E.tensor_tensor(
                        a2scr[:, 0:4 * EV], PS[nb][:, 0:4 * EV], nsv, ALU.add),
                        reads=[f"PB{nb}", f"num2{tb}"], writes=["a2scr"])
                    bk_done(nb)
                    p.emit(VE, lambda E: E.tensor_scalar(
                        out=rcp_s[:, 0:4],
                        in0=a2scr[:, 0:4 * EV].rearrange("q (i e) -> q i e", i=4)[:, :, HD],
                        scalar1=float(ATTEPS), scalar2=None, op0=ALU.add),
                        reads=["a2scr"], writes=["rcp"])
                    p.emit(VE, lambda E: E.reciprocal(rcp_s[:, 0:4], rcp_s[:, 0:4]),
                           reads=["rcp"], writes=["rcp"])
                    ov = o_tok[:, g * 256:(g + 1) * 256].rearrange("q (i e) -> q i e", i=4)
                    p.emit(VE, lambda E, ov=ov, ci=ci: E.tensor_tensor(
                        ov, a2scr[:, 0:4 * EV].rearrange("q (i e) -> q i e", i=4)[:, :, 0:HD],
                        rcp_s[:, 0:4].rearrange("q i -> q i ()").broadcast_to([128, 4, HD]),
                        ALU.mult),
                        reads=["a2scr", "rcp"], writes=[f"o_tok{ci % 2}"])
                if ci > 0:
                    a2_otr(ci - 1)
            a2_otr(NCH - 1)
            for c in range(NCH):
                for half in range(2):
                    bank = half
                    fns = []
                    for jd in range(DC):
                        def mm(E, c=c, half=half, jd=jd, bank=bank,
                               st=(lead(bank) if (jd == 0) else False)):
                            return E.matmul(
                                PS[bank][:],
                                oT[:, jd * TILE + c * 128: jd * TILE + (c + 1) * 128],
                                wo_s[:, jd * D + half * 512: jd * D + (half + 1) * 512],
                                start=st, stop=False)
                        fns.append(mm)

                    def mmb(E, half=half, bank=bank):
                        return E.matmul(
                            PS[bank][:], ones_row, bor_s[0:1, half * 512:(half + 1) * 512],
                            start=False, stop=True)
                    fns.append(mmb)
                    p.emit(TE, fns, reads=["oT", "wo", "bo", "ones_row"], writes=[f"PB{bank}"])
                    dv = out_sl[:, c * D + half * 512: c * D + (half + 1) * 512]
                    pv = part2[:, c * D + half * 512: c * D + (half + 1) * 512]
                    p.emit(VE, lambda E, dv=dv, pv=pv, bank=bank: E.tensor_tensor(
                        dv, PS[bank][:], pv, ALU.add),
                        reads=[f"PB{bank}", f"part2{tb}"], writes=["out_sl"])
                    bk_done(bank)
            p.emit(SY, lambda E, ti=ti: E.dma_start(
                out=out[ti * TILE:(ti + 1) * TILE, :].rearrange("(c q) d -> q c d", q=128),
                in_=out_sl.rearrange("q (c d) -> q c d", c=NCH)),
                reads=["out_sl"], writes=[f"adram{ti}"], sem="ST", inc=16)

        total_st = p.cnt["ST0"]
        h_st = p.semh["ST0"]
        p.ops["sync"].append(lambda E, h=h_st, v=total_st: E.wait_ge(h, v))

        block = es.enter_context(nc.Block())

        def replay(name):
            def f(E):
                for op in p.ops[name]:
                    op(E)
            return f
        block.sync(replay("sync"))
        block.tensor(replay("tensor"))
        block.vector(replay("vector"))
        block.scalar(replay("scalar"))
        block.gpsimd(replay("gpsimd"))

    return nc


def _prep(Wq, bq, Wk, bk, Wv, bv, Wo, bo, w_up, w_down):
    f = np.float32
    return {
        "wqT": np.ascontiguousarray(np.asarray(Wq, f).T),
        "wkT": np.ascontiguousarray(np.asarray(Wk, f).T),
        "wvT": np.ascontiguousarray(np.asarray(Wv, f).T),
        "woT": np.ascontiguousarray(np.asarray(Wo, f).T),
        "bq8": np.ascontiguousarray(np.asarray(bq, f).reshape(DC, 128).T),
        "bk8": np.ascontiguousarray(np.asarray(bk, f).reshape(DC, 128).T),
        "bv_row": np.asarray(bv, f).reshape(1, D).copy(),
        "bo_row": np.asarray(bo, f).reshape(1, D).copy(),
        "wupT": np.ascontiguousarray(np.asarray(w_up, f).T),
        "wdnT": np.ascontiguousarray(np.asarray(w_down, f).T),
    }


def kernel(L0_attn_x, L0_attn_Wq, L0_attn_bq, L0_attn_Wk, L0_attn_bk,
           L0_attn_Wv, L0_attn_bv, L0_attn_Wo, L0_attn_bo,
           L0_ffn_w_up, L0_ffn_w_down):
    x = np.ascontiguousarray(np.asarray(L0_attn_x, dtype=np.float32))
    weights = _prep(L0_attn_Wq, L0_attn_bq, L0_attn_Wk, L0_attn_bk, L0_attn_Wv,
                    L0_attn_bv, L0_attn_Wo, L0_attn_bo, L0_ffn_w_up, L0_ffn_w_down)

    OWN = S // 4
    nown = OWN // TILE

    nc = build_nc(0, nown)
    in_maps = []
    for core in range(NCORE):
        b, pq = core // 4, core % 4
        xb = x[b]
        start = pq * OWN
        sm = np.zeros((128, NCORE), np.float32)
        for r in range(NCORE):
            if r // 4 == b and r % 4 < pq:
                sm[:, r] = 1.0
        m = {"x_own": np.ascontiguousarray(xb[start:start + OWN]), "smask": sm}
        m.update(weights)
        in_maps.append(m)
    res = run_bass_kernel_spmd(nc, in_maps, core_ids=list(range(NCORE)))
    outp = np.empty((B, S, D), np.float32)
    for core in range(NCORE):
        b, pq = core // 4, core % 4
        outp[b, pq * OWN:(pq + 1) * OWN] = np.asarray(res.results[core]["out"])
    return outp


# version 19
# speedup vs baseline: 1.2289x; 1.0069x over previous
import numpy as np
from contextlib import ExitStack

import concourse.bass as bass
import concourse.mybir as mybir
from concourse.bass_utils import run_bass_kernel_spmd

F32 = mybir.dt.float32
BF16 = mybir.dt.bfloat16
AF = mybir.ActivationFunctionType
ALU = mybir.AluOpType
AX = mybir.AxisListType

B, S, D, H, HD, DFF = 2, 8192, 1024, 16, 64, 4096
NCORE = 8
TILE = 512
CH = 128
EV = 68
LNEPS = 1e-5
ATTEPS = 1e-6
RMSEPS = 1e-8
NCH = TILE // CH
DC = D // 128
FC = DFF // 128

ENGS = ("sync", "tensor", "vector", "scalar", "gpsimd")
ESEM = {"tensor": "PE", "vector": "DV", "scalar": "AC", "gpsimd": "GP", "sync": "LD"}
SY, TE, VE, SC, GP = ENGS


class Prog:

    def __init__(self, nc):
        self.nc = nc
        self.ops = {e: [] for e in ENGS}
        self.semh = {}
        self.cnt = {}
        self.seen = {e: {} for e in ENGS}
        self.buf = {}
        self.dmarr = {}

    def mksem(self, es, *names):
        for n in names:
            self.semh[n] = es.enter_context(self.nc.semaphore(n))
            self.cnt[n] = 0

    def _wait(self, eng, sname, v):
        if self.seen[eng].get(sname, 0) >= v:
            return
        self.seen[eng][sname] = v
        h = self.semh[sname]
        self.ops[eng].append(lambda E, h=h, v=v: E.wait_ge(h, v))

    DMA_POOLS = {"LD": ["LD0", "LD1", "LD2", "LD3"],
                 "LDG": ["LG0", "LG1", "LG2", "LG3"],
                 "ST": ["ST0"]}

    def emit(self, eng, fns, reads=(), writes=(), sem=None, inc=1):
        if not isinstance(fns, (list, tuple)):
            fns = [fns]
        if sem is None:
            sem = ESEM[eng]
        if sem in self.DMA_POOLS:
            pool = self.DMA_POOLS[sem]
            idx = self.dmarr.get(sem, 0)
            self.dmarr[sem] = (idx + 1) % len(pool)
            sem = pool[idx]
            if self.cnt[sem] > 0:
                self._wait(eng, sem, self.cnt[sem])
        need = {}
        for b in reads:
            st = self.buf.get(b)
            if st and st["w"]:
                s, v = st["w"]
                need[s] = max(need.get(s, 0), v)
        for b in writes:
            st = self.buf.get(b)
            if st:
                toks = ([st["w"]] if st["w"] else []) + st["r"]
                for s, v in toks:
                    need[s] = max(need.get(s, 0), v)
        for s in sorted(need):
            self._wait(eng, s, need[s])
        for f in fns[:-1]:
            self.ops[eng].append(f)
        self.cnt[sem] += inc
        v = self.cnt[sem]
        h = self.semh[sem]
        last = fns[-1]
        self.ops[eng].append(lambda E, last=last, h=h, inc=inc: last(E).then_inc(h, inc))
        tok = (sem, v)
        for b in writes:
            self.buf[b] = {"w": tok, "r": []}
        for b in reads:
            st = self.buf.setdefault(b, {"w": None, "r": []})
            st["r"].append(tok)
        return tok

    def barrier(self, exclude=("CC",)):
        cur = {s: v for s, v in self.cnt.items() if v > 0 and s not in exclude}
        for e in ENGS:
            for s in sorted(cur):
                self._wait(e, s, cur[s])


def build_nc(npre_slabs, nown_tiles):
    OWN = nown_tiles * TILE
    PRE = max(npre_slabs * TILE, TILE)
    nc = bass.Bass("TRN2", target_bir_lowering=False)

    x_own = nc.dram_tensor("x_own", [OWN, D], F32, kind="ExternalInput").ap()
    smask = nc.dram_tensor("smask", [128, NCORE], F32, kind="ExternalInput").ap()
    wqT = nc.dram_tensor("wqT", [D, D], F32, kind="ExternalInput").ap()
    wkT = nc.dram_tensor("wkT", [D, D], F32, kind="ExternalInput").ap()
    wvT = nc.dram_tensor("wvT", [D, D], F32, kind="ExternalInput").ap()
    woT = nc.dram_tensor("woT", [D, D], F32, kind="ExternalInput").ap()
    bq8 = nc.dram_tensor("bq8", [128, DC], F32, kind="ExternalInput").ap()
    bk8 = nc.dram_tensor("bk8", [128, DC], F32, kind="ExternalInput").ap()
    bv_row = nc.dram_tensor("bv_row", [1, D], F32, kind="ExternalInput").ap()
    bo_row = nc.dram_tensor("bo_row", [1, D], F32, kind="ExternalInput").ap()
    wupT = nc.dram_tensor("wupT", [D, DFF], F32, kind="ExternalInput").ap()
    wdnT = nc.dram_tensor("wdnT", [DFF, D], F32, kind="ExternalInput").ap()
    out = nc.dram_tensor("out", [OWN, D], F32, kind="ExternalOutput").ap()
    wup_t = nc.dram_tensor("wup_t", [D, DFF], BF16, kind="Internal").ap()
    wdn_t = nc.dram_tensor("wdn_t", [DFF, D], BF16, kind="Internal").ap()
    qT_d = nc.dram_tensor("qT_d", [nown_tiles * 128, DC * TILE], BF16, kind="Internal").ap()
    num_d = nc.dram_tensor("num_d", [OWN, H * EV], BF16, kind="Internal").ap()
    st_loc = nc.dram_tensor("st_loc", [64, H * EV], F32, kind="Internal").ap()
    st_all = nc.dram_tensor("st_all", [NCORE * 64, H * EV], F32, kind="Internal").ap()

    with ExitStack() as es:
        pst = lambda name, shape, dt: es.enter_context(nc.psum_tensor(name, shape, dt))

        ARENA_BC = 106000
        arena = es.enter_context(nc.sbuf_tensor("arena", [128, ARENA_BC], BF16))
        cur = [0]

        def take(bcols, dt=BF16, parts=128):
            o = cur[0]
            bcols_al = (bcols + 15) // 16 * 16
            cur[0] += bcols_al
            assert cur[0] <= ARENA_BC, f"arena overflow at {cur[0]}"
            v = arena[0:parts, o:o + bcols]
            if dt == F32:
                v = v.bitcast(F32)
            return v

        x_sl = [take(8192, F32) for _ in range(2)]
        out_sl = take(8192, F32)
        xn_tok = take(4096)
        xnT = take(4096)
        st_f = take(2176, F32, parts=64)
        st_b = take(1088)
        bn_s = take(96, F32)
        mv_s = take(16, F32)
        rstd_s = take(8, F32)
        cA_s = take(8, F32)
        sffn_s = take(8, F32)
        cB_s = take(8, F32)
        lnt_s = take(32, F32)
        rcp_s = take(8, F32)
        s2col = take(8, F32)
        tpart = take(4, F32)
        tred = take(2, F32)
        thr_s = take(4, F32)
        nthr_s = take(4, F32)
        lneps_c = take(2, F32)
        mask8 = take(512)
        ident = take(128)
        ones_col = take(1)
        ones_cf = take(2, F32)
        bq_s = take(16, F32)
        bk_s = take(16, F32)
        sm_s = take(16, F32)
        shard_s = take(2176, F32, parts=64)
        ones_row = take(128, parts=1)
        ones_rf = take(256, F32, parts=1)
        ones_1f = take(2, F32, parts=1)
        bvr_s = take(1024, parts=1)
        bor_s = take(1024, parts=1)
        s2row = take(1024, F32, parts=1)
        trow_s = take(4, F32, parts=1)

        twf = [take(1024, F32) for _ in range(2)]
        tg1 = take(1024, F32)
        tout = [take(512) for _ in range(2)]
        dyn_base = cur[0]

        wo_s = take(8192)
        wts_top = cur[0]
        wq_s = take(8192)
        wk_s = take(8192)
        wv_s = take(8192)
        vtok = take(4352)
        ph_e = [take(1024, F32) for _ in range(2)]
        ph_r = [take(1024, F32) for _ in range(2)]
        var_base = cur[0]

        ktok = take(4096)
        qT = kT = ktok_c = o_tok = oT = am_s = None

        PS = [pst(f"P{i}", [128, 512], F32) for i in range(7)]
        P7 = pst("P7", [128, 1024], BF16)

        p = Prog(nc)
        p.mksem(es, "LD0", "LD1", "LD2", "LD3", "LG0", "LG1", "LG2", "LG3", "ST0", "CC", "PE", "DV", "AC", "GP")

        bank_fresh = {k: True for k in range(8)}

        def lead(i):
            f = bank_fresh[i]
            bank_fresh[i] = False
            return f

        def bk_done(i):
            bank_fresh[i] = True

        p.emit(GP, lambda E: E.memset(ident, 0.0), writes=["ident"])
        p.emit(GP, lambda E: E.affine_select(
            out=ident, in_=ident, compare_op=ALU.not_equal, fill=1.0,
            base=0, pattern=[[-1, 128]], channel_multiplier=1),
            reads=["ident"], writes=["ident"])
        p.emit(GP, lambda E: E.memset(mask8, 1.0), writes=["mask8"])
        for i in range(4):
            p.emit(GP, lambda E, i=i: E.affine_select(
                out=mask8[:, i * CH:(i + 1) * CH], in_=mask8[:, i * CH:(i + 1) * CH],
                compare_op=ALU.is_ge, fill=0.0, base=0,
                pattern=[[1, CH]], channel_multiplier=-1),
                reads=["mask8"], writes=["mask8"])
        p.emit(GP, lambda E: E.memset(lneps_c[:, 0:1], float(LNEPS)), writes=["lneps"])
        for t, n in [(ones_row, "ones_row"), (ones_col, "ones_col"),
                     (ones_rf, "ones_rf"), (ones_cf, "ones_cf"), (ones_1f, "ones_1f")]:
            p.emit(GP, lambda E, t=t: E.memset(t, 1.0), writes=[n])
        p.emit(VE, lambda E: E.memset(st_f, 0.0), writes=["st_f"])
        p.emit(VE, lambda E: E.memset(st_b, 0.0), writes=["st_b"])
        p.emit(VE, lambda E: E.memset(tpart, 0.0), writes=["tpart"])
        p.emit(VE, lambda E: E.memset(vtok, 0.0), writes=["vtok"])
        p.emit(VE, lambda E: E.memset(
            vtok.rearrange("q (c h e) -> q c h e", c=NCH, h=H)[:, :, :, HD:HD + 1], 1.0),
            reads=["vtok"], writes=["vtok"])

        for w_ap, tile, name in [(wqT, wq_s, "wq"), (wkT, wk_s, "wk"),
                                 (wvT, wv_s, "wv"), (woT, wo_s, "wo")]:
            p.emit(GP, lambda E, w=w_ap, t=tile: E.dma_start(
                out=t.rearrange("q (c d) -> q c d", c=DC),
                in_=w.rearrange("(c q) d -> q c d", q=128)),
                writes=[name], sem="LDG", inc=16)
        p.emit(SY, lambda E: E.dma_start(out=bq_s[:, 0:DC], in_=bq8), writes=["bq"], inc=16)
        p.emit(SY, lambda E: E.dma_start(out=bk_s[:, 0:DC], in_=bk8), writes=["bk"], inc=16)
        p.emit(GP, lambda E: E.dma_start(out=bvr_s, in_=bv_row), writes=["bv"], sem="LDG", inc=16)
        p.emit(GP, lambda E: E.dma_start(out=bor_s, in_=bo_row), writes=["bo"], sem="LDG", inc=16)
        p.emit(SY, lambda E: E.dma_start(out=sm_s[:, 0:NCORE], in_=smask),
               writes=["sm"], inc=16)

        def ln_stats(xb, xname, sffn=False):
            xs = x_sl[xb]
            for c in range(NCH):
                p.emit(VE, lambda E, c=c, xs=xs: E.bn_stats(
                    bn_s[:, c * 12:c * 12 + 6], xs[:, c * D:c * D + 512]),
                    reads=[xname], writes=[f"bn{c}"])
                p.emit(VE, lambda E, c=c, xs=xs: E.bn_stats(
                    bn_s[:, c * 12 + 6:c * 12 + 12], xs[:, c * D + 512:(c + 1) * D]),
                    reads=[xname, f"bn{c}"], writes=[f"bn{c}"])
                p.emit(VE, lambda E, c=c: E.bn_aggr(
                    mv_s[:, 2 * c:2 * c + 2], bn_s[:, c * 12:(c + 1) * 12]),
                    reads=[f"bn{c}"], writes=[f"mv{c}"])
            mvr = [f"mv{c}" for c in range(NCH)]
            p.emit(SC, lambda E: E.activation(
                lnt_s[:, 0:NCH], mv_s[:, 1:2 * NCH:2], AF.Ln, bias=lneps_c[:, 0:1], scale=1.0),
                reads=mvr + ["lneps"], writes=["lnt0"])
            p.emit(SC, lambda E: E.activation(
                rstd_s[:, 0:NCH], lnt_s[:, 0:NCH], AF.Exp, scale=-0.5),
                reads=["lnt0"], writes=["rstd"])
            p.emit(VE, lambda E: E.scalar_tensor_tensor(
                out=cA_s[:, 0:NCH], in0=mv_s[:, 0:2 * NCH:2], scalar=-1.0,
                in1=rstd_s[:, 0:NCH], op0=ALU.mult, op1=ALU.mult),
                reads=mvr + ["rstd"], writes=["cA"])
            if sffn:
                p.emit(SC, lambda E: E.activation(
                    lnt_s[:, NCH:2 * NCH], mv_s[:, 1:2 * NCH:2], AF.Ln, scale=1.0),
                    reads=mvr, writes=["lnt1"])
                p.emit(VE, lambda E: E.tensor_sub(
                    lnt_s[:, 2 * NCH:3 * NCH], lnt_s[:, NCH:2 * NCH], lnt_s[:, 0:NCH]),
                    reads=["lnt0", "lnt1"], writes=["lnt2"])
                p.emit(SC, lambda E: E.activation(
                    lnt_s[:, 3 * NCH:4 * NCH], lnt_s[:, 2 * NCH:3 * NCH], AF.Exp,
                    scale=0.5),
                    reads=["lnt2"], writes=["lnt3"])
                p.emit(VE, lambda E: E.tensor_scalar(
                    out=lnt_s[:, 3 * NCH:4 * NCH], in0=lnt_s[:, 3 * NCH:4 * NCH],
                    scalar1=float(RMSEPS), scalar2=None, op0=ALU.add),
                    reads=["lnt3"], writes=["lnt3"])
                p.emit(VE, lambda E: E.reciprocal(
                    lnt_s[:, 2 * NCH:3 * NCH], lnt_s[:, 3 * NCH:4 * NCH]),
                    reads=["lnt3"], writes=["lnt2"])
                p.emit(VE, lambda E: E.tensor_mul(
                    sffn_s[:, 0:NCH], rstd_s[:, 0:NCH], lnt_s[:, 2 * NCH:3 * NCH]),
                    reads=["rstd", "lnt2"], writes=["sffn"])
                p.emit(VE, lambda E: E.scalar_tensor_tensor(
                    out=cB_s[:, 0:NCH], in0=mv_s[:, 0:2 * NCH:2], scalar=-1.0,
                    in1=sffn_s[:, 0:NCH], op0=ALU.mult, op1=ALU.mult),
                    reads=mvr + ["sffn"], writes=["cB"])

        def xn_apply(xb, xname, scale_t, scale_name, shift_t, shift_name):
            xs = x_sl[xb]
            for c in range(NCH):
                p.emit(SC, lambda E, c=c, xs=xs: E.activation(
                    xn_tok[:, c * D:(c + 1) * D], xs[:, c * D:(c + 1) * D], AF.Identity,
                    bias=shift_t[:, c:c + 1], scale=scale_t[:, c:c + 1]),
                    reads=[xname, scale_name, shift_name], writes=["xn_tok"])

        def transpose_tok_to_ft(src, src_name, dst, dst_name):
            for c in range(NCH):
                fns = []
                for jm in range(DC):
                    def tr(E, c=c, jm=jm):
                        return E.transpose(
                            P7[:, jm * 128:(jm + 1) * 128],
                            src[:, c * D + jm * 128:c * D + (jm + 1) * 128], ident)
                    fns.append(tr)
                p.emit(TE, fns, reads=[src_name, "ident"], writes=["P7"])
                ov = dst.rearrange("q (jm t) -> q jm t", jm=DC)[:, :, c * 128:(c + 1) * 128]
                p.emit(SC, lambda E, ov=ov: E.copy(
                    ov, P7[:].rearrange("q (jm t) -> q jm t", jm=DC)),
                    reads=["P7"], writes=[dst_name])
                bk_done(7)

        def transpose_ft_to_tok(src, src_name, dst, dst_name):
            for c in range(NCH):
                fns = []
                for jd in range(DC):
                    def tr(E, c=c, jd=jd):
                        return E.transpose(
                            P7[:, jd * 128:(jd + 1) * 128],
                            src[:, jd * TILE + c * 128:jd * TILE + (c + 1) * 128], ident)
                    fns.append(tr)
                p.emit(TE, fns, reads=[src_name, "ident"], writes=["P7"])
                p.emit(VE, lambda E, c=c: E.tensor_copy(
                    dst[:, c * D:(c + 1) * D], P7[:]),
                    reads=["P7"], writes=[dst_name])
                bk_done(7)

        def proj_feature(wtile, wname, btile, bname, dst, dst_name):
            for jd in range(DC):
                bank = jd % 4
                pp = jd % 2
                fns = []
                for jm in range(DC):
                    def mm(E, jd=jd, jm=jm, bank=bank, st=(lead(bank) if (jm == 0) else False)):
                        return E.matmul(
                            PS[bank][:],
                            wtile[:, jm * D + jd * 128: jm * D + (jd + 1) * 128],
                            xnT[:, jm * TILE:(jm + 1) * TILE],
                            start=st, stop=(jm == DC - 1))
                    fns.append(mm)
                p.emit(TE, fns, reads=[wname, "xnT"], writes=[f"PB{bank}"])
                dv = dst[:, jd * TILE:(jd + 1) * TILE]
                p.emit(SC, lambda E, jd=jd, bank=bank, pp=pp: E.activation(
                    ph_e[pp][:, 0:512], PS[bank][:], AF.Exp, bias=btile[:, jd:jd + 1],
                    scale=1.0),
                    reads=[f"PB{bank}", bname], writes=[f"ph_e{pp}"])
                p.emit(SC, lambda E, jd=jd, bank=bank, pp=pp: E.activation(
                    ph_r[pp][:, 0:512], PS[bank][:], AF.Relu, bias=btile[:, jd:jd + 1],
                    scale=1.0),
                    reads=[f"PB{bank}", bname], writes=[f"ph_r{pp}"])
                p.emit(VE, lambda E, dv=dv, pp=pp: E.scalar_tensor_tensor(
                    out=dv, in0=ph_e[pp][:, 0:512], scalar=1.0, in1=ph_r[pp][:, 0:512],
                    op0=ALU.min, op1=ALU.add),
                    reads=[f"ph_e{pp}", f"ph_r{pp}"], writes=[dst_name])
                bk_done(bank)
                tern_tick()

        def proj_token(wtile, wname, brow, bname, writer, banks):
            for c in range(NCH):
                for half in range(2):
                    bank = banks[(c * 2 + half) % len(banks)]
                    fns = []
                    for jm in range(DC):
                        def mm(E, c=c, half=half, jm=jm, bank=bank, st=(lead(bank) if (jm == 0) else False)):
                            return E.matmul(
                                PS[bank][:],
                                xnT[:, jm * TILE + c * 128: jm * TILE + (c + 1) * 128],
                                wtile[:, jm * D + half * 512: jm * D + (half + 1) * 512],
                                start=st, stop=False)
                        fns.append(mm)

                    def mmb(E, half=half, bank=bank):
                        return E.matmul(
                            PS[bank][:], ones_row, brow[0:1, half * 512:(half + 1) * 512],
                            start=False, stop=True)
                    fns.append(mmb)
                    p.emit(TE, fns, reads=[wname, "xnT", bname, "ones_row"],
                           writes=[f"PB{bank}"])
                    writer(c, half, bank)
                    bk_done(bank)
                    tern_tick()

        SDSEG = [(4, 0, 6), (5, 6, 11), (6, 11, 16)]

        def state_fold():
            for bank, h0, h1 in SDSEG:
                p.emit(VE, lambda E, bank=bank, h0=h0, h1=h1: E.tensor_tensor(
                    st_f[:, h0 * EV:h1 * EV], st_f[:, h0 * EV:h1 * EV],
                    PS[bank][0:64, 0:(h1 - h0) * EV], ALU.add),
                    reads=[f"PB{bank}", "st_f"], writes=["st_f"])
                bk_done(bank)
            p.emit(VE, lambda E: E.tensor_copy(
                st_b[0:64, :].rearrange("q (h e) -> q h e", h=H)[:, 0:H:2, :],
                st_f[:, 0:H * EV].rearrange("q (h e) -> q h e", h=H)[:, 0:H:2, :]),
                reads=["st_f"], writes=["st_b"])
            p.emit(VE, lambda E: E.tensor_copy(
                st_b[64:128, :].rearrange("q (h e) -> q h e", h=H)[:, 1:H:2, :],
                st_f[:, 0:H * EV].rearrange("q (h e) -> q h e", h=H)[:, 1:H:2, :]),
                reads=["st_f"], writes=["st_b"])

        scanref = {}

        def scan_pre(ci):
            qT = scanref["qT"]; kT = scanref["kT"]
            ktok_c = scanref["ktok_c"][ci % 2]
            am_s = scanref["am2"][ci % 2]
            fns = []
            for jd in range(DC):
                def tr(E, jd=jd, ci=ci):
                    return E.transpose(
                        P7[:, jd * 128:(jd + 1) * 128],
                        kT[:, jd * TILE + ci * 128:jd * TILE + (ci + 1) * 128], ident)
                fns.append(tr)
            p.emit(TE, fns, reads=["kT", "ident"], writes=["P7"])
            p.emit(SC, lambda E, ktok_c=ktok_c: E.copy(ktok_c[:, 0:D], P7[:]),
                   reads=["P7"], writes=[f"ktokc{ci % 2}"])
            bk_done(7)
            fns = []
            wr = set()
            for h in range(H):
                bank = h % 4
                col = (h // 4) * 128

                def m1(E, h=h, bank=bank, col=col, ci=ci,
                       st=(lead(bank) if (h // 4 == 0) else False)):
                    return E.matmul(
                        PS[bank][:, col:col + 128],
                        kT[(h % 2) * 64:(h % 2) * 64 + 64,
                           (h // 2) * TILE + ci * 128:(h // 2) * TILE + (ci + 1) * 128],
                        qT[(h % 2) * 64:(h % 2) * 64 + 64,
                           (h // 2) * TILE + ci * 128:(h // 2) * TILE + (ci + 1) * 128],
                        start=st, stop=(h >= H - 4))
                fns.append(m1)
                wr.add(f"PB{bank}")
            p.emit(TE, fns, reads=["kT", "qT"], writes=sorted(wr))
            for bank in range(4):
                ov = am_s.rearrange("q (g b c) -> q g b c", g=4, b=4)[:, :, bank, :]
                p.emit(VE, lambda E, bank=bank, ov=ov: E.tensor_tensor(
                    ov, PS[bank][:].rearrange("q (g c) -> q g c", g=4),
                    mask8.rearrange("q (g c) -> q g c", g=4), ALU.mult),
                    reads=[f"PB{bank}", "mask8"], writes=[f"am{ci % 2}_{bank}"])
                bk_done(bank)

        def scan_m2pair(ci, pair):
            am_s = scanref["am2"][ci % 2]
            for g in (2 * pair, 2 * pair + 1):
                nb = 4 + (g % 2)
                fns = []
                for i in range(4):
                    h = g * 4 + i

                    def m2(E, h=h, i=i, nb=nb, ci=ci, am_s=am_s,
                           st=(lead(nb) if (i == 0) else False)):
                        return E.matmul(
                            PS[nb][:, i * EV:i * EV + EV],
                            am_s[:, h * 128:(h + 1) * 128],
                            vtok[:, ci * H * EV + h * EV: ci * H * EV + (h + 1) * EV],
                            start=st, stop=False)
                    fns.append(m2)
                p.emit(TE, fns,
                       reads=[f"am{ci % 2}_{b}" for b in range(4)] + ["vtok"],
                       writes=[f"PB{nb}"])

        def scan_m3save(ci, pair):
            qT = scanref["qT"]; numsave = scanref["numsave"]
            for g in (2 * pair, 2 * pair + 1):
                nb = 4 + (g % 2)
                fns = []
                for i in range(4):
                    h = g * 4 + i

                    def m3(E, h=h, i=i, nb=nb, ci=ci):
                        return E.matmul(
                            PS[nb][:, i * EV:i * EV + EV],
                            qT[:, (h // 2) * TILE + ci * 128:(h // 2) * TILE + (ci + 1) * 128],
                            st_b[:, h * EV:(h + 1) * EV],
                            start=False, stop=(i == 3))
                    fns.append(m3)
                p.emit(TE, fns, reads=["qT", "st_b"], writes=[f"PB{nb}"])
                p.emit(SC, lambda E, nb=nb, g=g, ci=ci: E.copy(
                    numsave[:, (ci % 2) * H * EV + g * 4 * EV:
                            (ci % 2) * H * EV + (g + 1) * 4 * EV],
                    PS[nb][:, 0:4 * EV]),
                    reads=[f"PB{nb}"], writes=[f"nsv{ci}_{g}"])
                bk_done(nb)
                tern_tick()

        def scan_tail(ci):
            ktok_c = scanref["ktok_c"][ci % 2]
            numsave = scanref["numsave"]
            fns = []
            wr = set()
            for bank, h0, h1 in SDSEG:
                for i in range(h1 - h0):
                    h = h0 + i

                    def m4(E, h=h, i=i, bank=bank, ci=ci, ktok_c=ktok_c,
                           last=(i == h1 - h0 - 1), st=(lead(bank) if (i == 0) else False)):
                        return E.matmul(
                            PS[bank][0:64, i * EV:i * EV + EV],
                            ktok_c[:, h * HD:(h + 1) * HD],
                            vtok[:, ci * H * EV + h * EV: ci * H * EV + (h + 1) * EV],
                            start=st, stop=last)
                    fns.append(m4)
                    wr.add(f"PB{bank}")
            p.emit(TE, fns, reads=[f"ktokc{ci % 2}", "vtok"], writes=sorted(wr))
            state_fold()
            if ci % 2 == 1:
                ti = scanref["ti"]
                base = ti * TILE + (ci - 1) * 128
                p.emit(SY, lambda E, base=base: E.dma_start(
                    out=num_d[base:base + 256, :].rearrange("(c q) e -> q c e", q=128),
                    in_=numsave.rearrange("q (c e) -> q c e", c=2)),
                    reads=[f"nsv{c}_{g}" for c in (ci - 1, ci) for g in range(4)],
                    writes=["num_dd"], sem="LD", inc=16)

        tern_chunks = [("up", j) for j in range(64)] + [("dn", j) for j in range(64)]

        def tern_load(kind, j, b):
            if kind == "up":
                src = wupT[(j // 8) * 128:(j // 8 + 1) * 128,
                           (j % 8) * 512:(j % 8 + 1) * 512]
            else:
                src = wdnT[(j // 2) * 128:(j // 2 + 1) * 128,
                           (j % 2) * 512:(j % 2 + 1) * 512]
            p.emit(GP, lambda E, s=src, b=b: E.dma_start(out=twf[b], in_=s),
                   writes=[f"twf{b}"], sem="LDG", inc=16)

        def tern_abs(kind, j, b):
            col = 0 if kind == "up" else 1
            p.emit(SC, lambda E, b=b: E.activation(
                tg1, twf[b], AF.Abs, accum_out=tred[:, 0:1]),
                reads=[f"twf{b}"], writes=["tred", "tg1"])
            p.emit(VE, lambda E, col=col: E.tensor_tensor(
                tpart[:, col:col + 1], tpart[:, col:col + 1], tred[:, 0:1], ALU.add),
                reads=["tred", "tpart"], writes=["tpart"])

        def tern_finalize():
            p.emit(TE, lambda E, st=lead(6): E.matmul(
                PS[6][0:1, 0:2], ones_cf[:, 0:1], tpart[:, 0:2], start=st, stop=True),
                reads=["tpart", "ones_cf"], writes=["PB6"])
            p.emit(SC, lambda E: E.activation(
                trow_s[0:1, 0:2], PS[6][0:1, 0:2], AF.Copy,
                scale=float(0.5 / (D * DFF))),
                reads=["PB6"], writes=["trow"])
            bk_done(6)
            p.emit(TE, lambda E, st=lead(6): E.matmul(
                PS[6][:, 0:2], ones_rf[0:1, 0:128], trow_s[0:1, 0:2],
                start=st, stop=True),
                reads=["trow", "ones_rf"], writes=["PB6"])
            p.emit(VE, lambda E: E.tensor_copy(thr_s[:, 0:2], PS[6][:, 0:2]),
                   reads=["PB6"], writes=["thr"])
            p.emit(VE, lambda E: E.tensor_scalar(
                out=nthr_s[:, 0:2], in0=thr_s[:, 0:2], scalar1=-1.0, scalar2=None,
                op0=ALU.mult),
                reads=["thr"], writes=["nthr"])
            bk_done(6)

        def tern_quant(kind, j, b):
            col = 0 if kind == "up" else 1
            p.emit(VE, lambda E, b=b, col=col: E.tensor_scalar(
                out=tg1, in0=twf[b], scalar1=nthr_s[:, col:col + 1], scalar2=None,
                op0=ALU.is_lt),
                reads=[f"twf{b}", "nthr"], writes=["tg1"])
            p.emit(VE, lambda E, b=b, col=col: E.scalar_tensor_tensor(
                out=tout[b], in0=twf[b], scalar=thr_s[:, col:col + 1], in1=tg1,
                op0=ALU.is_gt, op1=ALU.subtract),
                reads=[f"twf{b}", "thr", "tg1"], writes=[f"tout{b}"])
            if kind == "up":
                dst = wup_t[(j // 8) * 128:(j // 8 + 1) * 128,
                            (j % 8) * 512:(j % 8 + 1) * 512]
                scr = "scr_up"
            else:
                dst = wdn_t[(j // 2) * 128:(j // 2 + 1) * 128,
                            (j % 2) * 512:(j % 2 + 1) * 512]
                scr = "scr_dn"
            p.emit(SY, lambda E, d=dst, b=b: E.dma_start(out=d, in_=tout[b]),
                   reads=[f"tout{b}"], writes=[scr], sem="LD", inc=16)

        tst = {"a": 0, "b": 0, "fin": False}
        quant_done = set()

        def tern_step(n):
            for _ in range(n):
                if tst["a"] < len(tern_chunks):
                    k, j = tern_chunks[tst["a"]]
                    tern_load(k, j, tst["b"])
                    tern_abs(k, j, tst["b"])
                    tst["a"] += 1
                    tst["b"] ^= 1

        def tern_abs_flush():
            while tst["a"] < len(tern_chunks):
                tern_step(1)
            if not tst["fin"]:
                tern_finalize()
                tst["fin"] = True

        def tern_quant_chunks(chunks):
            for k, j in chunks:
                if (k, j) in quant_done:
                    continue
                quant_done.add((k, j))
                tern_load(k, j, tst["b"])
                tern_quant(k, j, tst["b"])
                tst["b"] ^= 1

        def tern_tick(n=2):
            if tst["a"] < len(tern_chunks):
                tern_step(n)
            elif not tst["fin"]:
                tern_finalize()
                tst["fin"] = True
            else:
                tern_quant_step(n)

        qsched = {"i": 0}

        def tern_quant_step(n):
            c = 0
            while qsched["i"] < len(tern_chunks) and c < n:
                k, j = tern_chunks[qsched["i"]]
                qsched["i"] += 1
                if (k, j) in quant_done:
                    continue
                tern_quant_chunks([(k, j)])
                c += 1

        p.barrier()
        cur[0] = var_base
        qT = take(4096)
        kT = take(4096)
        ktok_c = [take(1024) for _ in range(2)]
        am2 = [take(2048) for _ in range(2)]
        numsave = take(2176)
        scanref["qT"] = qT
        scanref["kT"] = kT
        scanref["ktok_c"] = ktok_c
        scanref["am2"] = am2
        scanref["numsave"] = numsave
        def a1_load_stats(ti):
            xb = ti % 2
            xname = f"x{xb}"
            p.emit(SY, lambda E, ti=ti, xb=xb: E.dma_start(
                out=x_sl[xb].rearrange("q (c d) -> q c d", c=NCH),
                in_=x_own[ti * TILE:(ti + 1) * TILE, :].rearrange("(c q) d -> q c d", q=128)),
                writes=[xname], inc=16)
            ln_stats(xb, xname)

        a1_load_stats(0)
        for ti in range(nown_tiles):
            xb = ti % 2
            xname = f"x{xb}"
            xn_apply(xb, xname, rstd_s, "rstd", cA_s, "cA")
            transpose_tok_to_ft(xn_tok, "xn_tok", xnT, "xnT")
            proj_feature(wq_s, "wq", bq_s, "bq", qT, "qT")
            proj_feature(wk_s, "wk", bk_s, "bk", kT, "kT")
            scanref["ti"] = ti
            p.emit(SY, lambda E, ti=ti: E.dma_start(
                out=qT_d[ti * 128:(ti + 1) * 128, :], in_=qT),
                reads=["qT"], writes=["qT_dd"], sem="LD", inc=16)
            if ti + 1 < nown_tiles:
                a1_load_stats(ti + 1)

            def v_writer(c, half, bank):
                ov = vtok.rearrange("q (c h e) -> q c h e", c=NCH, h=H)[
                    :, c, half * 8:(half + 1) * 8, 0:HD]
                p.emit(SC, lambda E, ov=ov, bank=bank: E.copy(
                    ov, PS[bank][:].rearrange("q (h e) -> q h e", h=8)),
                    reads=[f"PB{bank}"], writes=["vtok"])
            proj_token(wv_s, "wv", bvr_s, "bv", v_writer, [0, 1, 2, 3])

            scan_pre(0)
            for ci in range(NCH):
                scan_m2pair(ci, 0)
                if ci < NCH - 1:
                    scan_pre(ci + 1)
                scan_m3save(ci, 0)
                scan_m2pair(ci, 1)
                scan_m3save(ci, 1)
                scan_tail(ci)

        tern_abs_flush()
        tern_quant_step(len(tern_chunks))

        p.emit(SY, lambda E: E.dma_start(out=st_loc, in_=st_f[:, 0:H * EV]),
               reads=["st_f"], writes=["st_loc_d"], sem="LD", inc=16)
        p.emit(GP, lambda E: E.collective_compute(
            "AllGather", ALU.bypass,
            replica_groups=[list(range(NCORE))],
            ins=[st_loc.opt()], outs=[st_all.opt()]),
            reads=["st_loc_d"], writes=["st_all_d"], sem="CC", inc=1)
        p.barrier()
        cur[0] = wts_top
        hT = take(16384)
        fwu = [take(4096) for _ in range(2)]
        fwd = [take(2048) for _ in range(2)]
        hsq = [take(512) for _ in range(2)]

        def b_load_stats(ti):
            xb = ti % 2
            xname = f"x{xb}"
            p.emit(SY, lambda E, ti=ti, xb=xb: E.dma_start(
                out=x_sl[xb].rearrange("q (c d) -> q c d", c=NCH),
                in_=x_own[ti * TILE:(ti + 1) * TILE, :].rearrange("(c q) d -> q c d", q=128)),
                writes=[xname], inc=16)
            ln_stats(xb, xname, sffn=True)

        b_load_stats(0)
        for ti in range(nown_tiles):
            xb = ti % 2
            xname = f"x{xb}"
            xn_apply(xb, xname, sffn_s, "sffn", cB_s, "cB")
            transpose_tok_to_ft(xn_tok, "xn_tok", xnT, "xnT")

            for fs in range(DFF // 512):
                fb = fs % 2
                tern_quant_chunks([("up", jm * 8 + fs) for jm in range(DC)])
                src = wup_t.rearrange("(jm q) f -> q jm f", q=128)[:, :, fs * 512:(fs + 1) * 512]
                p.emit(GP, lambda E, src=src, fb=fb: E.dma_start(
                    out=fwu[fb].rearrange("q (jm f) -> q jm f", jm=DC), in_=src),
                    reads=["scr_up"], writes=[f"fwu{fb}"], sem="LDG", inc=16)
                for lf in range(4):
                    jf = fs * 4 + lf
                    bank = jf % 4
                    fns = []
                    for jm in range(DC):
                        def mm(E, jm=jm, lf=lf, fb=fb, bank=bank, st=(lead(bank) if (jm == 0) else False)):
                            return E.matmul(
                                PS[bank][:],
                                fwu[fb][:, jm * 512 + lf * 128: jm * 512 + (lf + 1) * 128],
                                xnT[:, jm * TILE:(jm + 1) * TILE],
                                start=st, stop=(jm == DC - 1))
                        fns.append(mm)
                    p.emit(TE, fns, reads=[f"fwu{fb}", "xnT"], writes=[f"PB{bank}"])
                    if jf > 0:
                        pj, phb = jf - 1, (jf - 1) % 2
                        p.emit(TE, lambda E, pj=pj, phb=phb,
                               st=(lead(6) if pj == 0 else False): E.matmul(
                            PS[6][0:1, :], ones_col, hsq[phb],
                            start=st, stop=False),
                            reads=[f"hsq{phb}", "ones_col"], writes=["PB6"])
                    p.emit(SC, lambda E, jf=jf, bank=bank: E.activation(
                        hT[:, jf * TILE:(jf + 1) * TILE], PS[bank][:], AF.Gelu),
                        reads=[f"PB{bank}"], writes=[f"hTp{jf % 2}"])
                    hb = jf % 2
                    p.emit(VE, lambda E, jf=jf, hb=hb: E.tensor_mul(
                        hsq[hb], hT[:, jf * TILE:(jf + 1) * TILE],
                        hT[:, jf * TILE:(jf + 1) * TILE]),
                        reads=[f"hTp{jf % 2}"], writes=[f"hsq{hb}"])
                    bk_done(bank)
            p.emit(TE, lambda E: E.matmul(
                PS[6][0:1, :], ones_col, hsq[(FC - 1) % 2],
                start=False, stop=True),
                reads=[f"hsq{(FC - 1) % 2}", "ones_col"], writes=["PB6"])
            if ti + 1 < nown_tiles:
                b_load_stats(ti + 1)
            p.emit(SC, lambda E: E.activation(
                s2row[0:1, 0:TILE], PS[6][0:1, :], AF.Ln, scale=float(1.0 / DFF)),
                reads=["PB6"], writes=["s2row"])
            bk_done(6)
            p.emit(SC, lambda E: E.activation(
                s2row[0:1, 0:TILE], s2row[0:1, 0:TILE], AF.Exp, scale=0.5),
                reads=["s2row"], writes=["s2row"])
            p.emit(VE, lambda E: E.tensor_scalar(
                out=s2row[0:1, 0:TILE], in0=s2row[0:1, 0:TILE], scalar1=float(RMSEPS),
                scalar2=None, op0=ALU.add),
                reads=["s2row"], writes=["s2row"])
            p.emit(VE, lambda E: E.reciprocal(s2row[0:1, 0:TILE], s2row[0:1, 0:TILE]),
                   reads=["s2row"], writes=["s2row"])
            fns = []
            for c in range(NCH):
                def mm(E, c=c, st=(lead(6) if (c == 0) else False)):
                    return E.matmul(
                        PS[6][:, c:c + 1], s2row[0:1, c * 128:(c + 1) * 128],
                        ones_1f[0:1, 0:1], start=st, stop=(c == NCH - 1))
                fns.append(mm)
            p.emit(TE, fns, reads=["s2row", "ones_1f"], writes=["PB6"])
            p.emit(VE, lambda E: E.tensor_copy(s2col[:, 0:NCH], PS[6][:, 0:NCH]),
                   reads=["PB6"], writes=["s2col"])
            bk_done(6)

            for half in range(2):
                for fs in range(DFF // 512):
                    fb = fs % 2
                    tern_quant_chunks([("dn", (fs * 4 + r) * 2 + half) for r in range(4)])
                    src = wdn_t.rearrange("(jf q) d -> q jf d", q=128)[
                        :, fs * 4:(fs + 1) * 4, half * 512:(half + 1) * 512]
                    p.emit(GP, lambda E, src=src, fb=fb: E.dma_start(
                        out=fwd[fb].rearrange("q (jf d) -> q jf d", jf=4), in_=src),
                        reads=["scr_dn"], writes=[f"fwd{fb}"], sem="LDG", inc=16)
                    fns = []
                    for lf in range(4):
                        jf = fs * 4 + lf
                        for c in range(NCH):
                            def mm(E, jf=jf, lf=lf, c=c, fb=fb, st=(lead(c) if (jf == 0) else False)):
                                return E.matmul(
                                    PS[c][:],
                                    hT[:, jf * TILE + c * 128: jf * TILE + (c + 1) * 128],
                                    fwd[fb][:, lf * 512:(lf + 1) * 512],
                                    start=st, stop=(jf == FC - 1))
                            fns.append(mm)
                    p.emit(TE, fns, reads=[f"fwd{fb}", "hTp0", "hTp1"],
                           writes=[f"PB{c}" for c in range(NCH)])
                for c in range(NCH):
                    dv = out_sl[:, c * D + half * 512: c * D + (half + 1) * 512]
                    pv = x_sl[xb][:, c * D + half * 512: c * D + (half + 1) * 512]
                    p.emit(VE, lambda E, dv=dv, pv=pv, c=c: E.scalar_tensor_tensor(
                        out=dv, in0=PS[c][:], scalar=s2col[:, c:c + 1], in1=pv,
                        op0=ALU.mult, op1=ALU.add),
                        reads=[f"PB{c}", "s2col", xname], writes=["out_sl"])
                    bk_done(c)
            p.emit(SY, lambda E, ti=ti: E.dma_start(
                out=out[ti * TILE:(ti + 1) * TILE, :].rearrange("(c q) d -> q c d", q=128),
                in_=out_sl.rearrange("q (c d) -> q c d", c=NCH)),
                reads=["out_sl"], writes=[f"fdram{ti}"], sem="ST", inc=16)

        p.emit(VE, lambda E: E.memset(st_f[:, 0:H * EV], 0.0),
               reads=["st_f"], writes=["st_f"])
        for r in range(NCORE):
            p.emit(SY, lambda E, r=r: E.dma_start(
                out=shard_s[:, 0:H * EV], in_=st_all[r * 64:(r + 1) * 64, :]),
                reads=["st_all_d"], writes=["shard"], inc=16)
            p.emit(VE, lambda E, r=r: E.scalar_tensor_tensor(
                out=st_f[:, 0:H * EV], in0=shard_s[:, 0:H * EV],
                scalar=sm_s[0:64, r:r + 1], in1=st_f[:, 0:H * EV],
                op0=ALU.mult, op1=ALU.add),
                reads=["shard", "sm", "st_f"], writes=["st_f"])
        p.emit(VE, lambda E: E.memset(st_b, 0.0), reads=["st_b"], writes=["st_b"])
        p.emit(VE, lambda E: E.tensor_copy(
            st_b[0:64, :].rearrange("q (h e) -> q h e", h=H)[:, 0:H:2, :],
            st_f[:, 0:H * EV].rearrange("q (h e) -> q h e", h=H)[:, 0:H:2, :]),
            reads=["st_f", "st_b"], writes=["st_b"])
        p.emit(VE, lambda E: E.tensor_copy(
            st_b[64:128, :].rearrange("q (h e) -> q h e", h=H)[:, 1:H:2, :],
            st_f[:, 0:H * EV].rearrange("q (h e) -> q h e", h=H)[:, 1:H:2, :]),
            reads=["st_f", "st_b"], writes=["st_b"])

        p.barrier()
        cur[0] = wts_top
        qT2b = [take(4096) for _ in range(2)]
        num2b = [take(4352) for _ in range(2)]
        part2b = [take(8192, F32) for _ in range(2)]
        o_tok2 = [take(1024) for _ in range(2)]
        oT = take(4096)
        a2scr = take(544, F32)

        def a2_loads(ti):
            tb = ti % 2
            p.emit(SY, lambda E, ti=ti, tb=tb: E.dma_start(
                out=qT2b[tb], in_=qT_d[ti * 128:(ti + 1) * 128, :]),
                reads=["qT_dd"], writes=[f"qT2{tb}"], inc=16)
            p.emit(SY, lambda E, ti=ti, tb=tb: E.dma_start(
                out=num2b[tb].rearrange("q (c e) -> q c e", c=NCH),
                in_=num_d[ti * TILE:(ti + 1) * TILE, :].rearrange("(c q) e -> q c e", q=128)),
                reads=["num_dd"], writes=[f"num2{tb}"], inc=16)
            p.emit(SY, lambda E, ti=ti, tb=tb: E.dma_start(
                out=part2b[tb].rearrange("q (c d) -> q c d", c=NCH),
                in_=out[ti * TILE:(ti + 1) * TILE, :].rearrange("(c q) d -> q c d", q=128)),
                reads=[f"fdram{ti}"], writes=[f"part2{tb}"], inc=16)

        a2_loads(0)
        for ti in range(nown_tiles):
            tb = ti % 2
            qT2, num2, part2 = qT2b[tb], num2b[tb], part2b[tb]
            if ti + 1 < nown_tiles:
                a2_loads(ti + 1)
            def a2_otr(ci):
                o_tok = o_tok2[ci % 2]
                fns = []
                for jd in range(DC):
                    def tr(E, jd=jd, o_tok=o_tok):
                        return E.transpose(
                            P7[:, jd * 128:(jd + 1) * 128],
                            o_tok[:, jd * 128:(jd + 1) * 128], ident)
                    fns.append(tr)
                p.emit(TE, fns, reads=[f"o_tok{ci % 2}", "ident"], writes=["P7"])
                ov = oT.rearrange("q (jd t) -> q jd t", jd=DC)[:, :, ci * 128:(ci + 1) * 128]
                p.emit(SC, lambda E, ov=ov: E.copy(
                    ov, P7[:].rearrange("q (jd t) -> q jd t", jd=DC)),
                    reads=["P7"], writes=["oT"])
                bk_done(7)

            for ci in range(NCH):
                o_tok = o_tok2[ci % 2]
                for g in range(4):
                    nb = g
                    fns = []
                    for i in range(4):
                        h = g * 4 + i

                        def m3c(E, h=h, i=i, nb=nb, ci=ci,
                                st=(lead(nb) if (i == 0) else False)):
                            return E.matmul(
                                PS[nb][:, i * EV:i * EV + EV],
                                qT2[:, (h // 2) * TILE + ci * 128:(h // 2) * TILE + (ci + 1) * 128],
                                st_b[:, h * EV:(h + 1) * EV],
                                start=st, stop=(i == 3))
                        fns.append(m3c)
                    p.emit(TE, fns, reads=[f"qT2{tb}", "st_b"], writes=[f"PB{nb}"])
                    nsv = num2[:, ci * H * EV + g * 4 * EV: ci * H * EV + (g + 1) * 4 * EV]
                    p.emit(VE, lambda E, nb=nb, nsv=nsv: E.tensor_tensor(
                        a2scr[:, 0:4 * EV], PS[nb][:, 0:4 * EV], nsv, ALU.add),
                        reads=[f"PB{nb}", f"num2{tb}"], writes=["a2scr"])
                    bk_done(nb)
                    p.emit(VE, lambda E: E.tensor_scalar(
                        out=rcp_s[:, 0:4],
                        in0=a2scr[:, 0:4 * EV].rearrange("q (i e) -> q i e", i=4)[:, :, HD],
                        scalar1=float(ATTEPS), scalar2=None, op0=ALU.add),
                        reads=["a2scr"], writes=["rcp"])
                    p.emit(VE, lambda E: E.reciprocal(rcp_s[:, 0:4], rcp_s[:, 0:4]),
                           reads=["rcp"], writes=["rcp"])
                    ov = o_tok[:, g * 256:(g + 1) * 256].rearrange("q (i e) -> q i e", i=4)
                    p.emit(VE, lambda E, ov=ov, ci=ci: E.tensor_tensor(
                        ov, a2scr[:, 0:4 * EV].rearrange("q (i e) -> q i e", i=4)[:, :, 0:HD],
                        rcp_s[:, 0:4].rearrange("q i -> q i ()").broadcast_to([128, 4, HD]),
                        ALU.mult),
                        reads=["a2scr", "rcp"], writes=[f"o_tok{ci % 2}"])
                if ci > 0:
                    a2_otr(ci - 1)
            a2_otr(NCH - 1)
            for c in range(NCH):
                for half in range(2):
                    bank = half
                    fns = []
                    for jd in range(DC):
                        def mm(E, c=c, half=half, jd=jd, bank=bank,
                               st=(lead(bank) if (jd == 0) else False)):
                            return E.matmul(
                                PS[bank][:],
                                oT[:, jd * TILE + c * 128: jd * TILE + (c + 1) * 128],
                                wo_s[:, jd * D + half * 512: jd * D + (half + 1) * 512],
                                start=st, stop=False)
                        fns.append(mm)

                    def mmb(E, half=half, bank=bank):
                        return E.matmul(
                            PS[bank][:], ones_row, bor_s[0:1, half * 512:(half + 1) * 512],
                            start=False, stop=True)
                    fns.append(mmb)
                    p.emit(TE, fns, reads=["oT", "wo", "bo", "ones_row"], writes=[f"PB{bank}"])
                    dv = out_sl[:, c * D + half * 512: c * D + (half + 1) * 512]
                    pv = part2[:, c * D + half * 512: c * D + (half + 1) * 512]
                    p.emit(VE, lambda E, dv=dv, pv=pv, bank=bank: E.tensor_tensor(
                        dv, PS[bank][:], pv, ALU.add),
                        reads=[f"PB{bank}", f"part2{tb}"], writes=["out_sl"])
                    bk_done(bank)
            p.emit(SY, lambda E, ti=ti: E.dma_start(
                out=out[ti * TILE:(ti + 1) * TILE, :].rearrange("(c q) d -> q c d", q=128),
                in_=out_sl.rearrange("q (c d) -> q c d", c=NCH)),
                reads=["out_sl"], writes=[f"adram{ti}"], sem="ST", inc=16)

        total_st = p.cnt["ST0"]
        h_st = p.semh["ST0"]
        p.ops["sync"].append(lambda E, h=h_st, v=total_st: E.wait_ge(h, v))

        block = es.enter_context(nc.Block())

        def replay(name):
            def f(E):
                for op in p.ops[name]:
                    op(E)
            return f
        block.sync(replay("sync"))
        block.tensor(replay("tensor"))
        block.vector(replay("vector"))
        block.scalar(replay("scalar"))
        block.gpsimd(replay("gpsimd"))

    return nc


def _prep(Wq, bq, Wk, bk, Wv, bv, Wo, bo, w_up, w_down):
    f = np.float32
    return {
        "wqT": np.ascontiguousarray(np.asarray(Wq, f).T),
        "wkT": np.ascontiguousarray(np.asarray(Wk, f).T),
        "wvT": np.ascontiguousarray(np.asarray(Wv, f).T),
        "woT": np.ascontiguousarray(np.asarray(Wo, f).T),
        "bq8": np.ascontiguousarray(np.asarray(bq, f).reshape(DC, 128).T),
        "bk8": np.ascontiguousarray(np.asarray(bk, f).reshape(DC, 128).T),
        "bv_row": np.asarray(bv, f).reshape(1, D).copy(),
        "bo_row": np.asarray(bo, f).reshape(1, D).copy(),
        "wupT": np.ascontiguousarray(np.asarray(w_up, f).T),
        "wdnT": np.ascontiguousarray(np.asarray(w_down, f).T),
    }


def kernel(L0_attn_x, L0_attn_Wq, L0_attn_bq, L0_attn_Wk, L0_attn_bk,
           L0_attn_Wv, L0_attn_bv, L0_attn_Wo, L0_attn_bo,
           L0_ffn_w_up, L0_ffn_w_down):
    x = np.ascontiguousarray(np.asarray(L0_attn_x, dtype=np.float32))
    weights = _prep(L0_attn_Wq, L0_attn_bq, L0_attn_Wk, L0_attn_bk, L0_attn_Wv,
                    L0_attn_bv, L0_attn_Wo, L0_attn_bo, L0_ffn_w_up, L0_ffn_w_down)

    OWN = S // 4
    nown = OWN // TILE

    nc = build_nc(0, nown)
    in_maps = []
    for core in range(NCORE):
        b, pq = core // 4, core % 4
        xb = x[b]
        start = pq * OWN
        sm = np.zeros((128, NCORE), np.float32)
        for r in range(NCORE):
            if r // 4 == b and r % 4 < pq:
                sm[:, r] = 1.0
        m = {"x_own": np.ascontiguousarray(xb[start:start + OWN]), "smask": sm}
        m.update(weights)
        in_maps.append(m)
    res = run_bass_kernel_spmd(nc, in_maps, core_ids=list(range(NCORE)))
    outp = np.empty((B, S, D), np.float32)
    for core in range(NCORE):
        b, pq = core // 4, core % 4
        outp[b, pq * OWN:(pq + 1) * OWN] = np.asarray(res.results[core]["out"])
    return outp


# version 20
# speedup vs baseline: 1.2404x; 1.0093x over previous
import numpy as np
from contextlib import ExitStack

import concourse.bass as bass
import concourse.mybir as mybir
from concourse.bass_utils import run_bass_kernel_spmd

F32 = mybir.dt.float32
BF16 = mybir.dt.bfloat16
AF = mybir.ActivationFunctionType
ALU = mybir.AluOpType
AX = mybir.AxisListType

B, S, D, H, HD, DFF = 2, 8192, 1024, 16, 64, 4096
NCORE = 8
TILE = 512
CH = 128
EV = 68
LNEPS = 1e-5
ATTEPS = 1e-6
RMSEPS = 1e-8
NCH = TILE // CH
DC = D // 128
FC = DFF // 128

ENGS = ("sync", "tensor", "vector", "scalar", "gpsimd")
ESEM = {"tensor": "PE", "vector": "DV", "scalar": "AC", "gpsimd": "GP", "sync": "LD"}
SY, TE, VE, SC, GP = ENGS


class Prog:

    def __init__(self, nc):
        self.nc = nc
        self.ops = {e: [] for e in ENGS}
        self.semh = {}
        self.cnt = {}
        self.seen = {e: {} for e in ENGS}
        self.buf = {}
        self.dmarr = {}

    def mksem(self, es, *names):
        for n in names:
            self.semh[n] = es.enter_context(self.nc.semaphore(n))
            self.cnt[n] = 0

    def _wait(self, eng, sname, v):
        if self.seen[eng].get(sname, 0) >= v:
            return
        self.seen[eng][sname] = v
        h = self.semh[sname]
        self.ops[eng].append(lambda E, h=h, v=v: E.wait_ge(h, v))

    DMA_POOLS = {"LD": ["LD0", "LD1", "LD2", "LD3"],
                 "LDG": ["LG0", "LG1", "LG2", "LG3"],
                 "ST": ["ST0"]}

    def emit(self, eng, fns, reads=(), writes=(), sem=None, inc=1):
        if not isinstance(fns, (list, tuple)):
            fns = [fns]
        if sem is None:
            sem = ESEM[eng]
        if sem in self.DMA_POOLS:
            pool = self.DMA_POOLS[sem]
            idx = self.dmarr.get(sem, 0)
            self.dmarr[sem] = (idx + 1) % len(pool)
            sem = pool[idx]
            if self.cnt[sem] > 0:
                self._wait(eng, sem, self.cnt[sem])
        need = {}
        for b in reads:
            st = self.buf.get(b)
            if st and st["w"]:
                s, v = st["w"]
                need[s] = max(need.get(s, 0), v)
        for b in writes:
            st = self.buf.get(b)
            if st:
                toks = ([st["w"]] if st["w"] else []) + st["r"]
                for s, v in toks:
                    need[s] = max(need.get(s, 0), v)
        for s in sorted(need):
            self._wait(eng, s, need[s])
        for f in fns[:-1]:
            self.ops[eng].append(f)
        self.cnt[sem] += inc
        v = self.cnt[sem]
        h = self.semh[sem]
        last = fns[-1]
        self.ops[eng].append(lambda E, last=last, h=h, inc=inc: last(E).then_inc(h, inc))
        tok = (sem, v)
        for b in writes:
            self.buf[b] = {"w": tok, "r": []}
        for b in reads:
            st = self.buf.setdefault(b, {"w": None, "r": []})
            st["r"].append(tok)
        return tok

    def barrier(self, exclude=("CC",)):
        cur = {s: v for s, v in self.cnt.items() if v > 0 and s not in exclude}
        for e in ENGS:
            for s in sorted(cur):
                self._wait(e, s, cur[s])


def build_nc(npre_slabs, nown_tiles):
    OWN = nown_tiles * TILE
    PRE = max(npre_slabs * TILE, TILE)
    nc = bass.Bass("TRN2", target_bir_lowering=False)

    x_own = nc.dram_tensor("x_own", [OWN, D], F32, kind="ExternalInput").ap()
    smask = nc.dram_tensor("smask", [128, NCORE], F32, kind="ExternalInput").ap()
    wqT = nc.dram_tensor("wqT", [D, D], F32, kind="ExternalInput").ap()
    wkT = nc.dram_tensor("wkT", [D, D], F32, kind="ExternalInput").ap()
    wvT = nc.dram_tensor("wvT", [D, D], F32, kind="ExternalInput").ap()
    woT = nc.dram_tensor("woT", [D, D], F32, kind="ExternalInput").ap()
    bq8 = nc.dram_tensor("bq8", [128, DC], F32, kind="ExternalInput").ap()
    bk8 = nc.dram_tensor("bk8", [128, DC], F32, kind="ExternalInput").ap()
    bv_row = nc.dram_tensor("bv_row", [1, D], F32, kind="ExternalInput").ap()
    bo_row = nc.dram_tensor("bo_row", [1, D], F32, kind="ExternalInput").ap()
    wupT = nc.dram_tensor("wupT", [D, DFF], F32, kind="ExternalInput").ap()
    wdnT = nc.dram_tensor("wdnT", [DFF, D], F32, kind="ExternalInput").ap()
    out = nc.dram_tensor("out", [OWN, D], F32, kind="ExternalOutput").ap()
    wup_t = nc.dram_tensor("wup_t", [D, DFF], BF16, kind="Internal").ap()
    wdn_t = nc.dram_tensor("wdn_t", [DFF, D], BF16, kind="Internal").ap()
    qT_d = nc.dram_tensor("qT_d", [nown_tiles * 128, DC * TILE], BF16, kind="Internal").ap()
    num_d = nc.dram_tensor("num_d", [OWN, H * EV], BF16, kind="Internal").ap()
    st_loc = nc.dram_tensor("st_loc", [64, H * EV], F32, kind="Internal").ap()
    st_all = nc.dram_tensor("st_all", [NCORE * 64, H * EV], F32, kind="Internal").ap()

    with ExitStack() as es:
        pst = lambda name, shape, dt: es.enter_context(nc.psum_tensor(name, shape, dt))

        ARENA_BC = 106000
        arena = es.enter_context(nc.sbuf_tensor("arena", [128, ARENA_BC], BF16))
        cur = [0]

        def take(bcols, dt=BF16, parts=128):
            o = cur[0]
            bcols_al = (bcols + 15) // 16 * 16
            cur[0] += bcols_al
            assert cur[0] <= ARENA_BC, f"arena overflow at {cur[0]}"
            v = arena[0:parts, o:o + bcols]
            if dt == F32:
                v = v.bitcast(F32)
            return v

        x_sl = [take(8192, F32) for _ in range(2)]
        out_sl = take(8192, F32)
        xn_tok = take(4096)
        xnT = take(4096)
        st_f = take(2176, F32, parts=64)
        st_b = take(1088)
        bn_s = take(96, F32)
        mv_s = take(16, F32)
        rstd_s = take(8, F32)
        cA_s = take(8, F32)
        sffn_s = take(8, F32)
        cB_s = take(8, F32)
        lnt_s = take(32, F32)
        rcp_s = take(8, F32)
        s2col = take(8, F32)
        tpart = take(4, F32)
        tred = take(2, F32)
        thr_s = take(4, F32)
        nthr_s = take(4, F32)
        lneps_c = take(2, F32)
        mask8 = take(512)
        ident = take(128)
        ones_col = take(1)
        ones_cf = take(2, F32)
        bq_s = take(16, F32)
        bk_s = take(16, F32)
        sm_s = take(16, F32)
        shard_s = take(2176, F32, parts=64)
        ones_row = take(128, parts=1)
        ones_rf = take(256, F32, parts=1)
        ones_1f = take(2, F32, parts=1)
        bvr_s = take(1024, parts=1)
        bor_s = take(1024, parts=1)
        s2row = take(1024, F32, parts=1)
        trow_s = take(4, F32, parts=1)

        twf = [take(1024, F32) for _ in range(2)]
        tg1 = take(1024, F32)
        tout = [take(512) for _ in range(2)]
        dyn_base = cur[0]

        wo_s = take(8192)
        wts_top = cur[0]
        wq_s = take(8192)
        wk_s = take(8192)
        wv_s = take(8192)
        vtok = take(4352)
        ph_e = [take(1024, F32) for _ in range(2)]
        ph_r = [take(1024, F32) for _ in range(2)]
        var_base = cur[0]

        ktok = take(4096)
        qT = kT = ktok_c = o_tok = oT = am_s = None

        PS = [pst(f"P{i}", [128, 512], F32) for i in range(7)]
        P7 = pst("P7", [128, 1024], BF16)

        p = Prog(nc)
        p.mksem(es, "LD0", "LD1", "LD2", "LD3", "LG0", "LG1", "LG2", "LG3", "ST0", "CC", "PE", "DV", "AC", "GP")

        bank_fresh = {k: True for k in range(8)}

        def lead(i):
            f = bank_fresh[i]
            bank_fresh[i] = False
            return f

        def bk_done(i):
            bank_fresh[i] = True

        p.emit(GP, lambda E: E.memset(ident, 0.0), writes=["ident"])
        p.emit(GP, lambda E: E.affine_select(
            out=ident, in_=ident, compare_op=ALU.not_equal, fill=1.0,
            base=0, pattern=[[-1, 128]], channel_multiplier=1),
            reads=["ident"], writes=["ident"])
        p.emit(GP, lambda E: E.memset(mask8, 1.0), writes=["mask8"])
        for i in range(4):
            p.emit(GP, lambda E, i=i: E.affine_select(
                out=mask8[:, i * CH:(i + 1) * CH], in_=mask8[:, i * CH:(i + 1) * CH],
                compare_op=ALU.is_ge, fill=0.0, base=0,
                pattern=[[1, CH]], channel_multiplier=-1),
                reads=["mask8"], writes=["mask8"])
        p.emit(GP, lambda E: E.memset(lneps_c[:, 0:1], float(LNEPS)), writes=["lneps"])
        for t, n in [(ones_row, "ones_row"), (ones_col, "ones_col"),
                     (ones_rf, "ones_rf"), (ones_cf, "ones_cf"), (ones_1f, "ones_1f")]:
            p.emit(GP, lambda E, t=t: E.memset(t, 1.0), writes=[n])
        p.emit(VE, lambda E: E.memset(st_f, 0.0), writes=["st_f"])
        p.emit(VE, lambda E: E.memset(st_b, 0.0), writes=["st_b"])
        p.emit(VE, lambda E: E.memset(tpart, 0.0), writes=["tpart"])
        p.emit(VE, lambda E: E.memset(vtok, 0.0),
               writes=[f"vtok{c}" for c in range(NCH)])
        p.emit(VE, lambda E: E.memset(
            vtok.rearrange("q (c h e) -> q c h e", c=NCH, h=H)[:, :, :, HD:HD + 1], 1.0),
            reads=[f"vtok{c}" for c in range(NCH)],
            writes=[f"vtok{c}" for c in range(NCH)])

        for w_ap, tile, name in [(wqT, wq_s, "wq"), (wkT, wk_s, "wk"),
                                 (wvT, wv_s, "wv"), (woT, wo_s, "wo")]:
            p.emit(GP, lambda E, w=w_ap, t=tile: E.dma_start(
                out=t.rearrange("q (c d) -> q c d", c=DC),
                in_=w.rearrange("(c q) d -> q c d", q=128)),
                writes=[name], sem="LDG", inc=16)
        p.emit(SY, lambda E: E.dma_start(out=bq_s[:, 0:DC], in_=bq8), writes=["bq"], inc=16)
        p.emit(SY, lambda E: E.dma_start(out=bk_s[:, 0:DC], in_=bk8), writes=["bk"], inc=16)
        p.emit(GP, lambda E: E.dma_start(out=bvr_s, in_=bv_row), writes=["bv"], sem="LDG", inc=16)
        p.emit(GP, lambda E: E.dma_start(out=bor_s, in_=bo_row), writes=["bo"], sem="LDG", inc=16)
        p.emit(SY, lambda E: E.dma_start(out=sm_s[:, 0:NCORE], in_=smask),
               writes=["sm"], inc=16)

        def ln_stats(xb, xname, sffn=False):
            xs = x_sl[xb]
            for c in range(NCH):
                p.emit(VE, lambda E, c=c, xs=xs: E.bn_stats(
                    bn_s[:, c * 12:c * 12 + 6], xs[:, c * D:c * D + 512]),
                    reads=[xname], writes=[f"bn{c}"])
                p.emit(VE, lambda E, c=c, xs=xs: E.bn_stats(
                    bn_s[:, c * 12 + 6:c * 12 + 12], xs[:, c * D + 512:(c + 1) * D]),
                    reads=[xname, f"bn{c}"], writes=[f"bn{c}"])
                p.emit(VE, lambda E, c=c: E.bn_aggr(
                    mv_s[:, 2 * c:2 * c + 2], bn_s[:, c * 12:(c + 1) * 12]),
                    reads=[f"bn{c}"], writes=[f"mv{c}"])
            mvr = [f"mv{c}" for c in range(NCH)]
            p.emit(SC, lambda E: E.activation(
                lnt_s[:, 0:NCH], mv_s[:, 1:2 * NCH:2], AF.Ln, bias=lneps_c[:, 0:1], scale=1.0),
                reads=mvr + ["lneps"], writes=["lnt0"])
            p.emit(SC, lambda E: E.activation(
                rstd_s[:, 0:NCH], lnt_s[:, 0:NCH], AF.Exp, scale=-0.5),
                reads=["lnt0"], writes=["rstd"])
            p.emit(VE, lambda E: E.scalar_tensor_tensor(
                out=cA_s[:, 0:NCH], in0=mv_s[:, 0:2 * NCH:2], scalar=-1.0,
                in1=rstd_s[:, 0:NCH], op0=ALU.mult, op1=ALU.mult),
                reads=mvr + ["rstd"], writes=["cA"])
            if sffn:
                p.emit(SC, lambda E: E.activation(
                    lnt_s[:, NCH:2 * NCH], mv_s[:, 1:2 * NCH:2], AF.Ln, scale=1.0),
                    reads=mvr, writes=["lnt1"])
                p.emit(VE, lambda E: E.tensor_sub(
                    lnt_s[:, 2 * NCH:3 * NCH], lnt_s[:, NCH:2 * NCH], lnt_s[:, 0:NCH]),
                    reads=["lnt0", "lnt1"], writes=["lnt2"])
                p.emit(SC, lambda E: E.activation(
                    lnt_s[:, 3 * NCH:4 * NCH], lnt_s[:, 2 * NCH:3 * NCH], AF.Exp,
                    scale=0.5),
                    reads=["lnt2"], writes=["lnt3"])
                p.emit(VE, lambda E: E.tensor_scalar(
                    out=lnt_s[:, 3 * NCH:4 * NCH], in0=lnt_s[:, 3 * NCH:4 * NCH],
                    scalar1=float(RMSEPS), scalar2=None, op0=ALU.add),
                    reads=["lnt3"], writes=["lnt3"])
                p.emit(VE, lambda E: E.reciprocal(
                    lnt_s[:, 2 * NCH:3 * NCH], lnt_s[:, 3 * NCH:4 * NCH]),
                    reads=["lnt3"], writes=["lnt2"])
                p.emit(VE, lambda E: E.tensor_mul(
                    sffn_s[:, 0:NCH], rstd_s[:, 0:NCH], lnt_s[:, 2 * NCH:3 * NCH]),
                    reads=["rstd", "lnt2"], writes=["sffn"])
                p.emit(VE, lambda E: E.scalar_tensor_tensor(
                    out=cB_s[:, 0:NCH], in0=mv_s[:, 0:2 * NCH:2], scalar=-1.0,
                    in1=sffn_s[:, 0:NCH], op0=ALU.mult, op1=ALU.mult),
                    reads=mvr + ["sffn"], writes=["cB"])

        def xn_apply(xb, xname, scale_t, scale_name, shift_t, shift_name):
            xs = x_sl[xb]
            for c in range(NCH):
                p.emit(SC, lambda E, c=c, xs=xs: E.activation(
                    xn_tok[:, c * D:(c + 1) * D], xs[:, c * D:(c + 1) * D], AF.Identity,
                    bias=shift_t[:, c:c + 1], scale=scale_t[:, c:c + 1]),
                    reads=[xname, scale_name, shift_name], writes=["xn_tok"])

        def transpose_tok_to_ft(src, src_name, dst, dst_name):
            for c in range(NCH):
                fns = []
                for jm in range(DC):
                    def tr(E, c=c, jm=jm):
                        return E.transpose(
                            P7[:, jm * 128:(jm + 1) * 128],
                            src[:, c * D + jm * 128:c * D + (jm + 1) * 128], ident)
                    fns.append(tr)
                p.emit(TE, fns, reads=[src_name, "ident"], writes=["P7"])
                ov = dst.rearrange("q (jm t) -> q jm t", jm=DC)[:, :, c * 128:(c + 1) * 128]
                p.emit(SC, lambda E, ov=ov: E.copy(
                    ov, P7[:].rearrange("q (jm t) -> q jm t", jm=DC)),
                    reads=["P7"], writes=[dst_name])
                bk_done(7)

        def transpose_ft_to_tok(src, src_name, dst, dst_name):
            for c in range(NCH):
                fns = []
                for jd in range(DC):
                    def tr(E, c=c, jd=jd):
                        return E.transpose(
                            P7[:, jd * 128:(jd + 1) * 128],
                            src[:, jd * TILE + c * 128:jd * TILE + (c + 1) * 128], ident)
                    fns.append(tr)
                p.emit(TE, fns, reads=[src_name, "ident"], writes=["P7"])
                p.emit(VE, lambda E, c=c: E.tensor_copy(
                    dst[:, c * D:(c + 1) * D], P7[:]),
                    reads=["P7"], writes=[dst_name])
                bk_done(7)

        def proj_feature(wtile, wname, btile, bname, dst, dst_name):
            for jd in range(DC):
                bank = jd % 4
                pp = jd % 2
                fns = []
                for jm in range(DC):
                    def mm(E, jd=jd, jm=jm, bank=bank, st=(lead(bank) if (jm == 0) else False)):
                        return E.matmul(
                            PS[bank][:],
                            wtile[:, jm * D + jd * 128: jm * D + (jd + 1) * 128],
                            xnT[:, jm * TILE:(jm + 1) * TILE],
                            start=st, stop=(jm == DC - 1))
                    fns.append(mm)
                p.emit(TE, fns, reads=[wname, "xnT"], writes=[f"PB{bank}"])
                dv = dst[:, jd * TILE:(jd + 1) * TILE]
                p.emit(SC, lambda E, jd=jd, bank=bank, pp=pp: E.activation(
                    ph_e[pp][:, 0:512], PS[bank][:], AF.Exp, bias=btile[:, jd:jd + 1],
                    scale=1.0),
                    reads=[f"PB{bank}", bname], writes=[f"ph_e{pp}"])
                p.emit(SC, lambda E, jd=jd, bank=bank, pp=pp: E.activation(
                    ph_r[pp][:, 0:512], PS[bank][:], AF.Relu, bias=btile[:, jd:jd + 1],
                    scale=1.0),
                    reads=[f"PB{bank}", bname], writes=[f"ph_r{pp}"])
                p.emit(VE, lambda E, dv=dv, pp=pp: E.scalar_tensor_tensor(
                    out=dv, in0=ph_e[pp][:, 0:512], scalar=1.0, in1=ph_r[pp][:, 0:512],
                    op0=ALU.min, op1=ALU.add),
                    reads=[f"ph_e{pp}", f"ph_r{pp}"], writes=[dst_name])
                bk_done(bank)
                tern_tick()

        def proj_token(wtile, wname, brow, bname, writer, banks):
            for c in range(NCH):
                for half in range(2):
                    bank = banks[(c * 2 + half) % len(banks)]
                    fns = []
                    for jm in range(DC):
                        def mm(E, c=c, half=half, jm=jm, bank=bank, st=(lead(bank) if (jm == 0) else False)):
                            return E.matmul(
                                PS[bank][:],
                                xnT[:, jm * TILE + c * 128: jm * TILE + (c + 1) * 128],
                                wtile[:, jm * D + half * 512: jm * D + (half + 1) * 512],
                                start=st, stop=False)
                        fns.append(mm)

                    def mmb(E, half=half, bank=bank):
                        return E.matmul(
                            PS[bank][:], ones_row, brow[0:1, half * 512:(half + 1) * 512],
                            start=False, stop=True)
                    fns.append(mmb)
                    p.emit(TE, fns, reads=[wname, "xnT", bname, "ones_row"],
                           writes=[f"PB{bank}"])
                    writer(c, half, bank)
                    bk_done(bank)
                    tern_tick()

        SDSEG = [(4, 0, 6), (5, 6, 11), (6, 11, 16)]

        def state_fold():
            for bank, h0, h1 in SDSEG:
                p.emit(VE, lambda E, bank=bank, h0=h0, h1=h1: E.tensor_tensor(
                    st_f[:, h0 * EV:h1 * EV], st_f[:, h0 * EV:h1 * EV],
                    PS[bank][0:64, 0:(h1 - h0) * EV], ALU.add),
                    reads=[f"PB{bank}", "st_f"], writes=["st_f"])
                bk_done(bank)
            p.emit(VE, lambda E: E.tensor_copy(
                st_b[0:64, :].rearrange("q (h e) -> q h e", h=H)[:, 0:H:2, :],
                st_f[:, 0:H * EV].rearrange("q (h e) -> q h e", h=H)[:, 0:H:2, :]),
                reads=["st_f"], writes=["st_b"])
            p.emit(VE, lambda E: E.tensor_copy(
                st_b[64:128, :].rearrange("q (h e) -> q h e", h=H)[:, 1:H:2, :],
                st_f[:, 0:H * EV].rearrange("q (h e) -> q h e", h=H)[:, 1:H:2, :]),
                reads=["st_f"], writes=["st_b"])

        scanref = {}

        def scan_pre(ci):
            qT = scanref["qT"]; kT = scanref["kT"]
            ktok_c = scanref["ktok_c"][ci % 2]
            am_s = scanref["am2"][ci % 2]
            fns = []
            for jd in range(DC):
                def tr(E, jd=jd, ci=ci):
                    return E.transpose(
                        P7[:, jd * 128:(jd + 1) * 128],
                        kT[:, jd * TILE + ci * 128:jd * TILE + (ci + 1) * 128], ident)
                fns.append(tr)
            p.emit(TE, fns, reads=["kT", "ident"], writes=["P7"])
            p.emit(SC, lambda E, ktok_c=ktok_c: E.copy(ktok_c[:, 0:D], P7[:]),
                   reads=["P7"], writes=[f"ktokc{ci % 2}"])
            bk_done(7)
            fns = []
            wr = set()
            for h in range(H):
                bank = h % 4
                col = (h // 4) * 128

                def m1(E, h=h, bank=bank, col=col, ci=ci,
                       st=(lead(bank) if (h // 4 == 0) else False)):
                    return E.matmul(
                        PS[bank][:, col:col + 128],
                        kT[(h % 2) * 64:(h % 2) * 64 + 64,
                           (h // 2) * TILE + ci * 128:(h // 2) * TILE + (ci + 1) * 128],
                        qT[(h % 2) * 64:(h % 2) * 64 + 64,
                           (h // 2) * TILE + ci * 128:(h // 2) * TILE + (ci + 1) * 128],
                        start=st, stop=(h >= H - 4))
                fns.append(m1)
                wr.add(f"PB{bank}")
            p.emit(TE, fns, reads=["kT", "qT"], writes=sorted(wr))
            for bank in range(4):
                ov = am_s.rearrange("q (g b c) -> q g b c", g=4, b=4)[:, :, bank, :]
                p.emit(VE, lambda E, bank=bank, ov=ov: E.tensor_tensor(
                    ov, PS[bank][:].rearrange("q (g c) -> q g c", g=4),
                    mask8.rearrange("q (g c) -> q g c", g=4), ALU.mult),
                    reads=[f"PB{bank}", "mask8"], writes=[f"am{ci % 2}_{bank}"])
                bk_done(bank)

        def scan_m2pair(ci, pair):
            am_s = scanref["am2"][ci % 2]
            for g in (2 * pair, 2 * pair + 1):
                nb = 4 + (g % 2)
                fns = []
                for i in range(4):
                    h = g * 4 + i

                    def m2(E, h=h, i=i, nb=nb, ci=ci, am_s=am_s,
                           st=(lead(nb) if (i == 0) else False)):
                        return E.matmul(
                            PS[nb][:, i * EV:i * EV + EV],
                            am_s[:, h * 128:(h + 1) * 128],
                            vtok[:, ci * H * EV + h * EV: ci * H * EV + (h + 1) * EV],
                            start=st, stop=False)
                    fns.append(m2)
                p.emit(TE, fns,
                       reads=[f"am{ci % 2}_{b}" for b in range(4)] + [f"vtok{ci}"],
                       writes=[f"PB{nb}"])

        def scan_m3save(ci, pair):
            qT = scanref["qT"]; numsave = scanref["numsave"]
            for g in (2 * pair, 2 * pair + 1):
                nb = 4 + (g % 2)
                fns = []
                for i in range(4):
                    h = g * 4 + i

                    def m3(E, h=h, i=i, nb=nb, ci=ci):
                        return E.matmul(
                            PS[nb][:, i * EV:i * EV + EV],
                            qT[:, (h // 2) * TILE + ci * 128:(h // 2) * TILE + (ci + 1) * 128],
                            st_b[:, h * EV:(h + 1) * EV],
                            start=False, stop=(i == 3))
                    fns.append(m3)
                p.emit(TE, fns, reads=["qT", "st_b"], writes=[f"PB{nb}"])
                p.emit(SC, lambda E, nb=nb, g=g, ci=ci: E.copy(
                    numsave[:, (ci % 2) * H * EV + g * 4 * EV:
                            (ci % 2) * H * EV + (g + 1) * 4 * EV],
                    PS[nb][:, 0:4 * EV]),
                    reads=[f"PB{nb}"], writes=[f"nsv{ci}_{g}"])
                bk_done(nb)
                tern_tick()

        def scan_tail(ci):
            ktok_c = scanref["ktok_c"][ci % 2]
            numsave = scanref["numsave"]
            fns = []
            wr = set()
            for bank, h0, h1 in SDSEG:
                for i in range(h1 - h0):
                    h = h0 + i

                    def m4(E, h=h, i=i, bank=bank, ci=ci, ktok_c=ktok_c,
                           last=(i == h1 - h0 - 1), st=(lead(bank) if (i == 0) else False)):
                        return E.matmul(
                            PS[bank][0:64, i * EV:i * EV + EV],
                            ktok_c[:, h * HD:(h + 1) * HD],
                            vtok[:, ci * H * EV + h * EV: ci * H * EV + (h + 1) * EV],
                            start=st, stop=last)
                    fns.append(m4)
                    wr.add(f"PB{bank}")
            p.emit(TE, fns, reads=[f"ktokc{ci % 2}", f"vtok{ci}"], writes=sorted(wr))
            state_fold()
            if ci % 2 == 1:
                ti = scanref["ti"]
                base = ti * TILE + (ci - 1) * 128
                p.emit(SY, lambda E, base=base: E.dma_start(
                    out=num_d[base:base + 256, :].rearrange("(c q) e -> q c e", q=128),
                    in_=numsave.rearrange("q (c e) -> q c e", c=2)),
                    reads=[f"nsv{c}_{g}" for c in (ci - 1, ci) for g in range(4)],
                    writes=["num_dd"], sem="LD", inc=16)

        tern_chunks = [("up", j) for j in range(64)] + [("dn", j) for j in range(64)]

        def tern_load(kind, j, b):
            if kind == "up":
                src = wupT[(j // 8) * 128:(j // 8 + 1) * 128,
                           (j % 8) * 512:(j % 8 + 1) * 512]
            else:
                src = wdnT[(j // 2) * 128:(j // 2 + 1) * 128,
                           (j % 2) * 512:(j % 2 + 1) * 512]
            p.emit(GP, lambda E, s=src, b=b: E.dma_start(out=twf[b], in_=s),
                   writes=[f"twf{b}"], sem="LDG", inc=16)

        def tern_abs(kind, j, b):
            col = 0 if kind == "up" else 1
            p.emit(SC, lambda E, b=b: E.activation(
                tg1, twf[b], AF.Abs, accum_out=tred[:, 0:1]),
                reads=[f"twf{b}"], writes=["tred", "tg1"])
            p.emit(VE, lambda E, col=col: E.tensor_tensor(
                tpart[:, col:col + 1], tpart[:, col:col + 1], tred[:, 0:1], ALU.add),
                reads=["tred", "tpart"], writes=["tpart"])

        def tern_finalize():
            p.emit(TE, lambda E, st=lead(6): E.matmul(
                PS[6][0:1, 0:2], ones_cf[:, 0:1], tpart[:, 0:2], start=st, stop=True),
                reads=["tpart", "ones_cf"], writes=["PB6"])
            p.emit(SC, lambda E: E.activation(
                trow_s[0:1, 0:2], PS[6][0:1, 0:2], AF.Copy,
                scale=float(0.5 / (D * DFF))),
                reads=["PB6"], writes=["trow"])
            bk_done(6)
            p.emit(TE, lambda E, st=lead(6): E.matmul(
                PS[6][:, 0:2], ones_rf[0:1, 0:128], trow_s[0:1, 0:2],
                start=st, stop=True),
                reads=["trow", "ones_rf"], writes=["PB6"])
            p.emit(VE, lambda E: E.tensor_copy(thr_s[:, 0:2], PS[6][:, 0:2]),
                   reads=["PB6"], writes=["thr"])
            p.emit(VE, lambda E: E.tensor_scalar(
                out=nthr_s[:, 0:2], in0=thr_s[:, 0:2], scalar1=-1.0, scalar2=None,
                op0=ALU.mult),
                reads=["thr"], writes=["nthr"])
            bk_done(6)

        def tern_quant(kind, j, b):
            col = 0 if kind == "up" else 1
            p.emit(VE, lambda E, b=b, col=col: E.tensor_scalar(
                out=tg1, in0=twf[b], scalar1=nthr_s[:, col:col + 1], scalar2=None,
                op0=ALU.is_lt),
                reads=[f"twf{b}", "nthr"], writes=["tg1"])
            p.emit(VE, lambda E, b=b, col=col: E.scalar_tensor_tensor(
                out=tout[b], in0=twf[b], scalar=thr_s[:, col:col + 1], in1=tg1,
                op0=ALU.is_gt, op1=ALU.subtract),
                reads=[f"twf{b}", "thr", "tg1"], writes=[f"tout{b}"])
            if kind == "up":
                dst = wup_t[(j // 8) * 128:(j // 8 + 1) * 128,
                            (j % 8) * 512:(j % 8 + 1) * 512]
                scr = "scr_up"
            else:
                dst = wdn_t[(j // 2) * 128:(j // 2 + 1) * 128,
                            (j % 2) * 512:(j % 2 + 1) * 512]
                scr = "scr_dn"
            p.emit(SY, lambda E, d=dst, b=b: E.dma_start(out=d, in_=tout[b]),
                   reads=[f"tout{b}"], writes=[scr], sem="LD", inc=16)

        tst = {"a": 0, "b": 0, "fin": False}
        quant_done = set()

        def tern_step(n):
            for _ in range(n):
                if tst["a"] < len(tern_chunks):
                    k, j = tern_chunks[tst["a"]]
                    tern_load(k, j, tst["b"])
                    tern_abs(k, j, tst["b"])
                    tst["a"] += 1
                    tst["b"] ^= 1

        def tern_abs_flush():
            while tst["a"] < len(tern_chunks):
                tern_step(1)
            if not tst["fin"]:
                tern_finalize()
                tst["fin"] = True

        def tern_quant_chunks(chunks):
            for k, j in chunks:
                if (k, j) in quant_done:
                    continue
                quant_done.add((k, j))
                tern_load(k, j, tst["b"])
                tern_quant(k, j, tst["b"])
                tst["b"] ^= 1

        def tern_tick(n=2):
            if tst["a"] < len(tern_chunks):
                tern_step(n)
            elif not tst["fin"]:
                tern_finalize()
                tst["fin"] = True
            else:
                tern_quant_step(n)

        qsched = {"i": 0}

        def tern_quant_step(n):
            c = 0
            while qsched["i"] < len(tern_chunks) and c < n:
                k, j = tern_chunks[qsched["i"]]
                qsched["i"] += 1
                if (k, j) in quant_done:
                    continue
                tern_quant_chunks([(k, j)])
                c += 1

        p.barrier()
        cur[0] = var_base
        qT = take(4096)
        kT = take(4096)
        ktok_c = [take(1024) for _ in range(2)]
        am2 = [take(2048) for _ in range(2)]
        numsave = take(2176)
        scanref["qT"] = qT
        scanref["kT"] = kT
        scanref["ktok_c"] = ktok_c
        scanref["am2"] = am2
        scanref["numsave"] = numsave
        def a1_load_stats(ti):
            xb = ti % 2
            xname = f"x{xb}"
            p.emit(SY, lambda E, ti=ti, xb=xb: E.dma_start(
                out=x_sl[xb].rearrange("q (c d) -> q c d", c=NCH),
                in_=x_own[ti * TILE:(ti + 1) * TILE, :].rearrange("(c q) d -> q c d", q=128)),
                writes=[xname], inc=16)
            ln_stats(xb, xname)

        a1_load_stats(0)
        for ti in range(nown_tiles):
            xb = ti % 2
            xname = f"x{xb}"
            xn_apply(xb, xname, rstd_s, "rstd", cA_s, "cA")
            transpose_tok_to_ft(xn_tok, "xn_tok", xnT, "xnT")
            proj_feature(wq_s, "wq", bq_s, "bq", qT, "qT")
            proj_feature(wk_s, "wk", bk_s, "bk", kT, "kT")
            scanref["ti"] = ti
            p.emit(SY, lambda E, ti=ti: E.dma_start(
                out=qT_d[ti * 128:(ti + 1) * 128, :], in_=qT),
                reads=["qT"], writes=["qT_dd"], sem="LD", inc=16)
            if ti + 1 < nown_tiles:
                a1_load_stats(ti + 1)

            def v_writer(c, half, bank):
                ov = vtok.rearrange("q (c h e) -> q c h e", c=NCH, h=H)[
                    :, c, half * 8:(half + 1) * 8, 0:HD]
                p.emit(SC, lambda E, ov=ov, bank=bank: E.copy(
                    ov, PS[bank][:].rearrange("q (h e) -> q h e", h=8)),
                    reads=[f"PB{bank}"], writes=[f"vtok{c}"])
            proj_token(wv_s, "wv", bvr_s, "bv", v_writer, [0, 1, 2, 3])

            scan_pre(0)
            for ci in range(NCH):
                scan_m2pair(ci, 0)
                if ci < NCH - 1:
                    scan_pre(ci + 1)
                scan_m3save(ci, 0)
                scan_m2pair(ci, 1)
                scan_m3save(ci, 1)
                scan_tail(ci)

        tern_abs_flush()
        tern_quant_step(len(tern_chunks))

        p.emit(SY, lambda E: E.dma_start(out=st_loc, in_=st_f[:, 0:H * EV]),
               reads=["st_f"], writes=["st_loc_d"], sem="LD", inc=16)
        p.emit(GP, lambda E: E.collective_compute(
            "AllGather", ALU.bypass,
            replica_groups=[list(range(NCORE))],
            ins=[st_loc.opt()], outs=[st_all.opt()]),
            reads=["st_loc_d"], writes=["st_all_d"], sem="CC", inc=1)
        p.barrier()
        cur[0] = wts_top
        hT = take(16384)
        fwu = [take(4096) for _ in range(2)]
        fwd = [take(2048) for _ in range(2)]
        hsq = [take(512) for _ in range(2)]

        def b_load_stats(ti):
            xb = ti % 2
            xname = f"x{xb}"
            p.emit(SY, lambda E, ti=ti, xb=xb: E.dma_start(
                out=x_sl[xb].rearrange("q (c d) -> q c d", c=NCH),
                in_=x_own[ti * TILE:(ti + 1) * TILE, :].rearrange("(c q) d -> q c d", q=128)),
                writes=[xname], inc=16)
            ln_stats(xb, xname, sffn=True)

        b_load_stats(0)
        for ti in range(nown_tiles):
            xb = ti % 2
            xname = f"x{xb}"
            xn_apply(xb, xname, sffn_s, "sffn", cB_s, "cB")
            transpose_tok_to_ft(xn_tok, "xn_tok", xnT, "xnT")

            def fwu_load(fs):
                fb = fs % 2
                tern_quant_chunks([("up", jm * 8 + fs) for jm in range(DC)])
                srcv = wup_t.rearrange("(jm q) f -> q jm f", q=128)[:, :, fs * 512:(fs + 1) * 512]
                p.emit(GP, lambda E, srcv=srcv, fb=fb: E.dma_start(
                    out=fwu[fb].rearrange("q (jm f) -> q jm f", jm=DC), in_=srcv),
                    reads=["scr_up"], writes=[f"fwu{fb}"], sem="LDG", inc=16)

            for fs in range(DFF // 512):
                fb = fs % 2
                fwu_load(fs)
                for lf in range(4):
                    jf = fs * 4 + lf
                    bank = jf % 4
                    fns = []
                    for jm in range(DC):
                        def mm(E, jm=jm, lf=lf, fb=fb, bank=bank, st=(lead(bank) if (jm == 0) else False)):
                            return E.matmul(
                                PS[bank][:],
                                fwu[fb][:, jm * 512 + lf * 128: jm * 512 + (lf + 1) * 128],
                                xnT[:, jm * TILE:(jm + 1) * TILE],
                                start=st, stop=(jm == DC - 1))
                        fns.append(mm)
                    p.emit(TE, fns, reads=[f"fwu{fb}", "xnT"], writes=[f"PB{bank}"])
                    if jf > 0:
                        pj, phb = jf - 1, (jf - 1) % 2
                        p.emit(TE, lambda E, pj=pj, phb=phb,
                               st=(lead(6) if pj == 0 else False): E.matmul(
                            PS[6][0:1, :], ones_col, hsq[phb],
                            start=st, stop=False),
                            reads=[f"hsq{phb}", "ones_col"], writes=["PB6"])
                    p.emit(SC, lambda E, jf=jf, bank=bank: E.activation(
                        hT[:, jf * TILE:(jf + 1) * TILE], PS[bank][:], AF.Gelu),
                        reads=[f"PB{bank}"], writes=[f"hTp{jf % 2}"])
                    hb = jf % 2
                    p.emit(VE, lambda E, jf=jf, hb=hb: E.tensor_mul(
                        hsq[hb], hT[:, jf * TILE:(jf + 1) * TILE],
                        hT[:, jf * TILE:(jf + 1) * TILE]),
                        reads=[f"hTp{jf % 2}"], writes=[f"hsq{hb}"])
                    bk_done(bank)
            p.emit(TE, lambda E: E.matmul(
                PS[6][0:1, :], ones_col, hsq[(FC - 1) % 2],
                start=False, stop=True),
                reads=[f"hsq{(FC - 1) % 2}", "ones_col"], writes=["PB6"])
            if ti + 1 < nown_tiles:
                b_load_stats(ti + 1)
            p.emit(SC, lambda E: E.activation(
                s2row[0:1, 0:TILE], PS[6][0:1, :], AF.Ln, scale=float(1.0 / DFF)),
                reads=["PB6"], writes=["s2row"])
            bk_done(6)
            p.emit(SC, lambda E: E.activation(
                s2row[0:1, 0:TILE], s2row[0:1, 0:TILE], AF.Exp, scale=0.5),
                reads=["s2row"], writes=["s2row"])
            p.emit(VE, lambda E: E.tensor_scalar(
                out=s2row[0:1, 0:TILE], in0=s2row[0:1, 0:TILE], scalar1=float(RMSEPS),
                scalar2=None, op0=ALU.add),
                reads=["s2row"], writes=["s2row"])
            p.emit(VE, lambda E: E.reciprocal(s2row[0:1, 0:TILE], s2row[0:1, 0:TILE]),
                   reads=["s2row"], writes=["s2row"])
            fns = []
            for c in range(NCH):
                def mm(E, c=c, st=(lead(6) if (c == 0) else False)):
                    return E.matmul(
                        PS[6][:, c:c + 1], s2row[0:1, c * 128:(c + 1) * 128],
                        ones_1f[0:1, 0:1], start=st, stop=(c == NCH - 1))
                fns.append(mm)
            p.emit(TE, fns, reads=["s2row", "ones_1f"], writes=["PB6"])
            p.emit(VE, lambda E: E.tensor_copy(s2col[:, 0:NCH], PS[6][:, 0:NCH]),
                   reads=["PB6"], writes=["s2col"])
            bk_done(6)

            NDS = DFF // 512

            def fwd_load(idx):
                half, fs = idx // NDS, idx % NDS
                fb = idx % 2
                tern_quant_chunks([("dn", (fs * 4 + r) * 2 + half) for r in range(4)])
                srcv = wdn_t.rearrange("(jf q) d -> q jf d", q=128)[
                    :, fs * 4:(fs + 1) * 4, half * 512:(half + 1) * 512]
                p.emit(GP, lambda E, srcv=srcv, fb=fb: E.dma_start(
                    out=fwd[fb].rearrange("q (jf d) -> q jf d", jf=4), in_=srcv),
                    reads=["scr_dn"], writes=[f"fwd{fb}"], sem="LDG", inc=16)

            for half in range(2):
                for fs in range(DFF // 512):
                    idx = half * NDS + fs
                    fb = idx % 2
                    fwd_load(idx)
                    fns = []
                    for lf in range(4):
                        jf = fs * 4 + lf
                        for c in range(NCH):
                            def mm(E, jf=jf, lf=lf, c=c, fb=fb, st=(lead(c) if (jf == 0) else False)):
                                return E.matmul(
                                    PS[c][:],
                                    hT[:, jf * TILE + c * 128: jf * TILE + (c + 1) * 128],
                                    fwd[fb][:, lf * 512:(lf + 1) * 512],
                                    start=st, stop=(jf == FC - 1))
                            fns.append(mm)
                    p.emit(TE, fns, reads=[f"fwd{fb}", "hTp0", "hTp1"],
                           writes=[f"PB{c}" for c in range(NCH)])
                for c in range(NCH):
                    dv = out_sl[:, c * D + half * 512: c * D + (half + 1) * 512]
                    pv = x_sl[xb][:, c * D + half * 512: c * D + (half + 1) * 512]
                    p.emit(VE, lambda E, dv=dv, pv=pv, c=c: E.scalar_tensor_tensor(
                        out=dv, in0=PS[c][:], scalar=s2col[:, c:c + 1], in1=pv,
                        op0=ALU.mult, op1=ALU.add),
                        reads=[f"PB{c}", "s2col", xname], writes=["out_sl"])
                    bk_done(c)
            p.emit(SY, lambda E, ti=ti: E.dma_start(
                out=out[ti * TILE:(ti + 1) * TILE, :].rearrange("(c q) d -> q c d", q=128),
                in_=out_sl.rearrange("q (c d) -> q c d", c=NCH)),
                reads=["out_sl"], writes=[f"fdram{ti}"], sem="ST", inc=16)

        p.emit(VE, lambda E: E.memset(st_f[:, 0:H * EV], 0.0),
               reads=["st_f"], writes=["st_f"])
        for r in range(NCORE):
            p.emit(SY, lambda E, r=r: E.dma_start(
                out=shard_s[:, 0:H * EV], in_=st_all[r * 64:(r + 1) * 64, :]),
                reads=["st_all_d"], writes=["shard"], inc=16)
            p.emit(VE, lambda E, r=r: E.scalar_tensor_tensor(
                out=st_f[:, 0:H * EV], in0=shard_s[:, 0:H * EV],
                scalar=sm_s[0:64, r:r + 1], in1=st_f[:, 0:H * EV],
                op0=ALU.mult, op1=ALU.add),
                reads=["shard", "sm", "st_f"], writes=["st_f"])
        p.emit(VE, lambda E: E.memset(st_b, 0.0), reads=["st_b"], writes=["st_b"])
        p.emit(VE, lambda E: E.tensor_copy(
            st_b[0:64, :].rearrange("q (h e) -> q h e", h=H)[:, 0:H:2, :],
            st_f[:, 0:H * EV].rearrange("q (h e) -> q h e", h=H)[:, 0:H:2, :]),
            reads=["st_f", "st_b"], writes=["st_b"])
        p.emit(VE, lambda E: E.tensor_copy(
            st_b[64:128, :].rearrange("q (h e) -> q h e", h=H)[:, 1:H:2, :],
            st_f[:, 0:H * EV].rearrange("q (h e) -> q h e", h=H)[:, 1:H:2, :]),
            reads=["st_f", "st_b"], writes=["st_b"])

        p.barrier()
        cur[0] = wts_top
        qT2b = [take(4096) for _ in range(2)]
        num2b = [take(4352) for _ in range(2)]
        part2b = [take(8192, F32) for _ in range(2)]
        o_tok2 = [take(1024) for _ in range(2)]
        oT = take(4096)
        a2scr = take(544, F32)

        def a2_loads(ti):
            tb = ti % 2
            p.emit(SY, lambda E, ti=ti, tb=tb: E.dma_start(
                out=qT2b[tb], in_=qT_d[ti * 128:(ti + 1) * 128, :]),
                reads=["qT_dd"], writes=[f"qT2{tb}"], inc=16)
            p.emit(SY, lambda E, ti=ti, tb=tb: E.dma_start(
                out=num2b[tb].rearrange("q (c e) -> q c e", c=NCH),
                in_=num_d[ti * TILE:(ti + 1) * TILE, :].rearrange("(c q) e -> q c e", q=128)),
                reads=["num_dd"], writes=[f"num2{tb}"], inc=16)
            p.emit(SY, lambda E, ti=ti, tb=tb: E.dma_start(
                out=part2b[tb].rearrange("q (c d) -> q c d", c=NCH),
                in_=out[ti * TILE:(ti + 1) * TILE, :].rearrange("(c q) d -> q c d", q=128)),
                reads=[f"fdram{ti}"], writes=[f"part2{tb}"], inc=16)

        a2_loads(0)
        for ti in range(nown_tiles):
            tb = ti % 2
            qT2, num2, part2 = qT2b[tb], num2b[tb], part2b[tb]
            if ti + 1 < nown_tiles:
                a2_loads(ti + 1)
            def a2_otr(ci):
                o_tok = o_tok2[ci % 2]
                fns = []
                for jd in range(DC):
                    def tr(E, jd=jd, o_tok=o_tok):
                        return E.transpose(
                            P7[:, jd * 128:(jd + 1) * 128],
                            o_tok[:, jd * 128:(jd + 1) * 128], ident)
                    fns.append(tr)
                p.emit(TE, fns, reads=[f"o_tok{ci % 2}", "ident"], writes=["P7"])
                ov = oT.rearrange("q (jd t) -> q jd t", jd=DC)[:, :, ci * 128:(ci + 1) * 128]
                p.emit(SC, lambda E, ov=ov: E.copy(
                    ov, P7[:].rearrange("q (jd t) -> q jd t", jd=DC)),
                    reads=["P7"], writes=["oT"])
                bk_done(7)

            for ci in range(NCH):
                o_tok = o_tok2[ci % 2]
                for g in range(4):
                    nb = g
                    fns = []
                    for i in range(4):
                        h = g * 4 + i

                        def m3c(E, h=h, i=i, nb=nb, ci=ci,
                                st=(lead(nb) if (i == 0) else False)):
                            return E.matmul(
                                PS[nb][:, i * EV:i * EV + EV],
                                qT2[:, (h // 2) * TILE + ci * 128:(h // 2) * TILE + (ci + 1) * 128],
                                st_b[:, h * EV:(h + 1) * EV],
                                start=st, stop=(i == 3))
                        fns.append(m3c)
                    p.emit(TE, fns, reads=[f"qT2{tb}", "st_b"], writes=[f"PB{nb}"])
                    nsv = num2[:, ci * H * EV + g * 4 * EV: ci * H * EV + (g + 1) * 4 * EV]
                    p.emit(VE, lambda E, nb=nb, nsv=nsv: E.tensor_tensor(
                        a2scr[:, 0:4 * EV], PS[nb][:, 0:4 * EV], nsv, ALU.add),
                        reads=[f"PB{nb}", f"num2{tb}"], writes=["a2scr"])
                    bk_done(nb)
                    p.emit(VE, lambda E: E.tensor_scalar(
                        out=rcp_s[:, 0:4],
                        in0=a2scr[:, 0:4 * EV].rearrange("q (i e) -> q i e", i=4)[:, :, HD],
                        scalar1=float(ATTEPS), scalar2=None, op0=ALU.add),
                        reads=["a2scr"], writes=["rcp"])
                    p.emit(VE, lambda E: E.reciprocal(rcp_s[:, 0:4], rcp_s[:, 0:4]),
                           reads=["rcp"], writes=["rcp"])
                    ov = o_tok[:, g * 256:(g + 1) * 256].rearrange("q (i e) -> q i e", i=4)
                    p.emit(VE, lambda E, ov=ov, ci=ci: E.tensor_tensor(
                        ov, a2scr[:, 0:4 * EV].rearrange("q (i e) -> q i e", i=4)[:, :, 0:HD],
                        rcp_s[:, 0:4].rearrange("q i -> q i ()").broadcast_to([128, 4, HD]),
                        ALU.mult),
                        reads=["a2scr", "rcp"], writes=[f"o_tok{ci % 2}"])
                if ci > 0:
                    a2_otr(ci - 1)
            a2_otr(NCH - 1)
            for c in range(NCH):
                for half in range(2):
                    bank = half
                    fns = []
                    for jd in range(DC):
                        def mm(E, c=c, half=half, jd=jd, bank=bank,
                               st=(lead(bank) if (jd == 0) else False)):
                            return E.matmul(
                                PS[bank][:],
                                oT[:, jd * TILE + c * 128: jd * TILE + (c + 1) * 128],
                                wo_s[:, jd * D + half * 512: jd * D + (half + 1) * 512],
                                start=st, stop=False)
                        fns.append(mm)

                    def mmb(E, half=half, bank=bank):
                        return E.matmul(
                            PS[bank][:], ones_row, bor_s[0:1, half * 512:(half + 1) * 512],
                            start=False, stop=True)
                    fns.append(mmb)
                    p.emit(TE, fns, reads=["oT", "wo", "bo", "ones_row"], writes=[f"PB{bank}"])
                    dv = out_sl[:, c * D + half * 512: c * D + (half + 1) * 512]
                    pv = part2[:, c * D + half * 512: c * D + (half + 1) * 512]
                    p.emit(VE, lambda E, dv=dv, pv=pv, bank=bank: E.tensor_tensor(
                        dv, PS[bank][:], pv, ALU.add),
                        reads=[f"PB{bank}", f"part2{tb}"], writes=["out_sl"])
                    bk_done(bank)
            p.emit(SY, lambda E, ti=ti: E.dma_start(
                out=out[ti * TILE:(ti + 1) * TILE, :].rearrange("(c q) d -> q c d", q=128),
                in_=out_sl.rearrange("q (c d) -> q c d", c=NCH)),
                reads=["out_sl"], writes=[f"adram{ti}"], sem="ST", inc=16)

        total_st = p.cnt["ST0"]
        h_st = p.semh["ST0"]
        p.ops["sync"].append(lambda E, h=h_st, v=total_st: E.wait_ge(h, v))

        block = es.enter_context(nc.Block())

        def replay(name):
            def f(E):
                for op in p.ops[name]:
                    op(E)
            return f
        block.sync(replay("sync"))
        block.tensor(replay("tensor"))
        block.vector(replay("vector"))
        block.scalar(replay("scalar"))
        block.gpsimd(replay("gpsimd"))

    return nc


def _prep(Wq, bq, Wk, bk, Wv, bv, Wo, bo, w_up, w_down):
    f = np.float32
    return {
        "wqT": np.ascontiguousarray(np.asarray(Wq, f).T),
        "wkT": np.ascontiguousarray(np.asarray(Wk, f).T),
        "wvT": np.ascontiguousarray(np.asarray(Wv, f).T),
        "woT": np.ascontiguousarray(np.asarray(Wo, f).T),
        "bq8": np.ascontiguousarray(np.asarray(bq, f).reshape(DC, 128).T),
        "bk8": np.ascontiguousarray(np.asarray(bk, f).reshape(DC, 128).T),
        "bv_row": np.asarray(bv, f).reshape(1, D).copy(),
        "bo_row": np.asarray(bo, f).reshape(1, D).copy(),
        "wupT": np.ascontiguousarray(np.asarray(w_up, f).T),
        "wdnT": np.ascontiguousarray(np.asarray(w_down, f).T),
    }


def kernel(L0_attn_x, L0_attn_Wq, L0_attn_bq, L0_attn_Wk, L0_attn_bk,
           L0_attn_Wv, L0_attn_bv, L0_attn_Wo, L0_attn_bo,
           L0_ffn_w_up, L0_ffn_w_down):
    x = np.ascontiguousarray(np.asarray(L0_attn_x, dtype=np.float32))
    weights = _prep(L0_attn_Wq, L0_attn_bq, L0_attn_Wk, L0_attn_bk, L0_attn_Wv,
                    L0_attn_bv, L0_attn_Wo, L0_attn_bo, L0_ffn_w_up, L0_ffn_w_down)

    OWN = S // 4
    nown = OWN // TILE

    nc = build_nc(0, nown)
    in_maps = []
    for core in range(NCORE):
        b, pq = core // 4, core % 4
        xb = x[b]
        start = pq * OWN
        sm = np.zeros((128, NCORE), np.float32)
        for r in range(NCORE):
            if r // 4 == b and r % 4 < pq:
                sm[:, r] = 1.0
        m = {"x_own": np.ascontiguousarray(xb[start:start + OWN]), "smask": sm}
        m.update(weights)
        in_maps.append(m)
    res = run_bass_kernel_spmd(nc, in_maps, core_ids=list(range(NCORE)))
    outp = np.empty((B, S, D), np.float32)
    for core in range(NCORE):
        b, pq = core // 4, core % 4
        outp[b, pq * OWN:(pq + 1) * OWN] = np.asarray(res.results[core]["out"])
    return outp
